# version 08
# baseline (speedup 1.0000x reference)
import math
import numpy as np
import concourse.bass as bass
import concourse.mybir as mybir
from concourse.bass_utils import run_bass_kernel_spmd
from concourse.bass_core_rust import IndirectOffsetOnAxis
from contextlib import ExitStack

F32 = mybir.dt.float32
F32R = mybir.dt.float32r
BF16 = mybir.dt.bfloat16
U32 = mybir.dt.uint32
AF = mybir.ActivationFunctionType
ALU = mybir.AluOpType
AX = mybir.AxisListType

SERIAL_MOE = True
NCORE = 8
BATCH, SEQ, DM = 2, 8192, 1024
TPC = 2048
NCH = 8
NEG = -1.0e30


class Ctr:
    __slots__ = ("h", "n")

    def __init__(self, h):
        self.h = h
        self.n = 0


class Prog:
    ENGS = ("sync", "gpsimd", "tensor", "vector", "scalar")

    def __init__(self):
        self.ops = []
        self.extra = {}

    def on(self, eng, fn, after=(), inc=None, dma=False, waits=None, noextra=False):
        if not noextra:
            after = tuple(after) + tuple(c for c in self.extra.get(eng, ())
                                         if c not in after)
        wl = [(c, c.n) for c in after if c.n > 0]
        if waits:
            wl += [(c, v) for (c, v) in waits if v > 0]
        tgt = None
        if inc is not None:
            amt = 16 if dma else 1
            inc.n += amt
            tgt = (inc, amt)
        self.ops.append((eng, fn, tuple(wl), tgt))
        return (inc, inc.n) if inc is not None else None

    def emit(self, block):
        by = {}
        for op in self.ops:
            by.setdefault(op[0], []).append(op)

        def mk(items):
            def body(E):
                done = {}
                for _, fn, waits, tgt in items:
                    for c, v in waits:
                        if done.get(id(c), 0) < v:
                            E.wait_ge(c.h, v)
                            done[id(c)] = v
                    inst = fn(E)
                    if tgt is not None:
                        c, amt = tgt
                        inst.then_inc(c.h, amt)
            return body

        for eng in self.ENGS:
            if eng in by:
                getattr(block, eng)(mk(by[eng]))


class Dep:

    def __init__(self, P):
        self.P = P
        self.w = {}
        self.r = {}
        self.chain = False
        self.last = None

    def on(self, eng, fn, reads=(), writes=(), inc=None, dma=False, force=()):
        need = {}
        for k in reads:
            t = self.w.get(k)
            if t is not None and t[0] != eng:
                c, v = t[1], t[2]
                need[id(c)] = (c, max(v, need.get(id(c), (c, 0))[1]))
        for k in writes:
            for t in ([self.w[k]] if k in self.w else []) + self.r.get(k, []):
                if t[0] != eng:
                    c, v = t[1], t[2]
                    need[id(c)] = (c, max(v, need.get(id(c), (c, 0))[1]))
        if self.chain and self.last is not None:
            force = tuple(force) + (self.last,)
        for t in force:
            if t is not None:
                c, v = t[1], t[2]
                need[id(c)] = (c, max(v, need.get(id(c), (c, 0))[1]))
        tok = self.P.on(eng, fn, inc=inc, dma=dma,
                        waits=list(need.values()), noextra=True)
        t3 = (eng, tok[0], tok[1]) if tok is not None else None
        if self.chain and t3 is not None:
            self.last = t3
        if t3 is not None:
            for k in writes:
                self.w[k] = t3
                self.r[k] = []
            for k in reads:
                self.r.setdefault(k, []).append(t3)
        return t3

    def barrier(self, ctrs):
        wl = [(c, c.n) for c in ctrs if c.n > 0]
        for eng in Prog.ENGS:
            self.P.on(eng, lambda E: E.nop(), waits=wl, noextra=True)
        self.w = {}
        self.r = {}


def din(nc, name, shape, dtype=F32):
    return nc.dram_tensor(name, list(shape), dtype, kind="ExternalInput").ap()


def dout(nc, name, shape, dtype=F32):
    return nc.dram_tensor(name, list(shape), dtype, kind="ExternalOutput").ap()


def emit_ln(P, cE, cV, cS, n, src, out_f32, out_bf, onc, onr, psS, psB, sq, stats,
            eps=1e-5, gain=None, bias=None, phases=3):
    nsl = n // 512
    SU, SQ_, TA, TB = 0, 512, 1024, 1536
    for s in range(nsl):
        sl = slice(s * 512, (s + 1) * 512)

        if phases < 1:
            continue

        def mm_sum(E, sl=sl):
            for c in range(NCH):
                mm = E.matmul(psS[:, 0:512], onc[:], src(c)[:, sl],
                              start=(c == 0), stop=(c == NCH - 1))
            return mm
        P.on("tensor", mm_sum, after=(cV, cS), inc=cE)
        P.on("scalar", lambda E: E.copy(stats[:, SU:SU + 512], psS[:, 0:512]),
             after=(cE,), inc=cS)
        for c in range(NCH):
            P.on("scalar", lambda E, c=c, sl=sl: E.activation(
                sq[:, 0:512], src(c)[:, sl], AF.Square), after=(cE,), inc=cS)

            def mm_sq(E, c=c):
                return E.matmul(psS[:, 512:1024], onc[:], sq[:, 0:512],
                                start=(c == 0), stop=(c == NCH - 1))
            P.on("tensor", mm_sq, after=(cS,), inc=cE)
        P.on("scalar", lambda E: E.copy(stats[:, SQ_:SQ_ + 512], psS[:, 512:1024]),
             after=(cE,), inc=cS)
        if phases < 2:
            continue
        P.on("vector", lambda E: E.tensor_scalar_mul(stats[:, SU:SU + 512],
                                                     stats[:, SU:SU + 512], 1.0 / DM),
             after=(cS,), inc=cV)
        P.on("vector", lambda E: E.tensor_scalar_mul(stats[:, SQ_:SQ_ + 512],
                                                     stats[:, SQ_:SQ_ + 512], 1.0 / DM),
             after=(cS,), inc=cV)
        P.on("vector", lambda E: E.tensor_tensor(stats[:, TA:TA + 512], stats[:, SU:SU + 512],
                                                 stats[:, SU:SU + 512], ALU.mult),
             after=(cV,), inc=cV)
        P.on("vector", lambda E: E.tensor_sub(stats[:, SQ_:SQ_ + 512], stats[:, SQ_:SQ_ + 512],
                                              stats[:, TA:TA + 512]), after=(cV,), inc=cV)
        P.on("vector", lambda E: E.tensor_scalar_add(stats[:, TA:TA + 512],
                                                     stats[:, SQ_:SQ_ + 512], float(eps)),
             after=(cV,), inc=cV)
        P.on("scalar", lambda E: E.activation(stats[:, TB:TB + 512], stats[:, TA:TA + 512],
                                              AF.Sqrt), after=(cV,), inc=cS)
        P.on("vector", lambda E: E.reciprocal(stats[:, TA:TA + 512], stats[:, TB:TB + 512]),
             after=(cS,), inc=cV)
        P.on("vector", lambda E: E.tensor_tensor(stats[:, TB:TB + 512], stats[:, SU:SU + 512],
                                                 stats[:, TA:TA + 512], ALU.mult),
             after=(cV,), inc=cV)

        if phases < 3:
            continue

        def mm_bc(E):
            E.matmul(psB[:, 0:512], onr[:], stats[:, TA:TA + 512], start=True, stop=True)
            return E.matmul(psB[:, 512:1024], onr[:], stats[:, TB:TB + 512],
                            start=True, stop=True)
        P.on("tensor", mm_bc, after=(cV,), inc=cE)
        for c in range(NCH):
            P.on("vector", lambda E, c=c, sl=sl: E.tensor_tensor(
                sq[:, 0:512], src(c)[:, sl], psB[:, 0:512], ALU.mult),
                 after=(cE, cV), inc=cV)
            dst1 = out_f32 if out_f32 is not None else out_bf
            if gain is None:
                P.on("vector", lambda E, c=c, sl=sl, dst1=dst1: E.tensor_tensor(
                    dst1(c)[:, sl], sq[:, 0:512], psB[:, 512:1024], ALU.subtract),
                     after=(cV,), inc=cV)
            else:
                P.on("vector", lambda E, c=c, sl=sl: E.tensor_tensor(
                    sq[:, 0:512], sq[:, 0:512], psB[:, 512:1024], ALU.subtract),
                     after=(cV,), inc=cV)
                P.on("vector", lambda E, c=c, sl=sl, dst1=dst1: E.tensor_scalar(
                    dst1(c)[:, sl], sq[:, 0:512], gain(c), bias(c),
                    ALU.mult, ALU.add), after=(cV,), inc=cV)
            if out_bf is not None and out_f32 is not None:
                P.on("vector", lambda E, c=c, sl=sl: E.tensor_copy(
                    out_bf(c)[:, sl], out_f32(c)[:, sl]), after=(cV,), inc=cV)


def build_att(upto=99, lnph=3, store_bs=False):
    nc = bass.Bass("TRN2", target_bir_lowering=False)
    T = TPC
    x1_d = din(nc, "x1f", [DM, T])
    ev_d = din(nc, "evf", [DM, T])
    wqk_d = din(nc, "Wqk", [DM, 128])
    bqk_d = din(nc, "bqk", [1, 128])
    wv_d = din(nc, "Wv", [DM, DM])
    bv_d = din(nc, "bv", [1, DM])
    gam_d = din(nc, "gam", [128, 2])
    bet_d = din(nc, "bet", [128, 2])
    rel_d = din(nc, "rel", [32, 1])
    oh_d = din(nc, "oh", [32, 1024])
    c0_d = din(nc, "c0", [1, 1024])
    idx_d = din(nc, "tidx", [128, 4], U32)
    onc_d = din(nc, "onesc", [128, 1])
    onr_d = din(nc, "onesr", [1, 128])
    idb_d = din(nc, "identb", [128, 128], BF16)
    idf_d = din(nc, "identf", [128, 128])
    pvec_d = nc.dram_tensor("pvec", [1024, 1], F32, kind="Internal").ap()
    x3_d = dout(nc, "x3f", [DM, T])

    P = Prog()
    _sec = [0]
    _real_on = P.on
    def _gated_on(eng, fn, after=(), inc=None, dma=False):
        if _sec[0] <= upto:
            _real_on(eng, fn, after=after, inc=inc, dma=dma)
    P.on = _gated_on
    st = ExitStack()
    sb = lambda name, shape, dt=F32: st.enter_context(nc.sbuf_tensor(name, list(shape), dt))
    ps = lambda name, shape, dt=F32: st.enter_context(nc.psum_tensor(name, list(shape), dt))
    with st:
        A = sb("A", [128, NCH * T])
        Bs = sb("Bs", [128, NCH * T])
        x2bs = sb("x2bs", [128, NCH * 512], BF16)
        vxb = x2bs[:, 0:NCH * 128]
        qt = sb("qt", [128, T], BF16)
        kt = sb("kt", [128, T], BF16)
        vt = sb("vt", [128, 4 * 1024], BF16)
        wqkb = sb("wqkb", [128, NCH * 128], BF16)
        wvb = sb("wvb", [128, NCH * 1024], BF16)
        bvb = sb("bvb", [128, DM])
        bqkc = sb("bqkc", [128, 1])
        g0 = sb("g0", [128, 4])
        relv = sb("relv", [32, 1])
        prow = sb("prow", [1, 1024])
        prow2 = sb("prow2", [1, 1024])
        c0v = prow2[0:1, 0:1024]
        idxv = sb("idxv", [128, 4], U32)
        onc = sb("onc", [128, 1])
        onr = sb("onr_s", [1, 128])
        idb = sb("idb", [128, 128], BF16)
        idf = sb("idf_s", [128, 128])
        btile = sb("btile", [128, 4 * 512])
        sq = sb("sq", [128, 512])
        stats = sb("stats", [1, 2048])
        pbf = sb("pbft", [128, 512], BF16)
        ptb = sb("ptbt", [128, 4 * 128], BF16)
        nmax = sb("nmaxt", [128, 1])
        rsum = sb("rsum", [128, 2])
        attn = sb("attn", [128, 1024])
        ohv = attn[0:32, 0:1024]

        psS = ps("psS", [1, 1024])
        psB = ps("psB", [128, 1024])
        psM = ps("psM", [128, 512])
        psV = ps("psV", [128, 1024])
        psP = ps("psP", [128, 512], BF16)
        psA = psV

        cD = Ctr(st.enter_context(nc.semaphore("cD")))
        cW = Ctr(st.enter_context(nc.semaphore("cW")))
        cE = Ctr(st.enter_context(nc.semaphore("cE")))
        cV = Ctr(st.enter_context(nc.semaphore("cV")))
        cS = Ctr(st.enter_context(nc.semaphore("cS")))
        cO = Ctr(st.enter_context(nc.semaphore("cO")))
        cG = Ctr(st.enter_context(nc.semaphore("cG")))
        cI = Ctr(st.enter_context(nc.semaphore("cI")))
        P.extra["tensor"] = (cV, cS, cG, cI)
        P.extra["vector"] = (cE, cG, cI)
        P.extra["scalar"] = (cE, cG, cI)

        block = st.enter_context(nc.Block())

        for c in range(NCH):
            P.on("sync", lambda E, c=c: E.dma_start(out=A[:, c * T:(c + 1) * T],
                                                    in_=x1_d[c * 128:(c + 1) * 128, :]),
                 inc=cD, dma=True)
            P.on("sync", lambda E, c=c: E.dma_start(out=Bs[:, c * T:(c + 1) * T],
                                                    in_=ev_d[c * 128:(c + 1) * 128, :]),
                 inc=cD, dma=True)
        P.on("gpsimd", lambda E: E.dma_start(
            out=wqkb[:].rearrange("p (c n) -> p c n", n=128),
            in_=wqk_d.rearrange("(c p) n -> p c n", p=128)), inc=cG, dma=True)
        P.on("gpsimd", lambda E: E.dma_start(
            out=wvb[:].rearrange("p (c n) -> p c n", n=1024),
            in_=wv_d.rearrange("(c p) n -> p c n", p=128)), inc=cG, dma=True)
        P.on("sync", lambda E: E.dma_start(out=bqkc[:], in_=bqk_d.rearrange("o n -> n o")),
             inc=cW, dma=True)
        P.on("sync", lambda E: E.dma_start(out=g0[:, 0:2], in_=gam_d[:]), inc=cW, dma=True)
        P.on("sync", lambda E: E.dma_start(out=g0[:, 2:4], in_=bet_d[:]), inc=cW, dma=True)
        P.on("sync", lambda E: E.dma_start(out=relv[:], in_=rel_d[:]), inc=cW, dma=True)
        P.on("sync", lambda E: E.dma_start(out=ohv, in_=oh_d[:]), inc=cW, dma=True)
        P.on("sync", lambda E: E.dma_start(out=c0v, in_=c0_d[:]), inc=cW, dma=True)
        P.on("sync", lambda E: E.dma_start(out=idxv[:], in_=idx_d[:]), inc=cW, dma=True)
        P.on("sync", lambda E: E.dma_start(out=onc[:], in_=onc_d[:]), inc=cW, dma=True)
        P.on("sync", lambda E: E.dma_start(out=onr[:], in_=onr_d[:]), inc=cW, dma=True)
        P.on("sync", lambda E: E.dma_start(out=idb[:], in_=idb_d[:]), inc=cW, dma=True)
        P.on("sync", lambda E: E.dma_start(out=idf[:], in_=idf_d[:]), inc=cW, dma=True)
        P.on("sync", lambda E: E.dma_start(out=prow[:], in_=bv_d[:]), inc=cW, dma=True)

        _sec[0] = 1
        def mm_bv(E):
            E.matmul(psB[:, 0:512], onr[:], prow[:, 0:512], start=True, stop=True)
            return E.matmul(psB[:, 512:1024], onr[:], prow[:, 512:1024],
                            start=True, stop=True)
        P.on("tensor", mm_bv, after=(cW,), inc=cE)
        P.on("vector", lambda E: E.tensor_copy(bvb[:], psB[:, 0:1024]), after=(cE,), inc=cV)

        _sec[0] = 2
        P.on("tensor", lambda E: E.matmul(psS[:, 0:512], relv[:], ohv[:, 0:512], start=True, stop=True), after=(cW,), inc=cE)
        P.on("tensor", lambda E: E.matmul(psS[:, 512:1024], relv[:], ohv[:, 512:1024], start=True, stop=True), after=(cW,), inc=cE)
        P.on("vector", lambda E: E.tensor_add(prow[:, 0:512], psS[:, 0:512], c0v[0:1, 0:512]),
             after=(cE,), inc=cV)
        P.on("vector", lambda E: E.tensor_add(prow[:, 512:1024], psS[:, 512:1024],
                                              c0v[0:1, 512:1024]), after=(cE,), inc=cV)
        P.on("sync", lambda E: E.dma_start(out=pvec_d.rearrange("t o -> o t"), in_=prow[:]),
             after=(cV,), inc=cO, dma=True)
        for ib in range(4):
            P.on("gpsimd", lambda E, ib=ib: E.indirect_dma_start(
                btile[:, ib * 512:(ib + 1) * 512], None,
                pvec_d[:],
                IndirectOffsetOnAxis(idxv[:, ib:ib + 1], 0)),
                 after=(cO, cW, cG), inc=cI, dma=True)

        _sec[0] = 3
        for c in range(NCH):
            P.on("vector", lambda E, c=c: E.tensor_add(Bs[:, c * T:(c + 1) * T],
                                                       Bs[:, c * T:(c + 1) * T],
                                                       A[:, c * T:(c + 1) * T]),
                 after=(cD,), inc=cV)
        emit_ln(P, cE, cV, cS, T, lambda c: Bs[:, c * T:(c + 1) * T],
                lambda c: A[:, c * T:(c + 1) * T], None,
                onc, onr, psS, psB, sq, stats, phases=lnph)

        _sec[0] = 5
        for s in range(T // 512):
            for c in range(NCH):
                P.on("vector", lambda E, c=c, s=s: E.tensor_copy(
                    x2bs[:, c * 512:(c + 1) * 512],
                    A[:, c * T + s * 512:c * T + (s + 1) * 512]), after=(cV, cE), inc=cV)

            def mm_qk(E, s=s):
                for c in range(NCH):
                    mm = E.matmul(psM[:, 0:512], wqkb[:, c * 128:(c + 1) * 128],
                                  x2bs[:, c * 512:(c + 1) * 512],
                                  start=(c == 0), stop=(c == NCH - 1))
                return mm
            P.on("tensor", mm_qk, after=(cV, cW), inc=cE)
            P.on("scalar", lambda E: E.activation(sq[:, 0:512], psM[:, 0:512], AF.Silu,
                                                  bias=bqkc[:, 0:1]),
                 after=(cE, cV), inc=cS)
            P.on("vector", lambda E, s=s: E.tensor_scalar(qt[:, s * 512:(s + 1) * 512],
                                                          sq[:, 0:512], g0[:, 0:1], g0[:, 2:3],
                                                          ALU.mult, ALU.add),
                 after=(cS,), inc=cV)
            P.on("vector", lambda E, s=s: E.tensor_scalar(kt[:, s * 512:(s + 1) * 512],
                                                          sq[:, 0:512], g0[:, 1:2], g0[:, 3:4],
                                                          ALU.mult, ALU.add),
                 after=(cS,), inc=cV)

        _sec[0] = 6
        for ch in range(4):
            for jt in range(4):
                j0 = ch * 512 + jt * 128
                for c in range(NCH):
                    P.on("vector", lambda E, c=c, j0=j0: E.tensor_copy(
                        vxb[:, c * 128:(c + 1) * 128] if False else x2bs[:, c * 128:(c + 1) * 128],
                        A[:, c * T + j0:c * T + j0 + 128]), after=(cV, cE), inc=cV)

                def mm_v(E, j0=j0):
                    for half in range(2):
                        for c in range(NCH):
                            mm = E.matmul(psV[:, half * 512:(half + 1) * 512],
                                          x2bs[:, c * 128:(c + 1) * 128],
                                          wvb[:, c * 1024 + half * 512:c * 1024 + (half + 1) * 512],
                                          start=(c == 0), stop=(c == NCH - 1))
                    return mm
                P.on("tensor", mm_v, after=(cV, cW, cS), inc=cE)
                P.on("vector", lambda E: E.tensor_add(attn[:, 0:1024], psV[:, 0:1024],
                                                      bvb[:, 0:1024]), after=(cE,), inc=cV)
                P.on("scalar", lambda E, jt=jt: E.activation(vt[:, jt * 1024:(jt + 1) * 1024],
                                                             attn[:, 0:1024], AF.Silu),
                     after=(cV,), inc=cS)
            for it in range(4):
                i0 = ch * 512 + it * 128

                def mm_sim(E, ch=ch, i0=i0):
                    return E.matmul(psM[:, 0:512], qt[:, i0:i0 + 128],
                                    kt[:, ch * 512:(ch + 1) * 512], start=True, stop=True)
                P.on("tensor", mm_sim, after=(cV, cD, cS), inc=cE)
                P.on("vector", lambda E, it=it: E.scalar_tensor_tensor(
                    psM[:, 0:512], psM[:, 0:512], DM ** -0.5,
                    btile[:, it * 512:(it + 1) * 512], ALU.mult, ALU.add),
                     after=(cE, cD), inc=cV)
                P.on("vector", lambda E: E.reduce_max(nmax[:, 0:1], psM[:, 0:512], axis=AX.X,
                                                      negate=True), after=(cV,), inc=cV)
                P.on("scalar", lambda E: E.activation(pbf[:], psM[:, 0:512], AF.Exp,
                                                      bias=nmax[:, 0:1],
                                                      accum_out=rsum[:, 0:1]),
                     after=(cV,), inc=cS)

                def mm_pt(E):
                    for jc in range(4):
                        mm = E.transpose(psP[:, jc * 128:(jc + 1) * 128],
                                         pbf[:, jc * 128:(jc + 1) * 128], idb[:])
                    return mm
                P.on("tensor", mm_pt, after=(cS,), inc=cE)
                P.on("vector", lambda E: E.tensor_copy(ptb[:], psP[:, 0:512]),
                     after=(cE,), inc=cV)
                P.on("vector", lambda E: E.reciprocal(rsum[:, 1:2], rsum[:, 0:1]),
                     after=(cS,), inc=cV)

                def mm_av(E, ch=ch):
                    for half in range(2):
                        for jc in range(4):
                            mm = E.matmul(psA[:, half * 512:(half + 1) * 512],
                                          ptb[:, jc * 128:(jc + 1) * 128],
                                          vt[:, jc * 1024 + half * 512:jc * 1024 + (half + 1) * 512],
                                          start=(jc == 0), stop=(jc == 3))
                    return mm
                P.on("tensor", mm_av, after=(cV,), inc=cE)
                P.on("vector", lambda E: E.tensor_scalar_mul(attn[:], psA[:, 0:1024],
                                                             rsum[:, 1:2]),
                     after=(cE,), inc=cV)
                for g in range(2):
                    def mm_at4(E, g=g):
                        for c4 in range(4):
                            mm = E.transpose(psB[:, c4 * 128:(c4 + 1) * 128],
                                             attn[:, (g * 4 + c4) * 128:(g * 4 + c4 + 1) * 128],
                                             idf[:])
                        return mm
                    P.on("tensor", mm_at4, after=(cV,), inc=cE)
                    for c4 in range(4):
                        c = g * 4 + c4
                        P.on("vector", lambda E, c=c, c4=c4, i0=i0: E.tensor_add(
                            Bs[:, c * T + i0:c * T + i0 + 128],
                            A[:, c * T + i0:c * T + i0 + 128],
                            psB[:, c4 * 128:(c4 + 1) * 128]), after=(cE,), inc=cV)

        _sec[0] = 7
        emit_ln(P, cE, cV, cS, T, lambda c: Bs[:, c * T:(c + 1) * T],
                lambda c: A[:, c * T:(c + 1) * T], None,
                onc, onr, psS, psB, sq, stats)
        _sec[0] = 0
        SRC = Bs if store_bs else A
        for c in range(NCH):
            P.on("sync", lambda E, c=c: E.dma_start(out=x3_d[c * 128:(c + 1) * 128, :],
                                                    in_=SRC[:, c * T:(c + 1) * T]),
                 after=(cV,), inc=cO, dma=True)
        P.on("sync", lambda E: E.nop(), after=(cO,))
        P.emit(block)
    return nc


def att_consts():
    NB, MD, me = 32, 128, 16
    d = np.arange(512)
    nf = np.maximum(d, 1).astype(np.float64)
    vl = me + np.log(nf / me) / math.log(MD / me) * (NB - me)
    vl = np.minimum(vl, NB - 1)
    bucket = np.where(d < me, d, vl.astype(np.int32)).astype(np.int32)
    oh = np.zeros((32, 1024), np.float32)
    for t in range(0, 512):
        oh[bucket[511 - t], t] = math.sqrt(128.0)
    c0 = np.zeros((1, 1024), np.float32)
    c0[0, 512:] = NEG
    idx = np.zeros((128, 4), np.uint32)
    for ib in range(4):
        for p in range(128):
            idx[p, ib] = 511 - (ib * 128 + p)
    return {
        "oh": oh, "c0": c0, "tidx": idx,
        "onesc": np.ones((128, 1), np.float32),
        "onesr": np.ones((1, 128), np.float32),
        "identf": np.eye(128, dtype=np.float32),
    }


def _bf16(x):
    import ml_dtypes
    return np.asarray(x, np.float32).astype(ml_dtypes.bfloat16)


def run_att(x1, ev, W, upto=99, lnph=3, store_bs=False):
    import ml_dtypes
    nc = build_att(upto=upto, lnph=lnph, store_bs=store_bs)
    consts = att_consts()
    consts["identb"] = np.eye(128, dtype=np.float32).astype(ml_dtypes.bfloat16)
    in_maps = []
    for c in range(NCORE):
        b, t0 = c // 4, (c % 4) * TPC
        m = {
            "x1f": np.ascontiguousarray(x1[b, t0:t0 + TPC, :].T),
            "evf": np.ascontiguousarray(ev[b, t0:t0 + TPC, :].T),
            "Wqk": W["L1_attn_Wqk"], "bqk": W["L1_attn_bqk"][None],
            "Wv": W["L1_attn_Wv"], "bv": W["L1_attn_bv"][None],
            "gam": np.ascontiguousarray(W["L1_attn_gamma"].T),
            "bet": np.ascontiguousarray(W["L1_attn_beta"].T),
            "rel": W["L1_attn_rel_emb"],
        }
        m.update(consts)
        in_maps.append(m)
    res = run_bass_kernel_spmd(nc, in_maps, core_ids=list(range(NCORE)))
    x3 = np.zeros((BATCH, SEQ, DM), np.float32)
    for c in range(NCORE):
        b, t0 = c // 4, (c % 4) * TPC
        x3[b, t0:t0 + TPC, :] = res.results[c]["x3f"].T
    return x3


def build_moe():
    nc = bass.Bass("TRN2", target_bir_lowering=False)
    T = TPC
    NSL = T // 512
    x3_d = din(nc, "x3f", [DM, T])
    gw_d = din(nc, "gwT", [DM, 16])
    we1_d = din(nc, "We1T", [16, DM, 512])
    be1_d = din(nc, "be1c", [16, 128, 4])
    we3_d = din(nc, "We3T", [16, DM, 512])
    be3_d = din(nc, "be3c", [16, 128, 4])
    we2_d = din(nc, "We2Tp", [16, 512, DM])
    be2_d = din(nc, "be2c", [16, 128, 8])
    w2r_d = din(nc, "w2r0", [16, 1, DM])
    ws1_d = din(nc, "Ws1T", [DM, DM])
    bs1_d = din(nc, "bs1c", [128, 8])
    ws3_d = din(nc, "Ws3T", [DM, DM])
    bs3_d = din(nc, "bs3c", [128, 8])
    ws2_d = din(nc, "Ws2Tp", [DM, DM])
    bs2_d = din(nc, "bs2c", [128, 8])
    wsr_d = din(nc, "ws2r0", [1, DM])
    onm_d = din(nc, "onesm", [128, 128])
    nr2_d = din(nc, "negr2", [1, 128])
    sel_d = din(nc, "sel16", [16, 16 * 128])
    idf_d = din(nc, "identf", [128, 128])
    out_d = dout(nc, "outf", [DM, T])

    P = Prog()
    st = ExitStack()
    sb = lambda name, shape, dt=F32: st.enter_context(nc.sbuf_tensor(name, list(shape), dt))
    ps = lambda name, shape, dt=F32: st.enter_context(nc.psum_tensor(name, list(shape), dt))
    with st:
        x3b = sb("x3b", [128, NCH * T], BF16)
        yt = sb("yt", [128, NCH * T])
        z0 = sb("z0", [128, NCH * 512])
        at = sb("at", [128, 4 * 512])
        sf = sb("sf", [128, 4 * 512])
        sbf = sb("sbf", [128, NCH * 512], BF16)
        w1t = sb("w1t", [128, NCH * 512], BF16)
        w3t = sb("w3t", [128, NCH * 512], BF16)
        w2t = sb("w2t", [128, NCH * 512], BF16)
        w2tb = sb("w2tb", [128, NCH * 512], BF16)
        w2r = sb("w2r", [1, DM], BF16)
        be1 = sb("be1", [128, 8])
        be3 = sb("be3", [128, 8])
        be2 = sb("be2", [128, 8])
        wbc = sb("wbc", [128, 512])
        rst = sb("rst", [128, 512])
        tsr = sb("tsr", [128, 512])
        tsb = sb("tsb", [1, 512], BF16)
        sq = sb("sq", [128, 512], F32R)
        sq2 = sb("sq2", [128, 512])
        trow = sb("trow", [1, 512], F32R)
        prb = sb("prb", [128, 16])
        wte = sb("wte", [16, T])
        onm = sb("onm", [128, 128], F32R)
        nr2 = sb("nr2", [1, 128], F32R)
        idf = sb("idf_s", [128, 128])
        gwt = sb("gwt", [128, NCH * 16])
        s16 = sf[0:16, 0:T]
        sel = w2tb[0:16, :].bitcast(F32)
        m12 = sb("m12", [128, 4])

        psA_ = ps("psA", [128, 512])
        psB_ = ps("psB", [128, 512])
        psU = ps("psU", [128, 512])
        psR = ps("psR", [128, 512])
        psS16 = ps("psS16", [16, 512])
        psT16 = ps("psT16", [128, 16])
        psW = ps("psW", [128, 512])

        cD = Ctr(st.enter_context(nc.semaphore("cD")))
        cW = Ctr(st.enter_context(nc.semaphore("cW")))
        cE = Ctr(st.enter_context(nc.semaphore("cE")))
        cV = Ctr(st.enter_context(nc.semaphore("cV")))
        cS = Ctr(st.enter_context(nc.semaphore("cS")))
        cO = Ctr(st.enter_context(nc.semaphore("cO")))
        cG = Ctr(st.enter_context(nc.semaphore("cG")))
        P.extra["tensor"] = (cV, cS, cG)
        P.extra["vector"] = (cE, cG)
        P.extra["scalar"] = (cE, cG)
        block = st.enter_context(nc.Block())

        P.on("gpsimd", lambda E: E.dma_start(
            out=x3b[:].rearrange("p (c t) -> p c t", t=T),
            in_=x3_d.rearrange("(c p) t -> p c t", p=128)), inc=cG, dma=True)
        P.on("gpsimd", lambda E: E.dma_start(out=onm[:], in_=onm_d[:]), inc=cG, dma=True)
        P.on("gpsimd", lambda E: E.dma_start(out=nr2[:], in_=nr2_d[:]), inc=cG, dma=True)
        for nm, dst in ((sel_d, sel), (idf_d, idf[:]),):
            P.on("sync", lambda E, nm=nm, dst=dst: E.dma_start(out=dst if not hasattr(dst, "tensor") else dst, in_=nm[:]),
                 inc=cW, dma=True)
        P.on("sync", lambda E: E.dma_start(
            out=gwt[:].rearrange("p (c g) -> p c g", g=16),
            in_=gw_d.rearrange("(c p) g -> p c g", p=128)), inc=cW, dma=True)

        P.on("vector", lambda E: E.memset(yt[:], 0.0), inc=cV)
        P.on("vector", lambda E: E.memset(yt[0:1, 7 * T:8 * T], 1.0), inc=cV)

        for s in range(NSL):
            for c in range(NCH):
                P.on("sync", lambda E, c=c, s=s: E.dma_start(
                    out=z0[:, c * 512:(c + 1) * 512],
                    in_=x3_d[c * 128:(c + 1) * 128, s * 512:(s + 1) * 512]),
                     after=(cV, cE), inc=cD, dma=True)

            def mm_sc(E, s=s):
                for c in range(NCH):
                    mm = E.matmul(psS16[:, 0:512], gwt[:, c * 16:(c + 1) * 16],
                                  z0[:, c * 512:(c + 1) * 512],
                                  start=(c == 0), stop=(c == NCH - 1))
                return mm
            P.on("tensor", mm_sc, after=(cD, cW), inc=cE)
            P.on("vector", lambda E, s=s: E.tensor_copy(s16[:, s * 512:(s + 1) * 512],
                                                        psS16[:, 0:512]),
                 after=(cE,), inc=cV)
        for tt in range(T // 128):
            P.on("tensor", lambda E, tt=tt: E.transpose(
                psT16[:, 0:16], s16[:, tt * 128:(tt + 1) * 128], idf[0:16, 0:16]),
                 after=(cV,), inc=cE)
            P.on("vector", lambda E: E.tensor_copy(prb[:], psT16[:, 0:16]),
                 after=(cE,), inc=cV)
            P.on("vector", lambda E: E.reduce_max(m12[:, 0:1], prb[:], axis=AX.X,
                                                  negate=True), after=(cV,), inc=cV)
            P.on("scalar", lambda E: E.activation(prb[:], prb[:], AF.Exp,
                                                  bias=m12[:, 0:1], accum_out=m12[:, 1:2]),
                 after=(cV,), inc=cS)
            P.on("vector", lambda E: E.reciprocal(m12[:, 2:3], m12[:, 1:2]),
                 after=(cS,), inc=cV)
            P.on("vector", lambda E: E.tensor_scalar_mul(prb[:], prb[:], m12[:, 2:3]),
                 after=(cV, cS), inc=cV)
            P.on("vector", lambda E: E.reduce_max(m12[:, 0:1], prb[:], axis=AX.X),
                 after=(cV,), inc=cV)
            P.on("vector", lambda E: E.tensor_scalar(sq2[:, 0:16], prb[:], m12[:, 0:1],
                                                     1.0, ALU.is_lt, ALU.mult),
                 after=(cV,), inc=cV)
            P.on("vector", lambda E: E.tensor_tensor(sq2[:, 16:32], prb[:], sq2[:, 0:16],
                                                     ALU.mult), after=(cV,), inc=cV)
            P.on("vector", lambda E: E.reduce_max(m12[:, 1:2], sq2[:, 16:32], axis=AX.X),
                 after=(cV,), inc=cV)
            P.on("vector", lambda E: E.tensor_scalar(sq2[:, 0:16], prb[:], m12[:, 1:2],
                                                     1.0, ALU.is_ge, ALU.mult),
                 after=(cV,), inc=cV)
            P.on("vector", lambda E: E.tensor_tensor(prb[:], prb[:], sq2[:, 0:16],
                                                     ALU.mult), after=(cV,), inc=cV)
            P.on("tensor", lambda E: E.transpose(psS16[:, 0:128], prb[:], idf[:]),
                 after=(cV,), inc=cE)
            P.on("vector", lambda E, tt=tt: E.tensor_copy(wte[:, tt * 128:(tt + 1) * 128],
                                                          psS16[:, 0:128]),
                 after=(cE,), inc=cV)

        for e in range(17):
            shared = (e == 16)
            nOT = 8 if shared else 4
            we1 = ws1_d if shared else we1_d[e]
            we3 = ws3_d if shared else we3_d[e]
            we2 = ws2_d if shared else we2_d[e]
            w2rs = wsr_d if shared else w2r_d[e]
            if not shared:
                P.on("gpsimd", lambda E, we1=we1: E.dma_start(
                    out=w1t[:].rearrange("p (c t) -> p c t", t=512),
                    in_=we1.rearrange("(c p) t -> p c t", p=128)),
                     after=(cE, cS), inc=cG, dma=True)
                P.on("gpsimd", lambda E, we3=we3: E.dma_start(
                    out=w3t[:].rearrange("p (c t) -> p c t", t=512),
                    in_=we3.rearrange("(c p) t -> p c t", p=128)),
                     after=(cE, cS), inc=cG, dma=True)
            if shared:
                P.on("gpsimd", lambda E, we2=we2: E.dma_start(
                    out=w2t[:].rearrange("p (c t) -> p c t", t=1024),
                    in_=we2[0:512].rearrange("(c p) t -> p c t", p=128)),
                     after=(cE, cS), inc=cG, dma=True)
                P.on("gpsimd", lambda E, we2=we2: E.dma_start(
                    out=w2tb[:].rearrange("p (c t) -> p c t", t=1024),
                    in_=we2[512:1024].rearrange("(c p) t -> p c t", p=128)),
                     after=(cE, cS), inc=cG, dma=True)
            else:
                P.on("gpsimd", lambda E, we2=we2: E.dma_start(
                    out=w2t[:].rearrange("p (c t) -> p c t", t=1024),
                    in_=we2.rearrange("(c p) t -> p c t", p=128)),
                     after=(cE, cS), inc=cG, dma=True)
            P.on("gpsimd", lambda E, w2rs=w2rs: E.dma_start(out=w2r[:], in_=w2rs),
                 after=(cE, cS), inc=cG, dma=True)
            b1src = bs1_d if shared else be1_d[e]
            b3src = bs3_d if shared else be3_d[e]
            b2src = bs2_d if shared else be2_d[e]
            nb = 8 if shared else 4
            P.on("sync", lambda E, b1src=b1src, nb=nb: E.dma_start(out=be1[:, 0:nb],
                                                                  in_=b1src[:]),
                 after=(cS,), inc=cW, dma=True)
            P.on("sync", lambda E, b3src=b3src, nb=nb: E.dma_start(out=be3[:, 0:nb],
                                                                  in_=b3src[:]),
                 after=(cV,), inc=cW, dma=True)
            P.on("sync", lambda E, b2src=b2src: E.dma_start(out=be2[:], in_=b2src[:]),
                 after=(cV,), inc=cW, dma=True)
            for s in range(NSL):
                sl = slice(s * 512, (s + 1) * 512)
                stile = z0 if shared else sf
                if not shared:
                    P.on("tensor", lambda E, s=s, e=e: E.matmul(
                        psW[:], sel[:, e * 128:(e + 1) * 128], wte[:, sl],
                        start=True, stop=True), after=(cV, cS), inc=cE)
                    P.on("vector", lambda E: E.tensor_copy(wbc[:], psW[:]),
                         after=(cE,), inc=cV)
                for o in range(nOT):
                    ot = o % 4
                    if shared and ot == 0:
                        h = o // 4
                        P.on("gpsimd", lambda E, h=h, we1=we1: E.dma_start(
                            out=w1t[:].rearrange("p (c t) -> p c t", t=512),
                            in_=we1[:, h * 512:(h + 1) * 512].rearrange(
                                "(c p) t -> p c t", p=128)),
                             after=(cE, cS), inc=cG, dma=True)
                        P.on("gpsimd", lambda E, h=h, we3=we3: E.dma_start(
                            out=w3t[:].rearrange("p (c t) -> p c t", t=512),
                            in_=we3[:, h * 512:(h + 1) * 512].rearrange(
                                "(c p) t -> p c t", p=128)),
                             after=(cE, cS), inc=cG, dma=True)

                    def mm_u1(E, o=ot, sl=sl):
                        for c in range(NCH):
                            mm = E.matmul(psA_[:], w1t[:, c * 512 + o * 128:c * 512 + o * 128 + 128],
                                          x3b[:, c * T:(c + 1) * T][:, sl],
                                          start=(c == 0), stop=(c == NCH - 1))
                        return mm
                    P.on("tensor", mm_u1, after=(cW, cV, cS), inc=cE)
                    P.on("scalar", lambda E, o=o, ot=ot: E.activation(
                        at[:, ot * 512:(ot + 1) * 512], psA_[:], AF.Silu,
                        bias=be1[:, o:o + 1]), after=(cE,), inc=cS)

                    def mm_u3(E, o=ot, sl=sl):
                        for c in range(NCH):
                            mm = E.matmul(psB_[:], w3t[:, c * 512 + o * 128:c * 512 + o * 128 + 128],
                                          x3b[:, c * T:(c + 1) * T][:, sl],
                                          start=(c == 0), stop=(c == NCH - 1))
                        return mm
                    P.on("tensor", mm_u3, after=(cW, cV, cS), inc=cE)
                    P.on("vector", lambda E, o=o, ot=ot, stile=stile: E.scalar_tensor_tensor(
                        stile[:, o * 512:(o + 1) * 512], psB_[:], be3[:, o:o + 1],
                        at[:, ot * 512:(ot + 1) * 512], ALU.add, ALU.mult),
                         after=(cE, cS), inc=cV)
                for o in range(nOT):
                    P.on("scalar", lambda E, o=o, stile=stile: E.activation(
                        sq[:], stile[:, o * 512:(o + 1) * 512], AF.Square),
                         after=(cV, cE), inc=cS)
                    P.on("tensor", lambda E, o=o, nOT=nOT: E.matmul(
                        psR[:], onm[:], sq[:], start=(o == 0), stop=(o == nOT - 1)),
                         after=(cS,), inc=cE)
                P.on("vector", lambda E: E.tensor_scalar_add(tsr[:], psR[:], 1.0),
                     after=(cE,), inc=cV)
                P.on("scalar", lambda E: E.activation(tsr[:], tsr[:], AF.Sqrt),
                     after=(cV,), inc=cS)
                P.on("vector", lambda E: E.tensor_copy(tsb[:], tsr[0:1, :]),
                     after=(cS,), inc=cV)
                P.on("vector", lambda E, stile=stile, nOT=nOT: E.tensor_copy(
                    sbf[:, 0:nOT * 512], stile[:, 0:nOT * 512]), after=(cV, cS), inc=cV)
                for o2 in range(8):
                    def mm_u2(E, o2=o2, shared=shared):
                        if shared:
                            for hc in range(4):
                                E.matmul(psU[:], w2t[:, hc * 1024 + o2 * 128:hc * 1024 + o2 * 128 + 128],
                                         sbf[:, hc * 512:(hc + 1) * 512],
                                         start=(hc == 0), stop=False)
                            for hc in range(4):
                                E.matmul(psU[:], w2tb[:, hc * 1024 + o2 * 128:hc * 1024 + o2 * 128 + 128],
                                         sbf[:, (4 + hc) * 512:(5 + hc) * 512],
                                         start=False, stop=False)
                        else:
                            for hc in range(4):
                                E.matmul(psU[:], w2t[:, hc * 1024 + o2 * 128:hc * 1024 + o2 * 128 + 128],
                                         sbf[:, hc * 512:(hc + 1) * 512],
                                         start=(hc == 0), stop=False)
                        return E.matmul(psU[:], w2r[:, o2 * 128:(o2 + 1) * 128],
                                        tsb[:], start=False, stop=True)
                    P.on("tensor", mm_u2, after=(cV, cS, cW), inc=cE)
                    P.on("vector", lambda E, o2=o2: E.tensor_scalar(
                        z0[:, o2 * 512:(o2 + 1) * 512], psU[:], be2[:, o2:o2 + 1], None,
                        ALU.add), after=(cE,), inc=cV)
                for o2 in range(8):
                    P.on("scalar", lambda E, o2=o2: E.activation(
                        sq[:], z0[:, o2 * 512:(o2 + 1) * 512], AF.Square),
                         after=(cV, cE), inc=cS)
                    P.on("tensor", lambda E, o2=o2: E.matmul(
                        psR[:], onm[:], sq[:], start=(o2 == 0), stop=(o2 == 7)),
                         after=(cS,), inc=cE)
                P.on("vector", lambda E: E.tensor_scalar_add(sq2[:], psR[:], 1.0),
                     after=(cE,), inc=cV)
                P.on("scalar", lambda E: E.activation(sq2[:], sq2[:], AF.Sqrt),
                     after=(cV,), inc=cS)
                P.on("vector", lambda E: E.tensor_copy(z0[0:1, 7 * 512:8 * 512],
                                                       sq2[0:1, :]), after=(cS,), inc=cV)
                for o2 in range(8):
                    if shared:
                        P.on("vector", lambda E, o2=o2, sl=sl: E.tensor_add(
                            z0[:, o2 * 512:(o2 + 1) * 512], z0[:, o2 * 512:(o2 + 1) * 512],
                            yt[:, o2 * T:(o2 + 1) * T][:, sl]), after=(cV,), inc=cV)
                    else:
                        P.on("vector", lambda E, o2=o2: E.tensor_tensor(
                            z0[:, o2 * 512:(o2 + 1) * 512], z0[:, o2 * 512:(o2 + 1) * 512],
                            wbc[:], ALU.mult), after=(cV,), inc=cV)
                        P.on("vector", lambda E, o2=o2, sl=sl: E.tensor_add(
                            z0[:, o2 * 512:(o2 + 1) * 512], z0[:, o2 * 512:(o2 + 1) * 512],
                            yt[:, o2 * T:(o2 + 1) * T][:, sl]), after=(cV,), inc=cV)
                for o2 in range(8):
                    P.on("scalar", lambda E, o2=o2: E.activation(
                        sq[:], z0[:, o2 * 512:(o2 + 1) * 512], AF.Square),
                         after=(cV, cE), inc=cS)
                    P.on("tensor", lambda E, o2=o2: E.matmul(
                        psR[:], onm[:], sq[:], start=(o2 == 0), stop=False),
                         after=(cS,), inc=cE)
                P.on("scalar", lambda E: E.activation(trow[:], z0[0:1, 7 * 512:8 * 512],
                                                      AF.Square), after=(cV, cE), inc=cS)
                P.on("tensor", lambda E: E.matmul(psR[:], nr2[:], trow[:],
                                                  start=False, stop=True),
                     after=(cS,), inc=cE)
                P.on("vector", lambda E: E.tensor_scalar_mul(rst[:], psR[:], -1.0),
                     after=(cE,), inc=cV)
                P.on("vector", lambda E: E.tensor_scalar_max(rst[:], rst[:], 1e-6),
                     after=(cV,), inc=cV)
                P.on("scalar", lambda E: E.activation(rst[:], rst[:], AF.Sqrt),
                     after=(cV,), inc=cS)
                P.on("vector", lambda E: E.reciprocal(rst[:], rst[:]), after=(cS,), inc=cV)
                for o2 in range(8):
                    P.on("vector", lambda E, o2=o2, sl=sl: E.tensor_tensor(
                        yt[:, o2 * T:(o2 + 1) * T][:, sl], z0[:, o2 * 512:(o2 + 1) * 512],
                        rst[:], ALU.mult), after=(cV,), inc=cV)

        for s in range(NSL):
            sl = slice(s * 512, (s + 1) * 512)
            for c in range(7):
                P.on("sync", lambda E, c=c, s=s: E.dma_start(
                    out=z0[:, c * 512:(c + 1) * 512],
                    in_=x3_d[c * 128 + 1:c * 128 + 129, s * 512:(s + 1) * 512]),
                     after=(cV, cE), inc=cD, dma=True)
            P.on("sync", lambda E, s=s: E.dma_start(
                out=z0[0:1, 7 * 512:8 * 512], in_=x3_d[0:1, s * 512:(s + 1) * 512]),
                 after=(cV, cE), inc=cD, dma=True)
            P.on("sync", lambda E, s=s: E.dma_start(
                out=z0[1:128, 7 * 512:8 * 512], in_=x3_d[897:1024, s * 512:(s + 1) * 512]),
                 after=(cV, cE), inc=cD, dma=True)
            for c in range(NCH):
                P.on("vector", lambda E, c=c, sl=sl: E.tensor_add(
                    z0[:, c * 512:(c + 1) * 512], z0[:, c * 512:(c + 1) * 512],
                    yt[:, c * T:(c + 1) * T][:, sl]), after=(cD,), inc=cV)
            for c in range(NCH):
                P.on("tensor", lambda E, c=c: E.matmul(
                    psR[:], onm[:], z0[:, c * 512:(c + 1) * 512].bitcast(F32R) if False
                    else sqz(E, c), start=(c == 0), stop=(c == NCH - 1)),
                     after=(cS,), inc=cE) if False else None
            for c in range(NCH):
                P.on("vector", lambda E, c=c: E.tensor_copy(sq[:], z0[:, c * 512:(c + 1) * 512]),
                     after=(cV, cE), inc=cV)
                P.on("tensor", lambda E, c=c: E.matmul(
                    psA_[:], onm[:], sq[:], start=(c == 0), stop=(c == NCH - 1)),
                     after=(cV,), inc=cE)
                P.on("scalar", lambda E, c=c: E.activation(sq[:], z0[:, c * 512:(c + 1) * 512],
                                                           AF.Square), after=(cE, cV), inc=cS)
                P.on("tensor", lambda E, c=c: E.matmul(
                    psB_[:], onm[:], sq[:], start=(c == 0), stop=(c == NCH - 1)),
                     after=(cS,), inc=cE)
            P.on("vector", lambda E: E.tensor_scalar_mul(tsr[:], psA_[:], 1.0 / DM),
                 after=(cE,), inc=cV)
            P.on("vector", lambda E: E.tensor_scalar_mul(sq2[:], psB_[:], 1.0 / DM),
                 after=(cE,), inc=cV)
            P.on("vector", lambda E: E.tensor_tensor(rst[:], tsr[:], tsr[:], ALU.mult),
                 after=(cV,), inc=cV)
            P.on("vector", lambda E: E.tensor_sub(sq2[:], sq2[:], rst[:]),
                 after=(cV,), inc=cV)
            P.on("vector", lambda E: E.tensor_scalar_add(sq2[:], sq2[:], 1e-5),
                 after=(cV,), inc=cV)
            P.on("scalar", lambda E: E.activation(sq2[:], sq2[:], AF.Sqrt),
                 after=(cV,), inc=cS)
            P.on("vector", lambda E: E.reciprocal(rst[:], sq2[:]), after=(cS,), inc=cV)
            P.on("vector", lambda E: E.tensor_tensor(tsr[:], tsr[:], rst[:], ALU.mult),
                 after=(cV,), inc=cV)
            for c in range(NCH):
                P.on("vector", lambda E, c=c: E.tensor_tensor(
                    z0[:, c * 512:(c + 1) * 512], z0[:, c * 512:(c + 1) * 512], rst[:],
                    ALU.mult), after=(cV,), inc=cV)
                P.on("vector", lambda E, c=c: E.tensor_sub(
                    z0[:, c * 512:(c + 1) * 512], z0[:, c * 512:(c + 1) * 512], tsr[:]),
                     after=(cV,), inc=cV)
            for c in range(7):
                P.on("sync", lambda E, c=c, s=s: E.dma_start(
                    out=out_d[c * 128 + 1:c * 128 + 129, s * 512:(s + 1) * 512],
                    in_=z0[:, c * 512:(c + 1) * 512]), after=(cV,), inc=cO, dma=True)
            P.on("sync", lambda E, s=s: E.dma_start(
                out=out_d[0:1, s * 512:(s + 1) * 512], in_=z0[0:1, 7 * 512:8 * 512]),
                 after=(cV,), inc=cO, dma=True)
            P.on("sync", lambda E, s=s: E.dma_start(
                out=out_d[897:1024, s * 512:(s + 1) * 512], in_=z0[1:128, 7 * 512:8 * 512]),
                 after=(cV,), inc=cO, dma=True)
        P.on("sync", lambda E: E.nop(), after=(cO,))
        P.emit(block)
    return nc


def moe_perm():
    p = np.zeros(1024, np.int64)
    p[0:896] = np.arange(1, 897)
    p[896] = 0
    p[897:1024] = np.arange(897, 1024)
    return p


def moe_host_inputs(W):
    perm = moe_perm()
    def padT(w, out_dim):
        o, i = w.shape
        r = np.zeros((i, out_dim), np.float32)
        r[:, :o] = w.T
        return r
    def bcol(b, n):
        r = np.zeros((128 * n,), np.float32)
        r[:b.shape[0]] = b
        return np.ascontiguousarray(r.reshape(n, 128).T)
    E = 16
    We1T = np.stack([padT(W["L1_ffn_We1"][e], 512) for e in range(E)])
    We3T = np.stack([padT(W["L1_ffn_We3"][e], 512) for e in range(E)])
    be1 = np.stack([bcol(W["L1_ffn_be1"][e], 4) for e in range(E)])
    be3 = np.stack([bcol(W["L1_ffn_be3"][e], 4) for e in range(E)])
    We2Tp = np.zeros((E, 512, 1024), np.float32)
    w2r0 = np.zeros((E, 1, 1024), np.float32)
    be2c = np.zeros((E, 128, 8), np.float32)
    for e in range(E):
        w = W["L1_ffn_We2"][e]
        wT = w.T
        dev = np.zeros((512, 1023), np.float32)
        dev[0:511, :] = wT[1:512, :]
        colsrc = np.zeros((512, 1024), np.float32)
        colsrc[:, 1:1024] = dev
        out_p = colsrc[:, perm]
        out_p[:, 896] = 0.0
        We2Tp[e] = out_p
        r0 = np.zeros(1024, np.float32)
        r0[1:1024] = wT[0, :1023]
        w2r0[e, 0] = r0[perm]
        w2r0[e, 0, 896] = 0.0
        b = np.zeros(1024, np.float32)
        b[1:1024] = W["L1_ffn_be2"][e]
        bp = b[perm]
        bp[896] = 0.0
        be2c[e] = bp.reshape(8, 128).T
    Ws1T = padT(W["L1_ffn_Ws1"], 1024)
    Ws3T = padT(W["L1_ffn_Ws3"], 1024)
    bs1c = bcol(W["L1_ffn_bs1"], 8)
    bs3c = bcol(W["L1_ffn_bs3"], 8)
    w = W["L1_ffn_Ws2"]
    wT = w.T
    dev = np.zeros((1024, 1023), np.float32)
    dev[0:1023, :] = wT[1:1024, :]
    colsrc = np.zeros((1024, 1024), np.float32)
    colsrc[:, 1:1024] = dev
    Ws2Tp = colsrc[:, perm]
    Ws2Tp[:, 896] = 0.0
    r0 = np.zeros(1024, np.float32)
    r0[1:1024] = wT[0, :1023]
    ws2r0 = r0[perm][None]
    ws2r0[0, 896] = 0.0
    b = np.zeros(1024, np.float32)
    b[1:1024] = W["L1_ffn_bs2"]
    bp = b[perm]; bp[896] = 0.0
    bs2c = np.ascontiguousarray(bp.reshape(8, 128).T)
    sel16 = np.zeros((16, 16 * 128), np.float32)
    for e in range(16):
        sel16[e, e * 128:(e + 1) * 128] = 1.0
    return {
        "gwT": np.ascontiguousarray(W["L1_ffn_gate_w"].T),
        "We1T": We1T, "be1c": be1, "We3T": We3T, "be3c": be3,
        "We2Tp": We2Tp, "be2c": be2c, "w2r0": w2r0,
        "Ws1T": Ws1T, "bs1c": bs1c, "Ws3T": Ws3T, "bs3c": bs3c,
        "Ws2Tp": Ws2Tp, "bs2c": bs2c, "ws2r0": ws2r0,
        "onesm": np.ones((128, 128), np.float32),
        "negr2": np.full((1, 128), -2.0, np.float32),
        "sel16": sel16,
        "identf": np.eye(128, dtype=np.float32),
    }


def run_moe(x3, W):
    nc = build_moe()
    hi = moe_host_inputs(W)
    in_maps = []
    for c in range(NCORE):
        b, t0 = c // 4, (c % 4) * TPC
        m = {"x3f": np.ascontiguousarray(x3[b, t0:t0 + TPC, :].T)}
        m.update(hi)
        in_maps.append(m)
    res = run_bass_kernel_spmd(nc, in_maps, core_ids=list(range(NCORE)))
    out = np.zeros((BATCH, SEQ, DM), np.float32)
    for c in range(NCORE):
        b, t0 = c // 4, (c % 4) * TPC
        out[b, t0:t0 + TPC, :] = res.results[c]["outf"].T
    return out


def build_l0ffn(dump_pa=False, dump_kv=-1):
    nc = bass.Bass("TRN2", target_bir_lowering=False)
    T = TPC
    XW = 2561
    NW = 10
    x_d = din(nc, "xin", [DM, XW])
    wqkv_d = din(nc, "Wqkv", [DM, 3 * DM])
    wo_d = din(nc, "Wo", [DM, DM])
    w1_d = din(nc, "W1p", [DM, 5632])
    b1_d = din(nc, "b1p", [128, 44])
    w2_d = din(nc, "W2p", [2816, DM])
    b2_d = din(nc, "b2c", [128, 8])
    lng_d = din(nc, "lngc", [128, 8])
    lnb_d = din(nc, "lnbc", [128, 8])
    qs_d = din(nc, "qscc", [128, 1])
    ks_d = din(nc, "kscc", [128, 1])
    qsw_d = din(nc, "qswc", [128, 1])
    ksw_d = din(nc, "kswc", [128, 1])
    tab_d = din(nc, "tabs", [128, 4 * XW], BF16)
    mskE_d = din(nc, "mskE", [128, 1024], BF16)
    mskO_d = din(nc, "mskO", [128, 1024], BF16)
    mskA_d = din(nc, "mskA", [128, 1024], BF16)
    rot_d = din(nc, "rotm", [128, 128], BF16)
    on64_d = din(nc, "on64", [128, 128])
    onc_d = din(nc, "onesc", [128, 1])
    onr_d = din(nc, "onesr", [1, 128])
    idb_d = din(nc, "identb", [128, 128], BF16)
    pa_d = nc.dram_tensor("pa", [DM, 2304], F32, kind="Internal").ap()
    out_d = dout(nc, "xout", [DM, T])
    kvd = nc.dram_tensor("kvdbg", [128, 16384], BF16,
                         kind="ExternalOutput").ap() if dump_kv >= 0 else None

    P = Prog()
    st = ExitStack()
    sb = lambda name, shape, dt=F32: st.enter_context(nc.sbuf_tensor(name, list(shape), dt))
    ps = lambda name, shape, dt=F32: st.enter_context(nc.psum_tensor(name, list(shape), dt))
    with st:
        xin = sb("xint", [128, NCH * XW])
        wqb = sb("wqbt", [128, NCH * 3072], BF16)
        wob = sb("wobt", [128, NCH * 1024], BF16)
        tabs = sb("tabst", [128, 4 * XW], BF16)
        mskE = sb("mskEt", [128, 1024], BF16)
        mskO = sb("mskOt", [128, 1024], BF16)
        mskA = sb("mskAt", [128, 1024], BF16)
        kroll = sb("krollt", [128, NCH * 512], BF16)
        vroll = sb("vrollt", [128, 4 * 1024], BF16)
        scrA = sb("scrA", [128, NCH * 256])
        xsb = scrA[:, 0:1024].bitcast(BF16)
        khat = scrA[:, 1024:2048].bitcast(BF16)
        outfm = scrA[:].bitcast(BF16)
        qhat = sb("qhatt", [128, NCH * 256], BF16)
        simt = sb("simtt", [128, 512])
        simt2 = mskA[:].bitcast(F32)
        scr2 = (simt, simt2)
        pbf = sb("pbft", [128, 1024], BF16)
        ptb = sb("ptbt", [128, 4 * 256], BF16)
        nmax = sb("nmaxt", [128, 8])
        evbA = pbf[:].bitcast(F32)
        evbB = ptb[:].bitcast(F32)
        qsc = sb("qscb", [128, 1])
        ksc = sb("kscb", [128, 1])
        qsw = sb("qswb", [128, 1])
        ksw = sb("kswb", [128, 1])
        rotm = sb("rotmb", [128, 128], BF16)
        on64 = sb("on64b", [128, 128])
        onc = sb("oncb", [128, 1])
        onr = sb("onrb", [1, 128])
        idb = sb("idbb", [128, 128], BF16)
        lng = sb("lngb", [128, 8])
        lnb = sb("lnbb", [128, 8])
        b1c = sb("b1cb", [128, 44])
        b2c = sb("b2cb", [128, 8])
        pas = xin
        stats = xin[0:1, NCH * 1025 + 256:NCH * 1025 + 256 + 2048]
        sqB = xin[:, 18000:18512]
        hbf = wqb[:].bitcast(BF16)
        xnbt = wob[:].bitcast(BF16)
        gs0 = mskE[:].bitcast(F32)
        gs1 = mskO[:].bitcast(F32)
        w1buf = (scrA[:, 0:1024].bitcast(BF16), scrA[:, 1024:2048].bitcast(BF16))
        w2buf = (vroll[:, 0:2816], kroll[:, 0:2816])

        P1 = ps("P1", [128, 512])
        P2 = ps("P2", [128, 512])
        P3 = ps("P3", [128, 512])
        P4 = ps("P4", [128, 512])
        PB = ps("PBB", [128, 1024])
        PT = ps("PTT", [128, 1024], BF16)
        PT2 = ps("PTT2", [128, 1024], BF16)

        cD = Ctr(st.enter_context(nc.semaphore("cD")))
        cW = Ctr(st.enter_context(nc.semaphore("cW")))
        cE = Ctr(st.enter_context(nc.semaphore("cE")))
        cV = Ctr(st.enter_context(nc.semaphore("cV")))
        cS = Ctr(st.enter_context(nc.semaphore("cS")))
        cO = Ctr(st.enter_context(nc.semaphore("cO")))
        cG = Ctr(st.enter_context(nc.semaphore("cG")))
        CTRS = (cD, cW, cE, cV, cS, cO, cG)
        block = st.enter_context(nc.Block())
        D = Dep(P)

        D.on("sync", lambda E: E.dma_start(
            out=xin[:].rearrange("p (c t) -> p c t", t=XW),
            in_=x_d.rearrange("(c p) t -> p c t", p=128)), writes=("xin",), inc=cD, dma=True)
        D.on("gpsimd", lambda E: E.dma_start(
            out=wqb[:].rearrange("p (c t) -> p c t", t=3072),
            in_=wqkv_d.rearrange("(c p) t -> p c t", p=128)), writes=("wqb",), inc=cG, dma=True)
        D.on("gpsimd", lambda E: E.dma_start(
            out=wob[:].rearrange("p (c t) -> p c t", t=1024),
            in_=wo_d.rearrange("(c p) t -> p c t", p=128)), writes=("wob",), inc=cG, dma=True)
        for nm, dst, key in ((tab_d, tabs[:], "tabs"), (mskE_d, mskE[:], "mskE"),
                             (mskO_d, mskO[:], "mskO"), (mskA_d, mskA[:], "mskA"),
                             (rot_d, rotm[:], "cst"), (idb_d, idb[:], "cst"),
                             (qs_d, qsc[:], "cst"), (ks_d, ksc[:], "cst"),
                             (qsw_d, qsw[:], "cst"), (ksw_d, ksw[:], "cst"),
                             (onc_d, onc[:], "cst"), (onr_d, onr[:], "cst"),
                             (lng_d, lng[:], "cst"), (lnb_d, lnb[:], "cst"),
                             (b1_d, b1c[:], "cst"), (b2_d, b2c[:], "cst"),
                             (on64_d, on64[:], "cst")):
            D.on("sync", lambda E, nm=nm, dst=dst: E.dma_start(out=dst, in_=nm[:]),
                 writes=(key,), inc=cW, dma=True)
        for c0, scl in ((0, qsc), (XW, qsw), (2 * XW, ksc), (3 * XW, ksw)):
            D.on("vector", lambda E, c0=c0, scl=scl: E.tensor_scalar_mul(
                tabs[:, c0:c0 + XW], tabs[:, c0:c0 + XW], scl[:, 0:1]),
                 reads=("tabs", "cst"), writes=("tabs",), inc=cV)

        for k in range(NW):
            par = k % 2
            col = 1 + 256 * k
            for c in range(NCH):
                off = col if c < 4 else col - 1
                D.on("vector", lambda E, c=c, off=off: E.tensor_copy(
                    xsb[:, c * 256:(c + 1) * 256],
                    xin[:, c * XW + off:c * XW + off + 256]),
                     reads=("xin",), writes=("scrA",), inc=cV)
            for which in range(2):
                wofs = which * 1024
                dsth = qhat if which == 0 else khat
                dkey = "qhat" if which == 0 else "scrA"
                tc0 = (0 if which == 0 else 2 * XW)
                ts0 = (XW if which == 0 else 3 * XW)
                for oc in range(NCH):
                    PQ, pk = (P1, "P1") if oc % 2 == 0 else (P2, "P2")
                    pp = (oc % 2) if k >= 3 else 0
                    sk = "simt%d" % pp
                    sqr = scr2[pp][:, 0:256]
                    rsd = scr2[pp][:, 256:512]
                    PL, plk = (P3, "P3") if pp == 0 else (P4, "P4")

                    def mm_q(E, oc=oc, wofs=wofs, PQ=PQ):
                        for c in range(NCH):
                            mm = E.matmul(PQ[:, 0:256],
                                          wqb[:, c * 3072 + wofs + oc * 128:
                                              c * 3072 + wofs + oc * 128 + 128],
                                          xsb[:, c * 256:(c + 1) * 256],
                                          start=(c == 0), stop=(c == NCH - 1))
                        return mm
                    D.on("tensor", mm_q, reads=("wqb", "scrA"), writes=(pk,), inc=cE)
                    D.on("scalar", lambda E, PQ=PQ, sqr=sqr: E.activation(
                        sqr, PQ[:, 0:256], AF.Square),
                         reads=(pk,), writes=(sk,), inc=cS)
                    D.on("tensor", lambda E, sqr=sqr, PL=PL: E.matmul(
                        PL[:, 0:256], on64[:], sqr, start=True, stop=True),
                         reads=(sk, "cst"), writes=(plk,), inc=cE)
                    D.on("scalar", lambda E, rsd=rsd, PL=PL: E.activation(
                        rsd, PL[:, 0:256], AF.Sqrt),
                         reads=(plk,), writes=(sk,), inc=cS)
                    D.on("vector", lambda E, rsd=rsd: E.tensor_scalar_max(rsd, rsd, 1e-12),
                         reads=(sk,), writes=(sk,), inc=cV)
                    D.on("vector", lambda E, rsd=rsd: E.reciprocal(rsd, rsd),
                         reads=(sk,), writes=(sk,), inc=cV)
                    D.on("vector", lambda E, oc=oc, PQ=PQ, dsth=dsth, dkey=dkey, rsd=rsd:
                         E.tensor_tensor(dsth[:, oc * 256:(oc + 1) * 256], PQ[:, 0:256],
                                         rsd, ALU.mult),
                         reads=(pk, sk), writes=(dkey,), inc=cV)
                for oc in range(NCH):
                    PR_, prk = (P3, "P3") if oc % 2 == 0 else (P4, "P4")
                    pp = (oc % 2) if k >= 3 else 0
                    sk = "simt%d" % pp
                    t1 = scr2[pp][:, 0:256]
                    t2 = scr2[pp][:, 256:512]
                    D.on("tensor", lambda E, oc=oc, dsth=dsth, PR_=PR_: E.matmul(
                        PR_[:, 0:256], rotm[:], dsth[:, oc * 256:(oc + 1) * 256],
                        start=True, stop=True), reads=("cst", dkey), writes=(prk,),
                         inc=cE)
                    D.on("vector", lambda E, oc=oc, dsth=dsth, tc0=tc0, col=col, t1=t1:
                         E.tensor_tensor(t1, dsth[:, oc * 256:(oc + 1) * 256],
                                         tabs[:, tc0 + col:tc0 + col + 256], ALU.mult),
                         reads=(dkey, "tabs"), writes=(sk,), inc=cV)
                    D.on("vector", lambda E, ts0=ts0, col=col, PR_=PR_, t2=t2:
                         E.tensor_tensor(t2, PR_[:, 0:256],
                                         tabs[:, ts0 + col:ts0 + col + 256], ALU.mult),
                         reads=(prk, "tabs"), writes=(sk,), inc=cV)
                    if which == 0:
                        D.on("vector", lambda E, oc=oc, t1=t1, t2=t2: E.tensor_add(
                            qhat[:, oc * 256:(oc + 1) * 256], t1, t2),
                             reads=(sk,), writes=("qhat",), inc=cV)
                    else:
                        D.on("vector", lambda E, oc=oc, par=par, t1=t1, t2=t2: E.tensor_add(
                            kroll[:, oc * 512 + par * 256:oc * 512 + par * 256 + 256],
                            t1, t2), reads=(sk,), writes=("kroll%d" % par,), inc=cV)
            for jt in range(2):
                def mm_v(E, jt=jt):
                    for half in range(2):
                        for c in range(NCH):
                            mm = E.matmul(PB[:, half * 512:(half + 1) * 512],
                                          xsb[:, c * 256 + jt * 128:c * 256 + jt * 128 + 128],
                                          wqb[:, c * 3072 + 2048 + half * 512:
                                              c * 3072 + 2048 + (half + 1) * 512],
                                          start=(c == 0), stop=(c == NCH - 1))
                    return mm
                D.on("tensor", mm_v, reads=("scrA", "wqb"), writes=("PBh0", "PBh1"),
                     inc=cE)
                D.on("vector", lambda E, jt=jt, par=par: E.tensor_copy(
                    vroll[:, (par * 2 + jt) * 1024:(par * 2 + jt + 1) * 1024],
                    PB[:, 0:1024]), reads=("PBh0", "PBh1"),
                     writes=("vroll%d" % (par * 2 + jt),), inc=cV)
            if k == dump_kv:
                D.on("sync", lambda E: E.dma_start(out=kvd[:, 0:4096], in_=kroll[:]),
                     reads=("kroll0", "kroll1"), inc=cO, dma=True)
                D.on("sync", lambda E: E.dma_start(out=kvd[:, 4096:8192], in_=vroll[:]),
                     reads=("vroll0", "vroll1", "vroll2", "vroll3"), inc=cO, dma=True)
                D.on("sync", lambda E: E.dma_start(out=kvd[:, 8192:10240], in_=qhat[:]),
                     reads=("qhat",), inc=cO, dma=True)
            if k == 0:
                continue
            msk = mskA if k == 2 else (mskO if par else mskE)
            mkey = "mskA" if k == 2 else ("mskO" if par else "mskE")
            for h in range(16):
                hc, hb = h // 2, (h % 2) * 64
                for it in range(2):
                    PSm, smk = (P1, "P1") if (2 * h + it) % 2 == 0 else (P2, "P2")
                    j4 = (2 * h + it) % 4
                    i2 = (2 * h + it) % 2
                    nmx = nmax[:, j4:j4 + 1]
                    nms = nmax[:, 4 + j4:5 + j4]
                    pb_ = pbf[:, i2 * 512:(i2 + 1) * 512]
                    pkey = "pbf%d" % i2
                    i2s = i2 if k >= 3 else 0
                    sims = scr2[i2s]
                    sik = "simt%d" % i2s

                    def mm_sim(E, hc=hc, hb=hb, it=it, PSm=PSm):
                        return E.matmul(PSm[:, 0:512],
                                        qhat[hb:hb + 64, hc * 256 + it * 128:
                                             hc * 256 + it * 128 + 128],
                                        kroll[hb:hb + 64, hc * 512:(hc + 1) * 512],
                                        start=True, stop=True)
                    D.on("tensor", mm_sim, reads=("qhat", "kroll0", "kroll1"),
                         writes=(smk,), inc=cE)
                    D.on("vector", lambda E, it=it, msk=msk, PSm=PSm, sims=sims:
                         E.scalar_tensor_tensor(sims[:], PSm[:, 0:512], 8.0,
                                                msk[:, it * 512:(it + 1) * 512],
                                                ALU.mult, ALU.add),
                         reads=(smk, mkey), writes=(sik,), inc=cV)
                    D.on("scalar", lambda E, nms=nms, pb_=pb_, sims=sims: E.activation(
                        pb_, sims[:], AF.Exp, accum_out=nms),
                         reads=(sik,), writes=(pkey, "ns%d" % j4), inc=cS)
                    rtok = D.on("vector", lambda E, nmx=nmx, nms=nms:
                                E.reciprocal(nmx, nms),
                                reads=("ns%d" % j4,), writes=("nm%d" % j4,), inc=cV)
                    D.on("vector", lambda E, nmx=nmx, pb_=pb_: E.tensor_scalar_mul(
                        pb_, pb_, nmx), reads=(pkey, "nm%d" % j4), writes=(pkey,),
                         inc=cV, force=(rtok,))
                    PTx, ptk = (PT, "PT") if it == 0 else (PT2, "PT2")

                    def mm_pt(E, PTx=PTx, pb_=pb_):
                        for jc in range(4):
                            mm = E.transpose(PTx[:, jc * 128:(jc + 1) * 128],
                                             pb_[:, jc * 128:(jc + 1) * 128], idb[:])
                        return mm
                    D.on("tensor", mm_pt, reads=(pkey, "cst"), writes=(ptk,), inc=cE)
                    for jc in range(4):
                        D.on("vector", lambda E, jc=jc, it=it, PTx=PTx: E.tensor_copy(
                            ptb[:, jc * 256 + it * 128:jc * 256 + it * 128 + 128],
                            PTx[:, jc * 128:(jc + 1) * 128]),
                             reads=(ptk,), writes=("ptb",), inc=cV)

                if k == dump_kv and h == 0:
                    D.on("sync", lambda E: E.dma_start(out=kvd[:, 14336:15360],
                                                       in_=ptb[:]),
                         reads=("ptb",), inc=cO, dma=True)
                    D.on("sync", lambda E: E.dma_start(out=kvd[:, 15360:16384],
                                                       in_=pbf[:]),
                         reads=("pbf0", "pbf1"), inc=cO, dma=True)
                PAo, pak = (P3, "P3") if h % 2 == 0 else (P4, "P4")

                def mm_av(E, h=h, PAo=PAo):
                    for jc in range(4):
                        mm = E.matmul(PAo[0:64, 0:256],
                                      vroll[:, jc * 1024 + 64 * h:jc * 1024 + 64 * h + 64],
                                      ptb[:, jc * 256:(jc + 1) * 256],
                                      start=(jc == 0), stop=(jc == 3))
                    return mm
                D.on("tensor", mm_av, reads=("ptb", "vroll0", "vroll1", "vroll2",
                                             "vroll3"), writes=(pak,), inc=cE)
                D.on("vector", lambda E, hc=hc, hb=hb, PAo=PAo: E.tensor_copy(
                    outfm[hb:hb + 64, hc * 256:(hc + 1) * 256], PAo[0:64, 0:256]),
                     reads=(pak,), writes=("scrA",), inc=cV)
            if k == dump_kv:
                D.on("sync", lambda E: E.dma_start(out=kvd[:, 10240:14336], in_=outfm[:]),
                     reads=("scrA",), inc=cO, dma=True)
            for oc in range(NCH):
                half = oc % 2

                def mm_wo(E, oc=oc, half=half):
                    for c in range(NCH):
                        mm = E.matmul(PB[:, half * 512:half * 512 + 256],
                                      wob[:, c * 1024 + oc * 128:c * 1024 + oc * 128 + 128],
                                      outfm[:, c * 256:(c + 1) * 256],
                                      start=(c == 0), stop=(c == NCH - 1))
                    return mm
                D.on("tensor", mm_wo, reads=("wob", "scrA"),
                     writes=("PBh%d" % half,), inc=cE)
                D.on("vector", lambda E, oc=oc, col=col, half=half: E.tensor_add(
                    evbA[:, half * 256:(half + 1) * 256],
                    PB[:, half * 512:half * 512 + 256],
                    xin[:, oc * XW + col:oc * XW + col + 256]),
                     reads=("PBh%d" % half, "xin"), writes=("pbf%d" % half,), inc=cV)
                D.on("sync", lambda E, oc=oc, k=k, half=half: E.dma_start(
                    out=pa_d[oc * 128:(oc + 1) * 128, 256 * (k - 1):256 * k],
                    in_=evbA[:, half * 256:(half + 1) * 256]),
                     reads=("pbf%d" % half,), writes=("pa%d" % oc,), inc=cO, dma=True)

        if dump_pa:
            for c in range(NCH):
                D.on("sync", lambda E, c=c: E.dma_start(
                    out=out_d[c * 128:(c + 1) * 128, :],
                    in_=pa_d[c * 128:(c + 1) * 128, 256:2304]),
                     reads=("pa%d" % c,), inc=cO, dma=True)
            P.on("sync", lambda E: E.nop(), after=(cO,))
            P.emit(block)
            return nc

        D.barrier(CTRS)
        for sl_i in range(2):
            pcol = 255 + 1024 * sl_i
            for c in range(NCH):
                D.on("sync", lambda E, c=c, pcol=pcol: E.dma_start(
                    out=pas[:, c * 1025:(c + 1) * 1025],
                    in_=pa_d[c * 128:(c + 1) * 128, pcol:pcol + 1025]),
                     writes=("xin",), inc=cD, dma=True)

            def xs2(c):
                off = 1 if c < 4 else 0
                return pas[:, c * 1025 + off:c * 1025 + off + 1024]
            for s in range(2):
                sl = slice(s * 512, (s + 1) * 512)

                def mm_sum(E, sl=sl):
                    for c in range(NCH):
                        mm = E.matmul(P3[0:1, 0:512], onc[:], xs2(c)[:, sl],
                                      start=(c == 0), stop=(c == NCH - 1))
                    return mm
                D.on("tensor", mm_sum, reads=("xin", "cst"), writes=("P3",), inc=cE)
                D.on("scalar", lambda E: E.copy(stats[:, 0:512], P3[0:1, 0:512]),
                     reads=("P3",), writes=("stats",), inc=cS)
                for c in range(NCH):
                    D.on("scalar", lambda E, c=c, sl=sl: E.activation(
                        sqB, xs2(c)[:, sl], AF.Square), reads=("xin",),
                         writes=("sqB",), inc=cS)
                    D.on("tensor", lambda E, c=c: E.matmul(
                        P4[0:1, 0:512], onc[:], sqB, start=(c == 0),
                        stop=(c == NCH - 1)),
                         reads=("sqB", "cst"), writes=("P4",), inc=cE)
                D.on("scalar", lambda E: E.copy(stats[:, 512:1024], P4[0:1, 0:512]),
                     reads=("P4",), writes=("stats",), inc=cS)
                D.on("vector", lambda E: E.tensor_scalar_mul(
                    stats[:, 0:512], stats[:, 0:512], 1.0 / DM),
                     reads=("stats",), writes=("stats",), inc=cV)
                D.on("vector", lambda E: E.tensor_scalar_mul(
                    stats[:, 512:1024], stats[:, 512:1024], 1.0 / DM),
                     reads=("stats",), writes=("stats",), inc=cV)
                D.on("vector", lambda E: E.tensor_tensor(
                    stats[:, 1024:1536], stats[:, 0:512], stats[:, 0:512], ALU.mult),
                     reads=("stats",), writes=("stats",), inc=cV)
                D.on("vector", lambda E: E.tensor_sub(
                    stats[:, 512:1024], stats[:, 512:1024], stats[:, 1024:1536]),
                     reads=("stats",), writes=("stats",), inc=cV)
                D.on("vector", lambda E: E.tensor_scalar_add(
                    stats[:, 1024:1536], stats[:, 512:1024], 1e-5),
                     reads=("stats",), writes=("stats",), inc=cV)
                D.on("scalar", lambda E: E.activation(
                    stats[:, 1536:2048], stats[:, 1024:1536], AF.Sqrt),
                     reads=("stats",), writes=("stats",), inc=cS)
                D.on("vector", lambda E: E.reciprocal(
                    stats[:, 1024:1536], stats[:, 1536:2048]),
                     reads=("stats",), writes=("stats",), inc=cV)
                D.on("vector", lambda E: E.tensor_tensor(
                    stats[:, 1536:2048], stats[:, 0:512], stats[:, 1024:1536],
                    ALU.mult), reads=("stats",), writes=("stats",), inc=cV)

                def mm_bc(E):
                    E.matmul(PB[:, 0:512], onr[:], stats[:, 1024:1536],
                             start=True, stop=True)
                    return E.matmul(PB[:, 512:1024], onr[:], stats[:, 1536:2048],
                                    start=True, stop=True)
                D.on("tensor", mm_bc, reads=("stats", "cst"),
                     writes=("PBh0", "PBh1"), inc=cE)
                for c in range(NCH):
                    D.on("vector", lambda E, c=c, sl=sl: E.tensor_tensor(
                        sqB, xs2(c)[:, sl], PB[:, 0:512], ALU.mult),
                         reads=("xin", "PBh0"), writes=("sqB",), inc=cV)
                    D.on("vector", lambda E: E.tensor_tensor(
                        sqB, sqB, PB[:, 512:1024], ALU.subtract),
                         reads=("sqB", "PBh1"), writes=("sqB",), inc=cV)
                    D.on("vector", lambda E, c=c, sl=sl: E.tensor_scalar(
                        xnbt[:, c * 1024:(c + 1) * 1024][:, sl], sqB,
                        lng[:, c:c + 1], lnb[:, c:c + 1], ALU.mult, ALU.add),
                         reads=("sqB", "cst"), writes=("xnb",), inc=cV)
            for t in range(22):
                w1b = w1buf[t % 2]
                wk = "w1%d" % (t % 2)
                D.on("gpsimd", lambda E, t=t, w1b=w1b: E.dma_start(
                    out=w1b.rearrange("p (c t) -> p c t", t=256),
                    in_=w1_d[:, 256 * t:256 * (t + 1)].rearrange("(c p) t -> p c t", p=128)),
                     writes=(wk,), inc=cG, dma=True)
                for n2 in range(2):
                    PAx, pakx = (P1, "P1") if n2 == 0 else (P2, "P2")
                    PGx, pgkx = (P3, "P3") if n2 == 0 else (P4, "P4")
                    gsx, gkx = (gs0, "gs0") if n2 == 0 else (gs1, "gs1")

                    def mm_a(E, n2=n2, PAx=PAx, w1b=w1b):
                        for c in range(NCH):
                            mm = E.matmul(PAx[:, 0:512],
                                          w1b[:, c * 256:c * 256 + 128],
                                          xnbt[:, c * 1024 + n2 * 512:c * 1024 + (n2 + 1) * 512],
                                          start=(c == 0), stop=(c == NCH - 1))
                        return mm
                    D.on("tensor", mm_a, reads=(wk, "xnb"), writes=(pakx,), inc=cE)

                    def mm_g(E, n2=n2, PGx=PGx, w1b=w1b):
                        for c in range(NCH):
                            mm = E.matmul(PGx[:, 0:512],
                                          w1b[:, c * 256 + 128:c * 256 + 256],
                                          xnbt[:, c * 1024 + n2 * 512:c * 1024 + (n2 + 1) * 512],
                                          start=(c == 0), stop=(c == NCH - 1))
                        return mm
                    D.on("tensor", mm_g, reads=(wk, "xnb"), writes=(pgkx,), inc=cE)
                    D.on("scalar", lambda E, t=t, PGx=PGx, gsx=gsx: E.activation(
                        gsx[:, 0:512], PGx[:, 0:512], AF.Gelu,
                        bias=b1c[:, 2 * t + 1:2 * t + 2]),
                         reads=(pgkx, "cst"), writes=(gkx,), inc=cS)
                    D.on("vector", lambda E, t=t, n2=n2, PAx=PAx, gsx=gsx:
                         E.scalar_tensor_tensor(
                             hbf[:, t * 1024 + n2 * 512:t * 1024 + (n2 + 1) * 512],
                             PAx[:, 0:512], b1c[:, 2 * t:2 * t + 1], gsx[:, 0:512],
                             ALU.add, ALU.mult), reads=(pakx, "cst", gkx),
                         writes=("hbf",), inc=cV)
            for oc in range(NCH):
                w2b = w2buf[oc % 2]
                w2k = "w2%d" % (oc % 2)
                D.on("gpsimd", lambda E, oc=oc, w2b=w2b: E.dma_start(
                    out=w2b.rearrange("p (c t) -> p c t", t=128),
                    in_=w2_d[:, oc * 128:(oc + 1) * 128].rearrange("(c p) t -> p c t", p=128)),
                     writes=(w2k,), inc=cG, dma=True)
                for n2 in range(2):
                    half = n2

                    def mm_y(E, n2=n2, w2b=w2b):
                        for t in range(22):
                            mm = E.matmul(PB[:, n2 * 512:(n2 + 1) * 512],
                                          w2b[:, t * 128:(t + 1) * 128],
                                          hbf[:, t * 1024 + n2 * 512:t * 1024 + (n2 + 1) * 512],
                                          start=(t == 0), stop=(t == 21))
                        return mm
                    D.on("tensor", mm_y, reads=(w2k, "hbf"),
                         writes=("PBh%d" % half,), inc=cE)
                    evx = evbA if half == 0 else evbB
                    evkeys = ("pbf0", "pbf1") if half == 0 else ("ptb",)
                    D.on("vector", lambda E, oc=oc, n2=n2, evx=evx:
                         E.scalar_tensor_tensor(
                             evx[:, 0:512],
                             PB[:, n2 * 512:(n2 + 1) * 512], b2c[:, oc:oc + 1],
                             pas[:, oc * 1025 + 1 + n2 * 512:oc * 1025 + 1 + (n2 + 1) * 512],
                             ALU.add, ALU.add), reads=("PBh%d" % half, "cst", "xin"),
                         writes=evkeys, inc=cV)
                    D.on("sync", lambda E, oc=oc, n2=n2, sl_i=sl_i, evx=evx:
                         E.dma_start(
                             out=out_d[oc * 128:(oc + 1) * 128,
                                       sl_i * 1024 + n2 * 512:sl_i * 1024 + (n2 + 1) * 512],
                             in_=evx[:, 0:512]),
                         reads=evkeys, writes=("out",), inc=cO, dma=True)
        P.on("sync", lambda E: E.nop(), after=(cO,))
        P.emit(block)
    return nc


def l0ffn_host_consts():
    import ml_dtypes
    XW = 2561
    DH, W = 64, 256
    half = np.arange(0, DH, 2, dtype=np.float32)
    inv_freq = (1.0 / 10000.0 ** (half / DH)).astype(np.float32)
    base_scale = ((half + 0.4 * DH) / (1.4 * DH)).astype(np.float32)
    def mk_mask(parity, window0):
        m = np.full((2, 128, 512), NEG, np.float32)
        for itile in range(2):
            for p in range(128):
                i = itile * 128 + p
                cur0 = 0 if parity == 0 else 256
                prv0 = 256 if parity == 0 else 0
                m[itile, p, cur0 + 0:cur0 + i + 1] = 0.0
                if not window0:
                    m[itile, p, prv0 + i:prv0 + 256] = 0.0
        return m.transpose(1, 0, 2).reshape(128, 1024)
    mskE = mk_mask(0, False)
    mskO = mk_mask(1, False)
    mskA0 = mk_mask(0, True)
    rot = np.zeros((128, 128), np.float32)
    for blk in range(2):
        b = blk * 64
        for d in range(32):
            rot[b + d + 32, b + d] = -1.0
            rot[b + d, b + d + 32] = 1.0
    on64 = np.zeros((128, 128), np.float32)
    on64[0:64, 0:64] = 1.0
    on64[64:128, 64:128] = 1.0
    return {
        "mskE": mskE.astype(ml_dtypes.bfloat16), "mskO": mskO.astype(ml_dtypes.bfloat16),
        "mskA0": mskA0.astype(ml_dtypes.bfloat16),
        "rotm": rot.astype(ml_dtypes.bfloat16),
        "on64": on64,
        "onesc": np.ones((128, 1), np.float32),
        "onesr": np.ones((1, 128), np.float32),
        "identb": np.eye(128, dtype=np.float32).astype(ml_dtypes.bfloat16),
        "inv_freq": inv_freq, "base_scale": base_scale,
    }


def l0ffn_tables(core, consts):
    import ml_dtypes
    XW = 2561
    t0 = (core % 4) * TPC - 513
    t = (t0 + np.arange(XW, dtype=np.float32))
    freqs = t[:, None] * consts["inv_freq"][None, :]
    cos = np.cos(np.concatenate([freqs, freqs], 1))
    sin = np.sin(np.concatenate([freqs, freqs], 1))
    power = (t - SEQ // 2) / (256 // 2)
    scale = consts["base_scale"][None, :] ** power[:, None]
    scale = np.concatenate([scale, scale], 1).astype(np.float32)
    cq = (cos * scale).T
    sq = (sin * scale).T
    ck = (cos / scale).T
    sk = (sin / scale).T
    tab = np.concatenate([np.tile(x, (2, 1)) for x in (cq, sq, ck, sk)], axis=1)
    return np.ascontiguousarray(tab).astype(ml_dtypes.bfloat16)


def l0ffn_layer_weights(inp, i):
    W1 = inp["L0_ffn_W1"][i]
    W2 = inp["L0_ffn_W2"][i]
    b1 = inp["L0_ffn_b1"][i]
    W1p = np.zeros((DM, 5632), np.float32)
    b1p = np.zeros((128, 44), np.float32)
    for tt in range(22):
        lo, hi = 128 * tt, min(128 * (tt + 1), 2730)
        n = hi - lo
        if n > 0:
            W1p[:, 256 * tt:256 * tt + n] = W1[:, lo:hi]
            W1p[:, 256 * tt + 128:256 * tt + 128 + n] = W1[:, 2730 + lo:2730 + hi]
            b1p[0:n, 2 * tt] = b1[lo:hi]
            b1p[0:n, 2 * tt + 1] = b1[2730 + lo:2730 + hi]
    W2p = np.zeros((2816, DM), np.float32)
    W2p[0:2730, :] = W2
    W2pp = np.zeros((2816, DM), np.float32)
    for tt in range(22):
        lo, hi = 128 * tt, min(128 * (tt + 1), 2730)
        n = hi - lo
        if n > 0:
            W2pp[128 * tt:128 * tt + n, :] = W2[lo:hi, :]
    qs0 = inp["L0_ffn_q_scale"][i]
    ks0 = inp["L0_ffn_k_scale"][i]
    qs = np.tile(qs0, 2)[:, None].astype(np.float32)
    ks = np.tile(ks0, 2)[:, None].astype(np.float32)
    qsw = np.tile(np.concatenate([qs0[32:], qs0[:32]]), 2)[:, None].astype(np.float32)
    ksw = np.tile(np.concatenate([ks0[32:], ks0[:32]]), 2)[:, None].astype(np.float32)
    return {
        "Wqkv": inp["L0_ffn_Wqkv"][i],
        "Wo": inp["L0_ffn_Wo"][i],
        "W1p": W1p, "b1p": b1p, "W2p": W2pp,
        "b2c": np.ascontiguousarray(inp["L0_ffn_b2"][i].reshape(8, 128).T),
        "lngc": np.ascontiguousarray(inp["L0_ffn_ln_g"][i].reshape(8, 128).T),
        "lnbc": np.ascontiguousarray(inp["L0_ffn_ln_b"][i].reshape(8, 128).T),
        "qscc": qs, "kscc": ks, "qswc": qsw, "kswc": ksw,
    }


def run_l0ffn_layer(nc_cache, x_full, inp, i, consts, dump_pa=False, dump_kv=-1):
    key = ("l0ffn", dump_pa, dump_kv)
    if nc_cache.get(key) is None:
        nc_cache[key] = build_l0ffn(dump_pa=dump_pa, dump_kv=dump_kv)
    nc = nc_cache[key]
    lw = l0ffn_layer_weights(inp, i)
    in_maps = []
    for c in range(NCORE):
        b, t0 = c // 4, (c % 4) * TPC
        lo = t0 - 513
        if lo >= 0:
            xs = x_full[b, lo:t0 + TPC, :]
        else:
            xs = np.concatenate([np.zeros((513, DM), np.float32), x_full[b, 0:t0 + TPC, :]], 0)
        m = {"xin": np.ascontiguousarray(xs.T),
             "tabs": nc_cache["tabs"][c],
             "mskE": consts["mskE"], "mskO": consts["mskO"],
             "mskA": consts["mskA0"] if t0 == 0 else consts["mskE"],
             "rotm": consts["rotm"], "on64": consts["on64"],
             "onesc": consts["onesc"], "onesr": consts["onesr"],
             "identb": consts["identb"]}
        m.update(lw)
        in_maps.append(m)
    res = run_bass_kernel_spmd(nc, in_maps, core_ids=list(range(NCORE)))
    out = np.zeros((BATCH, SEQ, DM), np.float32)
    for c in range(NCORE):
        b, t0 = c // 4, (c % 4) * TPC
        out[b, t0:t0 + TPC, :] = res.results[c]["xout"].T
    if dump_kv >= 0:
        return out, np.asarray(res.results[0]["kvdbg"])
    return out


def build_ssm():
    nc = bass.Bass("TRN2", target_bir_lowering=False)
    u_d = din(nc, "usig", [2, 128, 8192])
    alog_d = din(nc, "alog", [1, 64])
    ldt_d = din(nc, "ldt", [1, 128])
    cnt_d = din(nc, "cnt", [64, 128])
    drow_d = din(nc, "drow", [1, 128])
    fz_d = din(nc, "Fz", [64, 256])
    wz_d = din(nc, "Wz", [128, 256])
    f2re_d = din(nc, "F2re", [128, 128])
    f2im_d = din(nc, "F2im", [128, 128])
    fiz_d = din(nc, "Fiz", [128, 256])
    fiz2_d = din(nc, "Fiz2", [128, 256])
    w2z_d = din(nc, "W2z", [128, 256])
    f3re_d = din(nc, "F3re", [128, 64])
    f3imn_d = din(nc, "F3imn", [128, 64])
    rampj_d = din(nc, "rampj", [1, 128])
    rampc_d = din(nc, "rampc", [1, 64])
    onr_d = din(nc, "onesr", [1, 128])
    idf_d = din(nc, "identf", [128, 128])
    dta_d = nc.dram_tensor("dtat", [128, 64], F32, kind="Internal").ap()
    y_d = dout(nc, "ysig", [2, 128, 8192])

    P = Prog()
    st = ExitStack()
    sb = lambda name, shape, dt=F32: st.enter_context(nc.sbuf_tensor(name, list(shape), dt))
    ps = lambda name, shape, dt=F32: st.enter_context(nc.psum_tensor(name, list(shape), dt))
    with st:
        fz = sb("fz_s", [64, 256], F32R)
        wz = sb("wz_s", [128, 256])
        f2re = sb("f2re_s", [128, 128], F32R)
        f2im = sb("f2im_s", [128, 128], F32R)
        fiz = sb("fiz_s", [128, 256], F32R)
        fiz2 = sb("fiz2_s", [128, 256], F32R)
        w2z = sb("w2z_s", [128, 256])
        f3re = sb("f3re_s", [128, 64], F32R)
        f3imn = sb("f3imn_s", [128, 64], F32R)
        rampj = sb("rampj_s", [1, 128])
        rampc = sb("rampc_s", [1, 64])
        onr = sb("onr_s", [1, 128])
        idf = sb("idf_s", [128, 128])
        arow = sb("arow_s", [1, 64])
        dtrow = sb("dtrow_s", [1, 128])
        cnt = sb("cnt_s", [64, 128])
        dbc = sb("dbc_s", [64, 128])
        dtar = sb("dtar_s", [1, 8192])
        uslab = sb("uslab_s", [64, 2 * 8 * 128], F32R)
        rj = sb("rj_s", [64, 128], F32R)
        rc = sb("rc_s", [64, 64], F32R)
        crc = sb("crc_s", [64, 64], F32R)
        kst = sb("kst_s", [64, 128], F32R)
        tmp1 = sb("tmp1_s", [128, 128])
        tmp2 = sb("tmp2_s", [128, 128])
        bre = sb("bre_s", [128, 256], F32R)
        bsw = sb("bsw_s", [128, 256], F32R)
        xk = sb("xk_s", [128, 256], F32R)
        xu = sb("xu_s", [128, 256], F32R)
        pre = sb("pre_s", [128, 256], F32R)
        b2 = sb("b2_s", [128, 256], F32R)
        ysb = sb("ysb_s", [64, 128])

        PA = ps("PA", [128, 256])
        PX = ps("PX", [128, 256])
        PG = ps("PG", [128, 256])
        PY = ps("PY", [64, 128])
        PK = ps("PK", [128, 128])

        cD = Ctr(st.enter_context(nc.semaphore("cD")))
        cW = Ctr(st.enter_context(nc.semaphore("cW")))
        cE = Ctr(st.enter_context(nc.semaphore("cE")))
        cV = Ctr(st.enter_context(nc.semaphore("cV")))
        cS = Ctr(st.enter_context(nc.semaphore("cS")))
        cO = Ctr(st.enter_context(nc.semaphore("cO")))
        cG = Ctr(st.enter_context(nc.semaphore("cG")))
        P.extra["tensor"] = (cV, cS, cG)
        P.extra["vector"] = (cE, cG)
        P.extra["scalar"] = (cE, cG)
        block = st.enter_context(nc.Block())

        for nm, dst in ((fz_d, fz), (f2re_d, f2re), (f2im_d, f2im), (fiz_d, fiz),
                        (fiz2_d, fiz2), (f3re_d, f3re), (f3imn_d, f3imn)):
            P.on("gpsimd", lambda E, nm=nm, dst=dst: E.dma_start(out=dst[:], in_=nm[:]),
                 inc=cG, dma=True)
        for nm, dst in ((wz_d, wz), (w2z_d, w2z), (rampj_d, rampj), (rampc_d, rampc),
                        (onr_d, onr), (idf_d, idf), (alog_d, arow), (ldt_d, dtrow),
                        (cnt_d, cnt)):
            P.on("sync", lambda E, nm=nm, dst=dst: E.dma_start(out=dst[:], in_=nm[:]),
                 inc=cW, dma=True)

        P.on("scalar", lambda E: E.activation(arow[:], arow[:], AF.Exp), after=(cW,), inc=cS)
        P.on("vector", lambda E: E.tensor_scalar_mul(arow[:], arow[:], -1.0),
             after=(cS,), inc=cV)
        P.on("scalar", lambda E: E.activation(dtrow[:], dtrow[:], AF.Exp), after=(cW,), inc=cS)
        P.on("tensor", lambda E: E.matmul(PA[0:64, 0:128], arow[:], dtrow[:],
                                          start=True, stop=True), after=(cV, cS), inc=cE)
        P.on("vector", lambda E: E.tensor_copy(tmp1[0:64, 0:128], PA[0:64, 0:128]),
             after=(cE,), inc=cV)
        P.on("tensor", lambda E: E.transpose(PK[:, 0:64], tmp1[0:64, 0:128], idf[0:64, 0:64]),
             after=(cV,), inc=cE)
        P.on("vector", lambda E: E.tensor_copy(tmp2[:, 0:64], PK[:, 0:64]),
             after=(cE,), inc=cV)
        P.on("sync", lambda E: E.dma_start(out=dta_d[:], in_=tmp2[:, 0:64]),
             after=(cV,), inc=cO, dma=True)
        P.on("sync", lambda E: E.dma_start(out=dtar[:],
                                           in_=dta_d.rearrange("a b -> (a b)").rearrange(
                                               "x -> () x")), after=(cO,), inc=cD, dma=True)
        P.on("tensor", lambda E: E.matmul(PK[0:64, 0:128], onr[0:1, 0:64], drow_d[:]
                                          if False else dbc[0:1, :],
                                          start=True, stop=True), after=(cV, cS), inc=cE
             ) if False else None
        P.on("sync", lambda E: E.dma_start(out=dbc[0:1, :], in_=drow_d[:]), inc=cW, dma=True)
        P.on("tensor", lambda E: E.matmul(PK[0:64, 0:128], onr[0:1, 0:64], dbc[0:1, :],
                                          start=True, stop=True), after=(cW, cV, cS), inc=cE)
        P.on("vector", lambda E: E.tensor_copy(dbc[:], PK[0:64, 0:128]), after=(cE,), inc=cV)

        def twiddle(P_, psrc, wtile, bout, bswout, tag):
            P_.on("vector", lambda E: E.tensor_tensor(tmp1[:], psrc[:, 0:128],
                                                      wtile[:, 0:128], ALU.mult),
                  after=(cE,), inc=cV)
            P_.on("vector", lambda E: E.tensor_tensor(tmp2[:], psrc[:, 128:256],
                                                      wtile[:, 128:256], ALU.mult),
                  after=(cE,), inc=cV)
            P_.on("vector", lambda E: E.tensor_sub(bout[:, 0:128], tmp1[:], tmp2[:]),
                  after=(cV,), inc=cV)
            P_.on("vector", lambda E: E.tensor_tensor(tmp1[:], psrc[:, 0:128],
                                                      wtile[:, 128:256], ALU.mult),
                  after=(cV,), inc=cV)
            P_.on("vector", lambda E: E.tensor_tensor(tmp2[:], psrc[:, 128:256],
                                                      wtile[:, 0:128], ALU.mult),
                  after=(cE, cV), inc=cV)
            P_.on("vector", lambda E: E.tensor_add(bout[:, 128:256], tmp1[:], tmp2[:]),
                  after=(cV,), inc=cV)
            if bswout is not None:
                P_.on("vector", lambda E: E.tensor_scalar_mul(bswout[:, 0:128],
                                                              bout[:, 128:256], -1.0),
                      after=(cV,), inc=cV)
                P_.on("vector", lambda E: E.tensor_copy(bswout[:, 128:256], bout[:, 0:128]),
                      after=(cV,), inc=cV)

        def fwd_fft(P_, lhs_st, xout):
            P_.on("tensor", lambda E: E.matmul(PA[:], lhs_st, fz[:], start=True, stop=True),
                  after=(cD, cW), inc=cE)
            twiddle(P_, PA, wz, bre, bsw, "f")

            def mm2(E):
                E.matmul(PX[:], f2re[:], bre[:], start=True, stop=False)
                return E.matmul(PX[:], f2im[:], bsw[:], start=False, stop=True)
            P_.on("tensor", mm2, after=(cV,), inc=cE)
            P_.on("vector", lambda E, xout=xout: E.tensor_copy(xout[:], PX[:]),
                  after=(cE,), inc=cV)

        for mg in range(16):
            for b_ in range(2):
                P.on("gpsimd", lambda E, mg=mg, b_=b_: E.dma_start(
                    out=uslab[:, b_ * 1024:(b_ + 1) * 1024].rearrange(
                        "p (m t) -> p m t", t=128),
                    in_=u_d[b_, mg * 8:(mg + 1) * 8, :].rearrange(
                        "m (c t) -> c m t", c=64)), after=(cE, cV), inc=cG, dma=True)
            for ml in range(8):
                m = mg * 8 + ml
                P.on("tensor", lambda E, m=m: E.matmul(
                    PK[0:64, 0:128], dtar[0:1, m * 64:(m + 1) * 64], rampj[:],
                    start=True, stop=True), after=(cD, cV, cS), inc=cE)
                P.on("scalar", lambda E: E.activation(rj[:], PK[0:64, 0:128], AF.Exp),
                     after=(cE,), inc=cS)
                P.on("tensor", lambda E, m=m: E.matmul(
                    PK[0:64, 0:64], dtar[0:1, m * 64:(m + 1) * 64], rampc[:],
                    start=True, stop=True), after=(cS, cV), inc=cE)
                P.on("scalar", lambda E: E.activation(rc[:], PK[0:64, 0:64], AF.Exp),
                     after=(cE,), inc=cS)
                P.on("vector", lambda E, m=m: E.tensor_scalar_mul(crc[:], rc[:],
                                                                 cnt[:, m:m + 1]),
                     after=(cS,), inc=cV)
                P.on("tensor", lambda E: E.matmul(PK[0:64, 0:128], crc[:], rj[:],
                                                  start=True, stop=True),
                     after=(cV,), inc=cE)
                P.on("vector", lambda E: E.tensor_copy(kst[:], PK[0:64, 0:128]),
                     after=(cE,), inc=cV)
                fwd_fft(P, kst[:], xk)
                for b in range(2):
                    fwd_fft(P, uslab[:, (b * 8 + ml) * 128:(b * 8 + ml + 1) * 128], xu)
                    P.on("vector", lambda E: E.tensor_tensor(tmp1[:], xu[:, 0:128],
                                                             xk[:, 0:128], ALU.mult),
                         after=(cV,), inc=cV)
                    P.on("vector", lambda E: E.tensor_tensor(tmp2[:], xu[:, 128:256],
                                                             xk[:, 128:256], ALU.mult),
                         after=(cV,), inc=cV)
                    P.on("vector", lambda E: E.tensor_sub(pre[:, 0:128], tmp1[:], tmp2[:]),
                         after=(cV,), inc=cV)
                    P.on("vector", lambda E: E.tensor_tensor(tmp1[:], xu[:, 0:128],
                                                             xk[:, 128:256], ALU.mult),
                         after=(cV,), inc=cV)
                    P.on("vector", lambda E: E.tensor_tensor(tmp2[:], xu[:, 128:256],
                                                             xk[:, 0:128], ALU.mult),
                         after=(cV,), inc=cV)
                    P.on("vector", lambda E: E.tensor_add(pre[:, 128:256], tmp1[:], tmp2[:]),
                         after=(cV,), inc=cV)
                    def mmg(E):
                        E.matmul(PG[:], pre[:, 0:128], fiz[:], start=True, stop=False)
                        return E.matmul(PG[:], pre[:, 128:256], fiz2[:], start=False,
                                        stop=True)
                    P.on("tensor", mmg, after=(cV,), inc=cE)
                    twiddle(P, PG, w2z, b2, None, "i")
                    def mmy(E):
                        E.matmul(PY[:], f3re[:], b2[:, 0:128], start=True, stop=False)
                        return E.matmul(PY[:], f3imn[:], b2[:, 128:256], start=False,
                                        stop=True)
                    P.on("tensor", mmy, after=(cV,), inc=cE)
                    P.on("vector", lambda E, b=b, ml=ml, m=m: E.scalar_tensor_tensor(
                        ysb[:], uslab[:, (b * 8 + ml) * 128:(b * 8 + ml + 1) * 128],
                        dbc[:, m:m + 1], PY[:], ALU.mult, ALU.add), after=(cE,), inc=cV)
                    P.on("sync", lambda E, b=b, m=m: E.dma_start(
                        out=y_d[b, m, :].rearrange("(c t) -> c t", c=64), in_=ysb[:]),
                         after=(cV,), inc=cO, dma=True)
                    P.on("vector", lambda E: E.tensor_copy(tmp1[0:1, 0:1], tmp1[0:1, 0:1]),
                         after=(cO,), inc=cV)
        P.on("sync", lambda E: E.nop(), after=(cO,))
        P.emit(block)
    return nc


def ssm_host_consts():
    N = 16384
    n2 = np.arange(128)[:, None]
    k1 = np.arange(128)[None, :]
    F_re = np.cos(2 * np.pi * n2 * k1 / 128)
    F_im = -np.sin(2 * np.pi * n2 * k1 / 128)
    Fz = np.concatenate([F_re[0:64], F_im[0:64]], 1).astype(np.float32)
    W_re = np.cos(2 * np.pi * n2 * k1 / N)
    W_im = -np.sin(2 * np.pi * n2 * k1 / N)
    Wz = np.concatenate([W_re, W_im], 1).astype(np.float32)
    F2re = F_re.astype(np.float32)
    F2im = F_im.astype(np.float32)
    a = np.arange(128)[None, :]
    kk = np.arange(128)[:, None]
    Fi_re = np.cos(2 * np.pi * kk * a / 128)
    Fi_im = np.sin(2 * np.pi * kk * a / 128)
    Fiz = np.concatenate([Fi_re, Fi_im], 1).astype(np.float32)
    Fiz2 = np.concatenate([-Fi_im, Fi_re], 1).astype(np.float32)
    W2z = np.concatenate([np.cos(2 * np.pi * kk * a / N),
                          np.sin(2 * np.pi * kk * a / N)], 1).astype(np.float32)
    t1 = np.arange(64)[None, :]
    F3re = (np.cos(2 * np.pi * kk * t1 / 128) / N).astype(np.float32)
    F3imn = (-np.sin(2 * np.pi * kk * t1 / 128) / N).astype(np.float32)
    return {
        "Fz": Fz, "Wz": Wz, "F2re": F2re, "F2im": F2im, "Fiz": Fiz, "Fiz2": Fiz2,
        "W2z": W2z, "F3re": F3re, "F3imn": F3imn,
        "rampj": np.arange(128, dtype=np.float32)[None],
        "rampc": (128.0 * np.arange(64, dtype=np.float32))[None],
        "onesr": np.ones((1, 128), np.float32),
        "identf": np.eye(128, dtype=np.float32),
    }


def run_ssm(inp):
    nc = build_ssm()
    consts = ssm_host_consts()
    u = inp["L0_attn_u"]
    in_maps = []
    for c in range(NCORE):
        sl = slice(c * 128, (c + 1) * 128)
        m = {
            "usig": np.ascontiguousarray(u[:, :, sl].transpose(0, 2, 1)),
            "alog": inp["L0_attn_A_log"][None],
            "ldt": inp["L0_attn_log_dt"][None, sl],
            "cnt": np.ascontiguousarray(inp["L0_attn_C_re"][sl, :].T),
            "drow": inp["L0_attn_D"][None, sl],
        }
        m.update(consts)
        in_maps.append(m)
    res = run_bass_kernel_spmd(nc, in_maps, core_ids=list(range(NCORE)))
    y = np.zeros((BATCH, SEQ, DM), np.float32)
    for c in range(NCORE):
        sl = slice(c * 128, (c + 1) * 128)
        y[:, :, sl] = res.results[c]["ysig"].transpose(0, 2, 1)
    return y


def build_ln0():
    nc = bass.Bass("TRN2", target_bir_lowering=False)
    T = TPC
    u_dd = din(nc, "uf", [DM, T])
    y_dd = din(nc, "yf", [DM, T])
    onc_d = din(nc, "onesc", [128, 1])
    onr_d = din(nc, "onesr", [1, 128])
    x1_d = dout(nc, "x1f", [DM, T])

    P = Prog()
    st = ExitStack()
    sb = lambda name, shape, dt=F32: st.enter_context(nc.sbuf_tensor(name, list(shape), dt))
    ps = lambda name, shape, dt=F32: st.enter_context(nc.psum_tensor(name, list(shape), dt))
    with st:
        A = sb("A_l", [128, NCH * T])
        Bs = sb("B_l", [128, NCH * T])
        sq = sb("sq_l", [128, 512])
        stats = sb("st_l", [1, 2048])
        onc = sb("onc_l", [128, 1])
        onr = sb("onr_l", [1, 128])
        psS = ps("psS", [1, 1024])
        psB = ps("psB", [128, 1024])
        cD = Ctr(st.enter_context(nc.semaphore("cD")))
        cW = Ctr(st.enter_context(nc.semaphore("cW")))
        cE = Ctr(st.enter_context(nc.semaphore("cE")))
        cV = Ctr(st.enter_context(nc.semaphore("cV")))
        cS = Ctr(st.enter_context(nc.semaphore("cS")))
        cO = Ctr(st.enter_context(nc.semaphore("cO")))
        cG = Ctr(st.enter_context(nc.semaphore("cG")))
        P.extra["tensor"] = (cV, cS, cG)
        P.extra["vector"] = (cE, cG)
        P.extra["scalar"] = (cE, cG)
        block = st.enter_context(nc.Block())
        P.on("sync", lambda E: E.dma_start(
            out=A[:].rearrange("p (c t) -> p c t", t=T),
            in_=u_dd.rearrange("(c p) t -> p c t", p=128)), inc=cD, dma=True)
        P.on("sync", lambda E: E.dma_start(
            out=Bs[:].rearrange("p (c t) -> p c t", t=T),
            in_=y_dd.rearrange("(c p) t -> p c t", p=128)), inc=cD, dma=True)
        P.on("sync", lambda E: E.dma_start(out=onc[:], in_=onc_d[:]), inc=cW, dma=True)
        P.on("sync", lambda E: E.dma_start(out=onr[:], in_=onr_d[:]), inc=cW, dma=True)
        for c in range(NCH):
            P.on("vector", lambda E, c=c: E.tensor_add(
                Bs[:, c * T:(c + 1) * T], Bs[:, c * T:(c + 1) * T],
                A[:, c * T:(c + 1) * T]), after=(cD, cW), inc=cV)
        emit_ln(P, cE, cV, cS, T, lambda c: Bs[:, c * T:(c + 1) * T],
                lambda c: A[:, c * T:(c + 1) * T], None, onc, onr, psS, psB, sq, stats)
        for c in range(NCH):
            P.on("sync", lambda E, c=c: E.dma_start(out=x1_d[c * 128:(c + 1) * 128, :],
                                                    in_=A[:, c * T:(c + 1) * T]),
                 after=(cV,), inc=cO, dma=True)
        P.on("sync", lambda E: E.nop(), after=(cO,))
        P.emit(block)
    return nc


def run_ln0(u, y):
    nc = build_ln0()
    onesc = np.ones((128, 1), np.float32)
    onesr = np.ones((1, 128), np.float32)
    in_maps = []
    for c in range(NCORE):
        b, t0 = c // 4, (c % 4) * TPC
        in_maps.append({"uf": np.ascontiguousarray(u[b, t0:t0 + TPC, :].T),
                        "yf": np.ascontiguousarray(y[b, t0:t0 + TPC, :].T),
                        "onesc": onesc, "onesr": onesr})
    res = run_bass_kernel_spmd(nc, in_maps, core_ids=list(range(NCORE)))
    x1 = np.zeros((BATCH, SEQ, DM), np.float32)
    for c in range(NCORE):
        b, t0 = c // 4, (c % 4) * TPC
        x1[b, t0:t0 + TPC, :] = res.results[c]["x1f"].T
    return x1


def kernel(**inputs):
    inp = {k: np.asarray(v) for k, v in inputs.items()}
    _ = inp.get("L0_attn_C_im")
    y = run_ssm(inp)
    x1 = run_ln0(inp["L0_attn_u"], y)
    consts = l0ffn_host_consts()
    cache = {"tabs": [l0ffn_tables(c, consts) for c in range(NCORE)]}
    x = x1
    for i in range(6):
        x = run_l0ffn_layer(cache, x, inp, i, consts)
    x3 = run_att(x1, x, inp)
    out = run_moe(x3, inp)
    return out


# version 09 (final)
# speedup vs baseline: 1.0102x; 1.0102x over previous
import math
import numpy as np
import concourse.bass as bass
import concourse.mybir as mybir
from concourse.bass_utils import run_bass_kernel_spmd
from concourse.bass_core_rust import IndirectOffsetOnAxis
from contextlib import ExitStack

F32 = mybir.dt.float32
F32R = mybir.dt.float32r
BF16 = mybir.dt.bfloat16
U32 = mybir.dt.uint32
AF = mybir.ActivationFunctionType
ALU = mybir.AluOpType
AX = mybir.AxisListType

SERIAL_MOE = True
NCORE = 8
BATCH, SEQ, DM = 2, 8192, 1024
TPC = 2048
NCH = 8
NEG = -1.0e30


class Ctr:
    __slots__ = ("h", "n")

    def __init__(self, h):
        self.h = h
        self.n = 0


class Prog:
    ENGS = ("sync", "gpsimd", "tensor", "vector", "scalar")

    def __init__(self):
        self.ops = []
        self.extra = {}

    def on(self, eng, fn, after=(), inc=None, dma=False, waits=None, noextra=False):
        if not noextra:
            after = tuple(after) + tuple(c for c in self.extra.get(eng, ())
                                         if c not in after)
        wl = [(c, c.n) for c in after if c.n > 0]
        if waits:
            wl += [(c, v) for (c, v) in waits if v > 0]
        tgt = None
        if inc is not None:
            amt = 16 if dma else 1
            inc.n += amt
            tgt = (inc, amt)
        self.ops.append((eng, fn, tuple(wl), tgt))
        return (inc, inc.n) if inc is not None else None

    def emit(self, block):
        by = {}
        for op in self.ops:
            by.setdefault(op[0], []).append(op)

        def mk(items):
            def body(E):
                done = {}
                for _, fn, waits, tgt in items:
                    for c, v in waits:
                        if done.get(id(c), 0) < v:
                            E.wait_ge(c.h, v)
                            done[id(c)] = v
                    inst = fn(E)
                    if tgt is not None:
                        c, amt = tgt
                        inst.then_inc(c.h, amt)
            return body

        for eng in self.ENGS:
            if eng in by:
                getattr(block, eng)(mk(by[eng]))


class Dep:

    def __init__(self, P):
        self.P = P
        self.w = {}
        self.r = {}
        self.chain = False
        self.last = None

    def on(self, eng, fn, reads=(), writes=(), inc=None, dma=False, force=()):
        need = {}
        for k in reads:
            t = self.w.get(k)
            if t is not None and t[0] != eng:
                c, v = t[1], t[2]
                need[id(c)] = (c, max(v, need.get(id(c), (c, 0))[1]))
        for k in writes:
            for t in ([self.w[k]] if k in self.w else []) + self.r.get(k, []):
                if t[0] != eng:
                    c, v = t[1], t[2]
                    need[id(c)] = (c, max(v, need.get(id(c), (c, 0))[1]))
        if self.chain and self.last is not None:
            force = tuple(force) + (self.last,)
        for t in force:
            if t is not None:
                c, v = t[1], t[2]
                need[id(c)] = (c, max(v, need.get(id(c), (c, 0))[1]))
        tok = self.P.on(eng, fn, inc=inc, dma=dma,
                        waits=list(need.values()), noextra=True)
        t3 = (eng, tok[0], tok[1]) if tok is not None else None
        if self.chain and t3 is not None:
            self.last = t3
        if t3 is not None:
            for k in writes:
                self.w[k] = t3
                self.r[k] = []
            for k in reads:
                self.r.setdefault(k, []).append(t3)
        return t3

    def barrier(self, ctrs):
        wl = [(c, c.n) for c in ctrs if c.n > 0]
        for eng in Prog.ENGS:
            self.P.on(eng, lambda E: E.nop(), waits=wl, noextra=True)
        self.w = {}
        self.r = {}


def din(nc, name, shape, dtype=F32):
    return nc.dram_tensor(name, list(shape), dtype, kind="ExternalInput").ap()


def dout(nc, name, shape, dtype=F32):
    return nc.dram_tensor(name, list(shape), dtype, kind="ExternalOutput").ap()


def emit_ln(P, cE, cV, cS, n, src, out_f32, out_bf, onc, onr, psS, psB, sq, stats,
            eps=1e-5, gain=None, bias=None, phases=3):
    nsl = n // 512
    SU, SQ_, TA, TB = 0, 512, 1024, 1536
    for s in range(nsl):
        sl = slice(s * 512, (s + 1) * 512)

        if phases < 1:
            continue

        def mm_sum(E, sl=sl):
            for c in range(NCH):
                mm = E.matmul(psS[:, 0:512], onc[:], src(c)[:, sl],
                              start=(c == 0), stop=(c == NCH - 1))
            return mm
        P.on("tensor", mm_sum, after=(cV, cS), inc=cE)
        P.on("scalar", lambda E: E.copy(stats[:, SU:SU + 512], psS[:, 0:512]),
             after=(cE,), inc=cS)
        for c in range(NCH):
            P.on("scalar", lambda E, c=c, sl=sl: E.activation(
                sq[:, 0:512], src(c)[:, sl], AF.Square), after=(cE,), inc=cS)

            def mm_sq(E, c=c):
                return E.matmul(psS[:, 512:1024], onc[:], sq[:, 0:512],
                                start=(c == 0), stop=(c == NCH - 1))
            P.on("tensor", mm_sq, after=(cS,), inc=cE)
        P.on("scalar", lambda E: E.copy(stats[:, SQ_:SQ_ + 512], psS[:, 512:1024]),
             after=(cE,), inc=cS)
        if phases < 2:
            continue
        P.on("vector", lambda E: E.tensor_scalar_mul(stats[:, SU:SU + 512],
                                                     stats[:, SU:SU + 512], 1.0 / DM),
             after=(cS,), inc=cV)
        P.on("vector", lambda E: E.tensor_scalar_mul(stats[:, SQ_:SQ_ + 512],
                                                     stats[:, SQ_:SQ_ + 512], 1.0 / DM),
             after=(cS,), inc=cV)
        P.on("vector", lambda E: E.tensor_tensor(stats[:, TA:TA + 512], stats[:, SU:SU + 512],
                                                 stats[:, SU:SU + 512], ALU.mult),
             after=(cV,), inc=cV)
        P.on("vector", lambda E: E.tensor_sub(stats[:, SQ_:SQ_ + 512], stats[:, SQ_:SQ_ + 512],
                                              stats[:, TA:TA + 512]), after=(cV,), inc=cV)
        P.on("vector", lambda E: E.tensor_scalar_add(stats[:, TA:TA + 512],
                                                     stats[:, SQ_:SQ_ + 512], float(eps)),
             after=(cV,), inc=cV)
        P.on("scalar", lambda E: E.activation(stats[:, TB:TB + 512], stats[:, TA:TA + 512],
                                              AF.Sqrt), after=(cV,), inc=cS)
        P.on("vector", lambda E: E.reciprocal(stats[:, TA:TA + 512], stats[:, TB:TB + 512]),
             after=(cS,), inc=cV)
        P.on("vector", lambda E: E.tensor_tensor(stats[:, TB:TB + 512], stats[:, SU:SU + 512],
                                                 stats[:, TA:TA + 512], ALU.mult),
             after=(cV,), inc=cV)

        if phases < 3:
            continue

        def mm_bc(E):
            E.matmul(psB[:, 0:512], onr[:], stats[:, TA:TA + 512], start=True, stop=True)
            return E.matmul(psB[:, 512:1024], onr[:], stats[:, TB:TB + 512],
                            start=True, stop=True)
        P.on("tensor", mm_bc, after=(cV,), inc=cE)
        for c in range(NCH):
            P.on("vector", lambda E, c=c, sl=sl: E.tensor_tensor(
                sq[:, 0:512], src(c)[:, sl], psB[:, 0:512], ALU.mult),
                 after=(cE, cV), inc=cV)
            dst1 = out_f32 if out_f32 is not None else out_bf
            if gain is None:
                P.on("vector", lambda E, c=c, sl=sl, dst1=dst1: E.tensor_tensor(
                    dst1(c)[:, sl], sq[:, 0:512], psB[:, 512:1024], ALU.subtract),
                     after=(cV,), inc=cV)
            else:
                P.on("vector", lambda E, c=c, sl=sl: E.tensor_tensor(
                    sq[:, 0:512], sq[:, 0:512], psB[:, 512:1024], ALU.subtract),
                     after=(cV,), inc=cV)
                P.on("vector", lambda E, c=c, sl=sl, dst1=dst1: E.tensor_scalar(
                    dst1(c)[:, sl], sq[:, 0:512], gain(c), bias(c),
                    ALU.mult, ALU.add), after=(cV,), inc=cV)
            if out_bf is not None and out_f32 is not None:
                P.on("vector", lambda E, c=c, sl=sl: E.tensor_copy(
                    out_bf(c)[:, sl], out_f32(c)[:, sl]), after=(cV,), inc=cV)


def build_att(upto=99, lnph=3, store_bs=False):
    nc = bass.Bass("TRN2", target_bir_lowering=False)
    T = TPC
    x1_d = din(nc, "x1f", [DM, T])
    ev_d = din(nc, "evf", [DM, T])
    wqk_d = din(nc, "Wqk", [DM, 128])
    bqk_d = din(nc, "bqk", [1, 128])
    wv_d = din(nc, "Wv", [DM, DM])
    bv_d = din(nc, "bv", [1, DM])
    gam_d = din(nc, "gam", [128, 2])
    bet_d = din(nc, "bet", [128, 2])
    rel_d = din(nc, "rel", [32, 1])
    oh_d = din(nc, "oh", [32, 1024])
    c0_d = din(nc, "c0", [1, 1024])
    idx_d = din(nc, "tidx", [128, 4], U32)
    onc_d = din(nc, "onesc", [128, 1])
    onr_d = din(nc, "onesr", [1, 128])
    idb_d = din(nc, "identb", [128, 128], BF16)
    idf_d = din(nc, "identf", [128, 128])
    pvec_d = nc.dram_tensor("pvec", [1024, 1], F32, kind="Internal").ap()
    x3_d = dout(nc, "x3f", [DM, T])

    P = Prog()
    _sec = [0]
    _real_on = P.on
    def _gated_on(eng, fn, after=(), inc=None, dma=False):
        if _sec[0] <= upto:
            _real_on(eng, fn, after=after, inc=inc, dma=dma)
    P.on = _gated_on
    st = ExitStack()
    sb = lambda name, shape, dt=F32: st.enter_context(nc.sbuf_tensor(name, list(shape), dt))
    ps = lambda name, shape, dt=F32: st.enter_context(nc.psum_tensor(name, list(shape), dt))
    with st:
        A = sb("A", [128, NCH * T])
        Bs = sb("Bs", [128, NCH * T])
        x2bs = sb("x2bs", [128, NCH * 512], BF16)
        vxb = x2bs[:, 0:NCH * 128]
        qt = sb("qt", [128, T], BF16)
        kt = sb("kt", [128, T], BF16)
        vt = sb("vt", [128, 4 * 1024], BF16)
        wqkb = sb("wqkb", [128, NCH * 128], BF16)
        wvb = sb("wvb", [128, NCH * 1024], BF16)
        bvb = sb("bvb", [128, DM])
        bqkc = sb("bqkc", [128, 1])
        g0 = sb("g0", [128, 4])
        relv = sb("relv", [32, 1])
        prow = sb("prow", [1, 1024])
        prow2 = sb("prow2", [1, 1024])
        c0v = prow2[0:1, 0:1024]
        idxv = sb("idxv", [128, 4], U32)
        onc = sb("onc", [128, 1])
        onr = sb("onr_s", [1, 128])
        idb = sb("idb", [128, 128], BF16)
        idf = sb("idf_s", [128, 128])
        btile = sb("btile", [128, 4 * 512])
        sq = sb("sq", [128, 512])
        stats = sb("stats", [1, 2048])
        pbf = sb("pbft", [128, 512], BF16)
        ptb = sb("ptbt", [128, 4 * 128], BF16)
        nmax = sb("nmaxt", [128, 1])
        rsum = sb("rsum", [128, 2])
        attn = sb("attn", [128, 1024])
        ohv = attn[0:32, 0:1024]

        psS = ps("psS", [1, 1024])
        psB = ps("psB", [128, 1024])
        psM = ps("psM", [128, 512])
        psV = ps("psV", [128, 1024])
        psP = ps("psP", [128, 512], BF16)
        psA = psV

        cD = Ctr(st.enter_context(nc.semaphore("cD")))
        cW = Ctr(st.enter_context(nc.semaphore("cW")))
        cE = Ctr(st.enter_context(nc.semaphore("cE")))
        cV = Ctr(st.enter_context(nc.semaphore("cV")))
        cS = Ctr(st.enter_context(nc.semaphore("cS")))
        cO = Ctr(st.enter_context(nc.semaphore("cO")))
        cG = Ctr(st.enter_context(nc.semaphore("cG")))
        cI = Ctr(st.enter_context(nc.semaphore("cI")))
        P.extra["tensor"] = (cV, cS, cG, cI)
        P.extra["vector"] = (cE, cG, cI)
        P.extra["scalar"] = (cE, cG, cI)

        block = st.enter_context(nc.Block())

        for c in range(NCH):
            P.on("sync", lambda E, c=c: E.dma_start(out=A[:, c * T:(c + 1) * T],
                                                    in_=x1_d[c * 128:(c + 1) * 128, :]),
                 inc=cD, dma=True)
            P.on("sync", lambda E, c=c: E.dma_start(out=Bs[:, c * T:(c + 1) * T],
                                                    in_=ev_d[c * 128:(c + 1) * 128, :]),
                 inc=cD, dma=True)
        P.on("gpsimd", lambda E: E.dma_start(
            out=wqkb[:].rearrange("p (c n) -> p c n", n=128),
            in_=wqk_d.rearrange("(c p) n -> p c n", p=128)), inc=cG, dma=True)
        P.on("gpsimd", lambda E: E.dma_start(
            out=wvb[:].rearrange("p (c n) -> p c n", n=1024),
            in_=wv_d.rearrange("(c p) n -> p c n", p=128)), inc=cG, dma=True)
        P.on("sync", lambda E: E.dma_start(out=bqkc[:], in_=bqk_d.rearrange("o n -> n o")),
             inc=cW, dma=True)
        P.on("sync", lambda E: E.dma_start(out=g0[:, 0:2], in_=gam_d[:]), inc=cW, dma=True)
        P.on("sync", lambda E: E.dma_start(out=g0[:, 2:4], in_=bet_d[:]), inc=cW, dma=True)
        P.on("sync", lambda E: E.dma_start(out=relv[:], in_=rel_d[:]), inc=cW, dma=True)
        P.on("sync", lambda E: E.dma_start(out=ohv, in_=oh_d[:]), inc=cW, dma=True)
        P.on("sync", lambda E: E.dma_start(out=c0v, in_=c0_d[:]), inc=cW, dma=True)
        P.on("sync", lambda E: E.dma_start(out=idxv[:], in_=idx_d[:]), inc=cW, dma=True)
        P.on("sync", lambda E: E.dma_start(out=onc[:], in_=onc_d[:]), inc=cW, dma=True)
        P.on("sync", lambda E: E.dma_start(out=onr[:], in_=onr_d[:]), inc=cW, dma=True)
        P.on("sync", lambda E: E.dma_start(out=idb[:], in_=idb_d[:]), inc=cW, dma=True)
        P.on("sync", lambda E: E.dma_start(out=idf[:], in_=idf_d[:]), inc=cW, dma=True)
        P.on("sync", lambda E: E.dma_start(out=prow[:], in_=bv_d[:]), inc=cW, dma=True)

        _sec[0] = 1
        def mm_bv(E):
            E.matmul(psB[:, 0:512], onr[:], prow[:, 0:512], start=True, stop=True)
            return E.matmul(psB[:, 512:1024], onr[:], prow[:, 512:1024],
                            start=True, stop=True)
        P.on("tensor", mm_bv, after=(cW,), inc=cE)
        P.on("vector", lambda E: E.tensor_copy(bvb[:], psB[:, 0:1024]), after=(cE,), inc=cV)

        _sec[0] = 2
        P.on("tensor", lambda E: E.matmul(psS[:, 0:512], relv[:], ohv[:, 0:512], start=True, stop=True), after=(cW,), inc=cE)
        P.on("tensor", lambda E: E.matmul(psS[:, 512:1024], relv[:], ohv[:, 512:1024], start=True, stop=True), after=(cW,), inc=cE)
        P.on("vector", lambda E: E.tensor_add(prow[:, 0:512], psS[:, 0:512], c0v[0:1, 0:512]),
             after=(cE,), inc=cV)
        P.on("vector", lambda E: E.tensor_add(prow[:, 512:1024], psS[:, 512:1024],
                                              c0v[0:1, 512:1024]), after=(cE,), inc=cV)
        P.on("sync", lambda E: E.dma_start(out=pvec_d.rearrange("t o -> o t"), in_=prow[:]),
             after=(cV,), inc=cO, dma=True)
        for ib in range(4):
            P.on("gpsimd", lambda E, ib=ib: E.indirect_dma_start(
                btile[:, ib * 512:(ib + 1) * 512], None,
                pvec_d[:],
                IndirectOffsetOnAxis(idxv[:, ib:ib + 1], 0)),
                 after=(cO, cW, cG), inc=cI, dma=True)

        _sec[0] = 3
        for c in range(NCH):
            P.on("vector", lambda E, c=c: E.tensor_add(Bs[:, c * T:(c + 1) * T],
                                                       Bs[:, c * T:(c + 1) * T],
                                                       A[:, c * T:(c + 1) * T]),
                 after=(cD,), inc=cV)
        emit_ln(P, cE, cV, cS, T, lambda c: Bs[:, c * T:(c + 1) * T],
                lambda c: A[:, c * T:(c + 1) * T], None,
                onc, onr, psS, psB, sq, stats, phases=lnph)

        _sec[0] = 5
        for s in range(T // 512):
            for c in range(NCH):
                P.on("vector", lambda E, c=c, s=s: E.tensor_copy(
                    x2bs[:, c * 512:(c + 1) * 512],
                    A[:, c * T + s * 512:c * T + (s + 1) * 512]), after=(cV, cE), inc=cV)

            def mm_qk(E, s=s):
                for c in range(NCH):
                    mm = E.matmul(psM[:, 0:512], wqkb[:, c * 128:(c + 1) * 128],
                                  x2bs[:, c * 512:(c + 1) * 512],
                                  start=(c == 0), stop=(c == NCH - 1))
                return mm
            P.on("tensor", mm_qk, after=(cV, cW), inc=cE)
            P.on("scalar", lambda E: E.activation(sq[:, 0:512], psM[:, 0:512], AF.Silu,
                                                  bias=bqkc[:, 0:1]),
                 after=(cE, cV), inc=cS)
            P.on("vector", lambda E, s=s: E.tensor_scalar(qt[:, s * 512:(s + 1) * 512],
                                                          sq[:, 0:512], g0[:, 0:1], g0[:, 2:3],
                                                          ALU.mult, ALU.add),
                 after=(cS,), inc=cV)
            P.on("vector", lambda E, s=s: E.tensor_scalar(kt[:, s * 512:(s + 1) * 512],
                                                          sq[:, 0:512], g0[:, 1:2], g0[:, 3:4],
                                                          ALU.mult, ALU.add),
                 after=(cS,), inc=cV)

        _sec[0] = 6
        for ch in range(4):
            for jt in range(4):
                j0 = ch * 512 + jt * 128
                for c in range(NCH):
                    P.on("vector", lambda E, c=c, j0=j0: E.tensor_copy(
                        vxb[:, c * 128:(c + 1) * 128] if False else x2bs[:, c * 128:(c + 1) * 128],
                        A[:, c * T + j0:c * T + j0 + 128]), after=(cV, cE), inc=cV)

                def mm_v(E, j0=j0):
                    for half in range(2):
                        for c in range(NCH):
                            mm = E.matmul(psV[:, half * 512:(half + 1) * 512],
                                          x2bs[:, c * 128:(c + 1) * 128],
                                          wvb[:, c * 1024 + half * 512:c * 1024 + (half + 1) * 512],
                                          start=(c == 0), stop=(c == NCH - 1))
                    return mm
                P.on("tensor", mm_v, after=(cV, cW, cS), inc=cE)
                P.on("vector", lambda E: E.tensor_add(attn[:, 0:1024], psV[:, 0:1024],
                                                      bvb[:, 0:1024]), after=(cE,), inc=cV)
                P.on("scalar", lambda E, jt=jt: E.activation(vt[:, jt * 1024:(jt + 1) * 1024],
                                                             attn[:, 0:1024], AF.Silu),
                     after=(cV,), inc=cS)
            for it in range(4):
                i0 = ch * 512 + it * 128

                def mm_sim(E, ch=ch, i0=i0):
                    return E.matmul(psM[:, 0:512], qt[:, i0:i0 + 128],
                                    kt[:, ch * 512:(ch + 1) * 512], start=True, stop=True)
                P.on("tensor", mm_sim, after=(cV, cD, cS), inc=cE)
                P.on("vector", lambda E, it=it: E.scalar_tensor_tensor(
                    psM[:, 0:512], psM[:, 0:512], DM ** -0.5,
                    btile[:, it * 512:(it + 1) * 512], ALU.mult, ALU.add),
                     after=(cE, cD), inc=cV)
                P.on("vector", lambda E: E.reduce_max(nmax[:, 0:1], psM[:, 0:512], axis=AX.X,
                                                      negate=True), after=(cV,), inc=cV)
                P.on("scalar", lambda E: E.activation(pbf[:], psM[:, 0:512], AF.Exp,
                                                      bias=nmax[:, 0:1],
                                                      accum_out=rsum[:, 0:1]),
                     after=(cV,), inc=cS)

                def mm_pt(E):
                    for jc in range(4):
                        mm = E.transpose(psP[:, jc * 128:(jc + 1) * 128],
                                         pbf[:, jc * 128:(jc + 1) * 128], idb[:])
                    return mm
                P.on("tensor", mm_pt, after=(cS,), inc=cE)
                P.on("vector", lambda E: E.tensor_copy(ptb[:], psP[:, 0:512]),
                     after=(cE,), inc=cV)
                P.on("vector", lambda E: E.reciprocal(rsum[:, 1:2], rsum[:, 0:1]),
                     after=(cS,), inc=cV)

                def mm_av(E, ch=ch):
                    for half in range(2):
                        for jc in range(4):
                            mm = E.matmul(psA[:, half * 512:(half + 1) * 512],
                                          ptb[:, jc * 128:(jc + 1) * 128],
                                          vt[:, jc * 1024 + half * 512:jc * 1024 + (half + 1) * 512],
                                          start=(jc == 0), stop=(jc == 3))
                    return mm
                P.on("tensor", mm_av, after=(cV,), inc=cE)
                P.on("vector", lambda E: E.tensor_scalar_mul(attn[:], psA[:, 0:1024],
                                                             rsum[:, 1:2]),
                     after=(cE,), inc=cV)
                for g in range(2):
                    def mm_at4(E, g=g):
                        for c4 in range(4):
                            mm = E.transpose(psB[:, c4 * 128:(c4 + 1) * 128],
                                             attn[:, (g * 4 + c4) * 128:(g * 4 + c4 + 1) * 128],
                                             idf[:])
                        return mm
                    P.on("tensor", mm_at4, after=(cV,), inc=cE)
                    for c4 in range(4):
                        c = g * 4 + c4
                        P.on("vector", lambda E, c=c, c4=c4, i0=i0: E.tensor_add(
                            Bs[:, c * T + i0:c * T + i0 + 128],
                            A[:, c * T + i0:c * T + i0 + 128],
                            psB[:, c4 * 128:(c4 + 1) * 128]), after=(cE,), inc=cV)

        _sec[0] = 7
        emit_ln(P, cE, cV, cS, T, lambda c: Bs[:, c * T:(c + 1) * T],
                lambda c: A[:, c * T:(c + 1) * T], None,
                onc, onr, psS, psB, sq, stats)
        _sec[0] = 0
        SRC = Bs if store_bs else A
        for c in range(NCH):
            P.on("sync", lambda E, c=c: E.dma_start(out=x3_d[c * 128:(c + 1) * 128, :],
                                                    in_=SRC[:, c * T:(c + 1) * T]),
                 after=(cV,), inc=cO, dma=True)
        P.on("sync", lambda E: E.nop(), after=(cO,))
        P.emit(block)
    return nc


def att_consts():
    NB, MD, me = 32, 128, 16
    d = np.arange(512)
    nf = np.maximum(d, 1).astype(np.float64)
    vl = me + np.log(nf / me) / math.log(MD / me) * (NB - me)
    vl = np.minimum(vl, NB - 1)
    bucket = np.where(d < me, d, vl.astype(np.int32)).astype(np.int32)
    oh = np.zeros((32, 1024), np.float32)
    for t in range(0, 512):
        oh[bucket[511 - t], t] = math.sqrt(128.0)
    c0 = np.zeros((1, 1024), np.float32)
    c0[0, 512:] = NEG
    idx = np.zeros((128, 4), np.uint32)
    for ib in range(4):
        for p in range(128):
            idx[p, ib] = 511 - (ib * 128 + p)
    return {
        "oh": oh, "c0": c0, "tidx": idx,
        "onesc": np.ones((128, 1), np.float32),
        "onesr": np.ones((1, 128), np.float32),
        "identf": np.eye(128, dtype=np.float32),
    }


def _bf16(x):
    import ml_dtypes
    return np.asarray(x, np.float32).astype(ml_dtypes.bfloat16)


def run_att(x1, ev, W, upto=99, lnph=3, store_bs=False):
    import ml_dtypes
    nc = build_att(upto=upto, lnph=lnph, store_bs=store_bs)
    consts = att_consts()
    consts["identb"] = np.eye(128, dtype=np.float32).astype(ml_dtypes.bfloat16)
    in_maps = []
    for c in range(NCORE):
        b, t0 = c // 4, (c % 4) * TPC
        m = {
            "x1f": np.ascontiguousarray(x1[b, t0:t0 + TPC, :].T),
            "evf": np.ascontiguousarray(ev[b, t0:t0 + TPC, :].T),
            "Wqk": W["L1_attn_Wqk"], "bqk": W["L1_attn_bqk"][None],
            "Wv": W["L1_attn_Wv"], "bv": W["L1_attn_bv"][None],
            "gam": np.ascontiguousarray(W["L1_attn_gamma"].T),
            "bet": np.ascontiguousarray(W["L1_attn_beta"].T),
            "rel": W["L1_attn_rel_emb"],
        }
        m.update(consts)
        in_maps.append(m)
    res = run_bass_kernel_spmd(nc, in_maps, core_ids=list(range(NCORE)))
    x3 = np.zeros((BATCH, SEQ, DM), np.float32)
    for c in range(NCORE):
        b, t0 = c // 4, (c % 4) * TPC
        x3[b, t0:t0 + TPC, :] = res.results[c]["x3f"].T
    return x3


def build_moe():
    nc = bass.Bass("TRN2", target_bir_lowering=False)
    T = TPC
    NSL = T // 512
    x3_d = din(nc, "x3f", [DM, T])
    gw_d = din(nc, "gwT", [DM, 16])
    we1_d = din(nc, "We1T", [16, DM, 512])
    be1_d = din(nc, "be1c", [16, 128, 4])
    we3_d = din(nc, "We3T", [16, DM, 512])
    be3_d = din(nc, "be3c", [16, 128, 4])
    we2_d = din(nc, "We2Tp", [16, 512, DM])
    be2_d = din(nc, "be2c", [16, 128, 8])
    w2r_d = din(nc, "w2r0", [16, 1, DM])
    ws1_d = din(nc, "Ws1T", [DM, DM])
    bs1_d = din(nc, "bs1c", [128, 8])
    ws3_d = din(nc, "Ws3T", [DM, DM])
    bs3_d = din(nc, "bs3c", [128, 8])
    ws2_d = din(nc, "Ws2Tp", [DM, DM])
    bs2_d = din(nc, "bs2c", [128, 8])
    wsr_d = din(nc, "ws2r0", [1, DM])
    onm_d = din(nc, "onesm", [128, 128])
    nr2_d = din(nc, "negr2", [1, 128])
    sel_d = din(nc, "sel16", [16, 16 * 128])
    idf_d = din(nc, "identf", [128, 128])
    out_d = dout(nc, "outf", [DM, T])

    P = Prog()
    st = ExitStack()
    sb = lambda name, shape, dt=F32: st.enter_context(nc.sbuf_tensor(name, list(shape), dt))
    ps = lambda name, shape, dt=F32: st.enter_context(nc.psum_tensor(name, list(shape), dt))
    with st:
        x3b = sb("x3b", [128, NCH * T], BF16)
        yt = sb("yt", [128, NCH * T])
        z0 = sb("z0", [128, NCH * 512])
        at = sb("at", [128, 4 * 512])
        sf = sb("sf", [128, 4 * 512])
        sbf = sb("sbf", [128, NCH * 512], BF16)
        w1t = sb("w1t", [128, NCH * 512], BF16)
        w3t = sb("w3t", [128, NCH * 512], BF16)
        w2t = sb("w2t", [128, NCH * 512], BF16)
        w2tb = sb("w2tb", [128, NCH * 512], BF16)
        w2r = sb("w2r", [1, DM], BF16)
        be1 = sb("be1", [128, 8])
        be3 = sb("be3", [128, 8])
        be2 = sb("be2", [128, 8])
        wbc = sb("wbc", [128, 512])
        rst = sb("rst", [128, 512])
        tsr = sb("tsr", [128, 512])
        tsb = sb("tsb", [1, 512], BF16)
        sq = sb("sq", [128, 512], F32R)
        sq2 = sb("sq2", [128, 512])
        trow = sb("trow", [1, 512], F32R)
        prb = sb("prb", [128, 16])
        wte = sb("wte", [16, T])
        onm = sb("onm", [128, 128], F32R)
        nr2 = sb("nr2", [1, 128], F32R)
        idf = sb("idf_s", [128, 128])
        gwt = sb("gwt", [128, NCH * 16])
        s16 = sf[0:16, 0:T]
        sel = w2tb[0:16, :].bitcast(F32)
        m12 = sb("m12", [128, 4])

        psA_ = ps("psA", [128, 512])
        psB_ = ps("psB", [128, 512])
        psU = ps("psU", [128, 512])
        psR = ps("psR", [128, 512])
        psS16 = ps("psS16", [16, 512])
        psT16 = ps("psT16", [128, 16])
        psW = ps("psW", [128, 512])

        cD = Ctr(st.enter_context(nc.semaphore("cD")))
        cW = Ctr(st.enter_context(nc.semaphore("cW")))
        cE = Ctr(st.enter_context(nc.semaphore("cE")))
        cV = Ctr(st.enter_context(nc.semaphore("cV")))
        cS = Ctr(st.enter_context(nc.semaphore("cS")))
        cO = Ctr(st.enter_context(nc.semaphore("cO")))
        cG = Ctr(st.enter_context(nc.semaphore("cG")))
        P.extra["tensor"] = (cV, cS, cG)
        P.extra["vector"] = (cE, cG)
        P.extra["scalar"] = (cE, cG)
        block = st.enter_context(nc.Block())

        P.on("gpsimd", lambda E: E.dma_start(
            out=x3b[:].rearrange("p (c t) -> p c t", t=T),
            in_=x3_d.rearrange("(c p) t -> p c t", p=128)), inc=cG, dma=True)
        P.on("gpsimd", lambda E: E.dma_start(out=onm[:], in_=onm_d[:]), inc=cG, dma=True)
        P.on("gpsimd", lambda E: E.dma_start(out=nr2[:], in_=nr2_d[:]), inc=cG, dma=True)
        for nm, dst in ((sel_d, sel), (idf_d, idf[:]),):
            P.on("sync", lambda E, nm=nm, dst=dst: E.dma_start(out=dst if not hasattr(dst, "tensor") else dst, in_=nm[:]),
                 inc=cW, dma=True)
        P.on("sync", lambda E: E.dma_start(
            out=gwt[:].rearrange("p (c g) -> p c g", g=16),
            in_=gw_d.rearrange("(c p) g -> p c g", p=128)), inc=cW, dma=True)

        P.on("vector", lambda E: E.memset(yt[:], 0.0), inc=cV)
        P.on("vector", lambda E: E.memset(yt[0:1, 7 * T:8 * T], 1.0), inc=cV)

        for s in range(NSL):
            for c in range(NCH):
                P.on("sync", lambda E, c=c, s=s: E.dma_start(
                    out=z0[:, c * 512:(c + 1) * 512],
                    in_=x3_d[c * 128:(c + 1) * 128, s * 512:(s + 1) * 512]),
                     after=(cV, cE), inc=cD, dma=True)

            def mm_sc(E, s=s):
                for c in range(NCH):
                    mm = E.matmul(psS16[:, 0:512], gwt[:, c * 16:(c + 1) * 16],
                                  z0[:, c * 512:(c + 1) * 512],
                                  start=(c == 0), stop=(c == NCH - 1))
                return mm
            P.on("tensor", mm_sc, after=(cD, cW), inc=cE)
            P.on("vector", lambda E, s=s: E.tensor_copy(s16[:, s * 512:(s + 1) * 512],
                                                        psS16[:, 0:512]),
                 after=(cE,), inc=cV)
        for tt in range(T // 128):
            P.on("tensor", lambda E, tt=tt: E.transpose(
                psT16[:, 0:16], s16[:, tt * 128:(tt + 1) * 128], idf[0:16, 0:16]),
                 after=(cV,), inc=cE)
            P.on("vector", lambda E: E.tensor_copy(prb[:], psT16[:, 0:16]),
                 after=(cE,), inc=cV)
            P.on("vector", lambda E: E.reduce_max(m12[:, 0:1], prb[:], axis=AX.X,
                                                  negate=True), after=(cV,), inc=cV)
            P.on("scalar", lambda E: E.activation(prb[:], prb[:], AF.Exp,
                                                  bias=m12[:, 0:1], accum_out=m12[:, 1:2]),
                 after=(cV,), inc=cS)
            P.on("vector", lambda E: E.reciprocal(m12[:, 2:3], m12[:, 1:2]),
                 after=(cS,), inc=cV)
            P.on("vector", lambda E: E.tensor_scalar_mul(prb[:], prb[:], m12[:, 2:3]),
                 after=(cV, cS), inc=cV)
            P.on("vector", lambda E: E.reduce_max(m12[:, 0:1], prb[:], axis=AX.X),
                 after=(cV,), inc=cV)
            P.on("vector", lambda E: E.tensor_scalar(sq2[:, 0:16], prb[:], m12[:, 0:1],
                                                     1.0, ALU.is_lt, ALU.mult),
                 after=(cV,), inc=cV)
            P.on("vector", lambda E: E.tensor_tensor(sq2[:, 16:32], prb[:], sq2[:, 0:16],
                                                     ALU.mult), after=(cV,), inc=cV)
            P.on("vector", lambda E: E.reduce_max(m12[:, 1:2], sq2[:, 16:32], axis=AX.X),
                 after=(cV,), inc=cV)
            P.on("vector", lambda E: E.tensor_scalar(sq2[:, 0:16], prb[:], m12[:, 1:2],
                                                     1.0, ALU.is_ge, ALU.mult),
                 after=(cV,), inc=cV)
            P.on("vector", lambda E: E.tensor_tensor(prb[:], prb[:], sq2[:, 0:16],
                                                     ALU.mult), after=(cV,), inc=cV)
            P.on("tensor", lambda E: E.transpose(psS16[:, 0:128], prb[:], idf[:]),
                 after=(cV,), inc=cE)
            P.on("vector", lambda E, tt=tt: E.tensor_copy(wte[:, tt * 128:(tt + 1) * 128],
                                                          psS16[:, 0:128]),
                 after=(cE,), inc=cV)

        for e in range(17):
            shared = (e == 16)
            nOT = 8 if shared else 4
            we1 = ws1_d if shared else we1_d[e]
            we3 = ws3_d if shared else we3_d[e]
            we2 = ws2_d if shared else we2_d[e]
            w2rs = wsr_d if shared else w2r_d[e]
            if not shared:
                P.on("gpsimd", lambda E, we1=we1: E.dma_start(
                    out=w1t[:].rearrange("p (c t) -> p c t", t=512),
                    in_=we1.rearrange("(c p) t -> p c t", p=128)),
                     after=(cE, cS), inc=cG, dma=True)
                P.on("gpsimd", lambda E, we3=we3: E.dma_start(
                    out=w3t[:].rearrange("p (c t) -> p c t", t=512),
                    in_=we3.rearrange("(c p) t -> p c t", p=128)),
                     after=(cE, cS), inc=cG, dma=True)
            if shared:
                P.on("gpsimd", lambda E, we2=we2: E.dma_start(
                    out=w2t[:].rearrange("p (c t) -> p c t", t=1024),
                    in_=we2[0:512].rearrange("(c p) t -> p c t", p=128)),
                     after=(cE, cS), inc=cG, dma=True)
                P.on("gpsimd", lambda E, we2=we2: E.dma_start(
                    out=w2tb[:].rearrange("p (c t) -> p c t", t=1024),
                    in_=we2[512:1024].rearrange("(c p) t -> p c t", p=128)),
                     after=(cE, cS), inc=cG, dma=True)
            else:
                P.on("gpsimd", lambda E, we2=we2: E.dma_start(
                    out=w2t[:].rearrange("p (c t) -> p c t", t=1024),
                    in_=we2.rearrange("(c p) t -> p c t", p=128)),
                     after=(cE, cS), inc=cG, dma=True)
            P.on("gpsimd", lambda E, w2rs=w2rs: E.dma_start(out=w2r[:], in_=w2rs),
                 after=(cE, cS), inc=cG, dma=True)
            b1src = bs1_d if shared else be1_d[e]
            b3src = bs3_d if shared else be3_d[e]
            b2src = bs2_d if shared else be2_d[e]
            nb = 8 if shared else 4
            P.on("sync", lambda E, b1src=b1src, nb=nb: E.dma_start(out=be1[:, 0:nb],
                                                                  in_=b1src[:]),
                 after=(cS,), inc=cW, dma=True)
            P.on("sync", lambda E, b3src=b3src, nb=nb: E.dma_start(out=be3[:, 0:nb],
                                                                  in_=b3src[:]),
                 after=(cV,), inc=cW, dma=True)
            P.on("sync", lambda E, b2src=b2src: E.dma_start(out=be2[:], in_=b2src[:]),
                 after=(cV,), inc=cW, dma=True)
            for s in range(NSL):
                sl = slice(s * 512, (s + 1) * 512)
                stile = z0 if shared else sf
                if not shared:
                    P.on("tensor", lambda E, s=s, e=e: E.matmul(
                        psW[:], sel[:, e * 128:(e + 1) * 128], wte[:, sl],
                        start=True, stop=True), after=(cV, cS), inc=cE)
                    P.on("vector", lambda E: E.tensor_copy(wbc[:], psW[:]),
                         after=(cE,), inc=cV)
                for o in range(nOT):
                    ot = o % 4
                    if shared and ot == 0:
                        h = o // 4
                        P.on("gpsimd", lambda E, h=h, we1=we1: E.dma_start(
                            out=w1t[:].rearrange("p (c t) -> p c t", t=512),
                            in_=we1[:, h * 512:(h + 1) * 512].rearrange(
                                "(c p) t -> p c t", p=128)),
                             after=(cE, cS), inc=cG, dma=True)
                        P.on("gpsimd", lambda E, h=h, we3=we3: E.dma_start(
                            out=w3t[:].rearrange("p (c t) -> p c t", t=512),
                            in_=we3[:, h * 512:(h + 1) * 512].rearrange(
                                "(c p) t -> p c t", p=128)),
                             after=(cE, cS), inc=cG, dma=True)

                    def mm_u1(E, o=ot, sl=sl):
                        for c in range(NCH):
                            mm = E.matmul(psA_[:], w1t[:, c * 512 + o * 128:c * 512 + o * 128 + 128],
                                          x3b[:, c * T:(c + 1) * T][:, sl],
                                          start=(c == 0), stop=(c == NCH - 1))
                        return mm
                    P.on("tensor", mm_u1, after=(cW, cV, cS), inc=cE)
                    P.on("scalar", lambda E, o=o, ot=ot: E.activation(
                        at[:, ot * 512:(ot + 1) * 512], psA_[:], AF.Silu,
                        bias=be1[:, o:o + 1]), after=(cE,), inc=cS)

                    def mm_u3(E, o=ot, sl=sl):
                        for c in range(NCH):
                            mm = E.matmul(psB_[:], w3t[:, c * 512 + o * 128:c * 512 + o * 128 + 128],
                                          x3b[:, c * T:(c + 1) * T][:, sl],
                                          start=(c == 0), stop=(c == NCH - 1))
                        return mm
                    P.on("tensor", mm_u3, after=(cW, cV, cS), inc=cE)
                    P.on("vector", lambda E, o=o, ot=ot, stile=stile: E.scalar_tensor_tensor(
                        stile[:, o * 512:(o + 1) * 512], psB_[:], be3[:, o:o + 1],
                        at[:, ot * 512:(ot + 1) * 512], ALU.add, ALU.mult),
                         after=(cE, cS), inc=cV)
                for o in range(nOT):
                    P.on("scalar", lambda E, o=o, stile=stile: E.activation(
                        sq[:], stile[:, o * 512:(o + 1) * 512], AF.Square),
                         after=(cV, cE), inc=cS)
                    P.on("tensor", lambda E, o=o, nOT=nOT: E.matmul(
                        psR[:], onm[:], sq[:], start=(o == 0), stop=(o == nOT - 1)),
                         after=(cS,), inc=cE)
                P.on("vector", lambda E: E.tensor_scalar_add(tsr[:], psR[:], 1.0),
                     after=(cE,), inc=cV)
                P.on("scalar", lambda E: E.activation(tsr[:], tsr[:], AF.Sqrt),
                     after=(cV,), inc=cS)
                P.on("vector", lambda E: E.tensor_copy(tsb[:], tsr[0:1, :]),
                     after=(cS,), inc=cV)
                P.on("vector", lambda E, stile=stile, nOT=nOT: E.tensor_copy(
                    sbf[:, 0:nOT * 512], stile[:, 0:nOT * 512]), after=(cV, cS), inc=cV)
                for o2 in range(8):
                    def mm_u2(E, o2=o2, shared=shared):
                        if shared:
                            for hc in range(4):
                                E.matmul(psU[:], w2t[:, hc * 1024 + o2 * 128:hc * 1024 + o2 * 128 + 128],
                                         sbf[:, hc * 512:(hc + 1) * 512],
                                         start=(hc == 0), stop=False)
                            for hc in range(4):
                                E.matmul(psU[:], w2tb[:, hc * 1024 + o2 * 128:hc * 1024 + o2 * 128 + 128],
                                         sbf[:, (4 + hc) * 512:(5 + hc) * 512],
                                         start=False, stop=False)
                        else:
                            for hc in range(4):
                                E.matmul(psU[:], w2t[:, hc * 1024 + o2 * 128:hc * 1024 + o2 * 128 + 128],
                                         sbf[:, hc * 512:(hc + 1) * 512],
                                         start=(hc == 0), stop=False)
                        return E.matmul(psU[:], w2r[:, o2 * 128:(o2 + 1) * 128],
                                        tsb[:], start=False, stop=True)
                    P.on("tensor", mm_u2, after=(cV, cS, cW), inc=cE)
                    P.on("vector", lambda E, o2=o2: E.tensor_scalar(
                        z0[:, o2 * 512:(o2 + 1) * 512], psU[:], be2[:, o2:o2 + 1], None,
                        ALU.add), after=(cE,), inc=cV)
                for o2 in range(8):
                    P.on("scalar", lambda E, o2=o2: E.activation(
                        sq[:], z0[:, o2 * 512:(o2 + 1) * 512], AF.Square),
                         after=(cV, cE), inc=cS)
                    P.on("tensor", lambda E, o2=o2: E.matmul(
                        psR[:], onm[:], sq[:], start=(o2 == 0), stop=(o2 == 7)),
                         after=(cS,), inc=cE)
                P.on("vector", lambda E: E.tensor_scalar_add(sq2[:], psR[:], 1.0),
                     after=(cE,), inc=cV)
                P.on("scalar", lambda E: E.activation(sq2[:], sq2[:], AF.Sqrt),
                     after=(cV,), inc=cS)
                P.on("vector", lambda E: E.tensor_copy(z0[0:1, 7 * 512:8 * 512],
                                                       sq2[0:1, :]), after=(cS,), inc=cV)
                for o2 in range(8):
                    if shared:
                        P.on("vector", lambda E, o2=o2, sl=sl: E.tensor_add(
                            z0[:, o2 * 512:(o2 + 1) * 512], z0[:, o2 * 512:(o2 + 1) * 512],
                            yt[:, o2 * T:(o2 + 1) * T][:, sl]), after=(cV,), inc=cV)
                    else:
                        P.on("vector", lambda E, o2=o2: E.tensor_tensor(
                            z0[:, o2 * 512:(o2 + 1) * 512], z0[:, o2 * 512:(o2 + 1) * 512],
                            wbc[:], ALU.mult), after=(cV,), inc=cV)
                        P.on("vector", lambda E, o2=o2, sl=sl: E.tensor_add(
                            z0[:, o2 * 512:(o2 + 1) * 512], z0[:, o2 * 512:(o2 + 1) * 512],
                            yt[:, o2 * T:(o2 + 1) * T][:, sl]), after=(cV,), inc=cV)
                for o2 in range(8):
                    P.on("scalar", lambda E, o2=o2: E.activation(
                        sq[:], z0[:, o2 * 512:(o2 + 1) * 512], AF.Square),
                         after=(cV, cE), inc=cS)
                    P.on("tensor", lambda E, o2=o2: E.matmul(
                        psR[:], onm[:], sq[:], start=(o2 == 0), stop=False),
                         after=(cS,), inc=cE)
                P.on("scalar", lambda E: E.activation(trow[:], z0[0:1, 7 * 512:8 * 512],
                                                      AF.Square), after=(cV, cE), inc=cS)
                P.on("tensor", lambda E: E.matmul(psR[:], nr2[:], trow[:],
                                                  start=False, stop=True),
                     after=(cS,), inc=cE)
                P.on("vector", lambda E: E.tensor_scalar_mul(rst[:], psR[:], -1.0),
                     after=(cE,), inc=cV)
                P.on("vector", lambda E: E.tensor_scalar_max(rst[:], rst[:], 1e-6),
                     after=(cV,), inc=cV)
                P.on("scalar", lambda E: E.activation(rst[:], rst[:], AF.Sqrt),
                     after=(cV,), inc=cS)
                P.on("vector", lambda E: E.reciprocal(rst[:], rst[:]), after=(cS,), inc=cV)
                for o2 in range(8):
                    P.on("vector", lambda E, o2=o2, sl=sl: E.tensor_tensor(
                        yt[:, o2 * T:(o2 + 1) * T][:, sl], z0[:, o2 * 512:(o2 + 1) * 512],
                        rst[:], ALU.mult), after=(cV,), inc=cV)

        for s in range(NSL):
            sl = slice(s * 512, (s + 1) * 512)
            for c in range(7):
                P.on("sync", lambda E, c=c, s=s: E.dma_start(
                    out=z0[:, c * 512:(c + 1) * 512],
                    in_=x3_d[c * 128 + 1:c * 128 + 129, s * 512:(s + 1) * 512]),
                     after=(cV, cE), inc=cD, dma=True)
            P.on("sync", lambda E, s=s: E.dma_start(
                out=z0[0:1, 7 * 512:8 * 512], in_=x3_d[0:1, s * 512:(s + 1) * 512]),
                 after=(cV, cE), inc=cD, dma=True)
            P.on("sync", lambda E, s=s: E.dma_start(
                out=z0[1:128, 7 * 512:8 * 512], in_=x3_d[897:1024, s * 512:(s + 1) * 512]),
                 after=(cV, cE), inc=cD, dma=True)
            for c in range(NCH):
                P.on("vector", lambda E, c=c, sl=sl: E.tensor_add(
                    z0[:, c * 512:(c + 1) * 512], z0[:, c * 512:(c + 1) * 512],
                    yt[:, c * T:(c + 1) * T][:, sl]), after=(cD,), inc=cV)
            for c in range(NCH):
                P.on("tensor", lambda E, c=c: E.matmul(
                    psR[:], onm[:], z0[:, c * 512:(c + 1) * 512].bitcast(F32R) if False
                    else sqz(E, c), start=(c == 0), stop=(c == NCH - 1)),
                     after=(cS,), inc=cE) if False else None
            for c in range(NCH):
                P.on("vector", lambda E, c=c: E.tensor_copy(sq[:], z0[:, c * 512:(c + 1) * 512]),
                     after=(cV, cE), inc=cV)
                P.on("tensor", lambda E, c=c: E.matmul(
                    psA_[:], onm[:], sq[:], start=(c == 0), stop=(c == NCH - 1)),
                     after=(cV,), inc=cE)
                P.on("scalar", lambda E, c=c: E.activation(sq[:], z0[:, c * 512:(c + 1) * 512],
                                                           AF.Square), after=(cE, cV), inc=cS)
                P.on("tensor", lambda E, c=c: E.matmul(
                    psB_[:], onm[:], sq[:], start=(c == 0), stop=(c == NCH - 1)),
                     after=(cS,), inc=cE)
            P.on("vector", lambda E: E.tensor_scalar_mul(tsr[:], psA_[:], 1.0 / DM),
                 after=(cE,), inc=cV)
            P.on("vector", lambda E: E.tensor_scalar_mul(sq2[:], psB_[:], 1.0 / DM),
                 after=(cE,), inc=cV)
            P.on("vector", lambda E: E.tensor_tensor(rst[:], tsr[:], tsr[:], ALU.mult),
                 after=(cV,), inc=cV)
            P.on("vector", lambda E: E.tensor_sub(sq2[:], sq2[:], rst[:]),
                 after=(cV,), inc=cV)
            P.on("vector", lambda E: E.tensor_scalar_add(sq2[:], sq2[:], 1e-5),
                 after=(cV,), inc=cV)
            P.on("scalar", lambda E: E.activation(sq2[:], sq2[:], AF.Sqrt),
                 after=(cV,), inc=cS)
            P.on("vector", lambda E: E.reciprocal(rst[:], sq2[:]), after=(cS,), inc=cV)
            P.on("vector", lambda E: E.tensor_tensor(tsr[:], tsr[:], rst[:], ALU.mult),
                 after=(cV,), inc=cV)
            for c in range(NCH):
                P.on("vector", lambda E, c=c: E.tensor_tensor(
                    z0[:, c * 512:(c + 1) * 512], z0[:, c * 512:(c + 1) * 512], rst[:],
                    ALU.mult), after=(cV,), inc=cV)
                P.on("vector", lambda E, c=c: E.tensor_sub(
                    z0[:, c * 512:(c + 1) * 512], z0[:, c * 512:(c + 1) * 512], tsr[:]),
                     after=(cV,), inc=cV)
            for c in range(7):
                P.on("sync", lambda E, c=c, s=s: E.dma_start(
                    out=out_d[c * 128 + 1:c * 128 + 129, s * 512:(s + 1) * 512],
                    in_=z0[:, c * 512:(c + 1) * 512]), after=(cV,), inc=cO, dma=True)
            P.on("sync", lambda E, s=s: E.dma_start(
                out=out_d[0:1, s * 512:(s + 1) * 512], in_=z0[0:1, 7 * 512:8 * 512]),
                 after=(cV,), inc=cO, dma=True)
            P.on("sync", lambda E, s=s: E.dma_start(
                out=out_d[897:1024, s * 512:(s + 1) * 512], in_=z0[1:128, 7 * 512:8 * 512]),
                 after=(cV,), inc=cO, dma=True)
        P.on("sync", lambda E: E.nop(), after=(cO,))
        P.emit(block)
    return nc


def moe_perm():
    p = np.zeros(1024, np.int64)
    p[0:896] = np.arange(1, 897)
    p[896] = 0
    p[897:1024] = np.arange(897, 1024)
    return p


def moe_host_inputs(W):
    perm = moe_perm()
    def padT(w, out_dim):
        o, i = w.shape
        r = np.zeros((i, out_dim), np.float32)
        r[:, :o] = w.T
        return r
    def bcol(b, n):
        r = np.zeros((128 * n,), np.float32)
        r[:b.shape[0]] = b
        return np.ascontiguousarray(r.reshape(n, 128).T)
    E = 16
    We1T = np.stack([padT(W["L1_ffn_We1"][e], 512) for e in range(E)])
    We3T = np.stack([padT(W["L1_ffn_We3"][e], 512) for e in range(E)])
    be1 = np.stack([bcol(W["L1_ffn_be1"][e], 4) for e in range(E)])
    be3 = np.stack([bcol(W["L1_ffn_be3"][e], 4) for e in range(E)])
    We2Tp = np.zeros((E, 512, 1024), np.float32)
    w2r0 = np.zeros((E, 1, 1024), np.float32)
    be2c = np.zeros((E, 128, 8), np.float32)
    for e in range(E):
        w = W["L1_ffn_We2"][e]
        wT = w.T
        dev = np.zeros((512, 1023), np.float32)
        dev[0:511, :] = wT[1:512, :]
        colsrc = np.zeros((512, 1024), np.float32)
        colsrc[:, 1:1024] = dev
        out_p = colsrc[:, perm]
        out_p[:, 896] = 0.0
        We2Tp[e] = out_p
        r0 = np.zeros(1024, np.float32)
        r0[1:1024] = wT[0, :1023]
        w2r0[e, 0] = r0[perm]
        w2r0[e, 0, 896] = 0.0
        b = np.zeros(1024, np.float32)
        b[1:1024] = W["L1_ffn_be2"][e]
        bp = b[perm]
        bp[896] = 0.0
        be2c[e] = bp.reshape(8, 128).T
    Ws1T = padT(W["L1_ffn_Ws1"], 1024)
    Ws3T = padT(W["L1_ffn_Ws3"], 1024)
    bs1c = bcol(W["L1_ffn_bs1"], 8)
    bs3c = bcol(W["L1_ffn_bs3"], 8)
    w = W["L1_ffn_Ws2"]
    wT = w.T
    dev = np.zeros((1024, 1023), np.float32)
    dev[0:1023, :] = wT[1:1024, :]
    colsrc = np.zeros((1024, 1024), np.float32)
    colsrc[:, 1:1024] = dev
    Ws2Tp = colsrc[:, perm]
    Ws2Tp[:, 896] = 0.0
    r0 = np.zeros(1024, np.float32)
    r0[1:1024] = wT[0, :1023]
    ws2r0 = r0[perm][None]
    ws2r0[0, 896] = 0.0
    b = np.zeros(1024, np.float32)
    b[1:1024] = W["L1_ffn_bs2"]
    bp = b[perm]; bp[896] = 0.0
    bs2c = np.ascontiguousarray(bp.reshape(8, 128).T)
    sel16 = np.zeros((16, 16 * 128), np.float32)
    for e in range(16):
        sel16[e, e * 128:(e + 1) * 128] = 1.0
    return {
        "gwT": np.ascontiguousarray(W["L1_ffn_gate_w"].T),
        "We1T": We1T, "be1c": be1, "We3T": We3T, "be3c": be3,
        "We2Tp": We2Tp, "be2c": be2c, "w2r0": w2r0,
        "Ws1T": Ws1T, "bs1c": bs1c, "Ws3T": Ws3T, "bs3c": bs3c,
        "Ws2Tp": Ws2Tp, "bs2c": bs2c, "ws2r0": ws2r0,
        "onesm": np.ones((128, 128), np.float32),
        "negr2": np.full((1, 128), -2.0, np.float32),
        "sel16": sel16,
        "identf": np.eye(128, dtype=np.float32),
    }


def run_moe(x3, W):
    nc = build_moe()
    hi = moe_host_inputs(W)
    in_maps = []
    for c in range(NCORE):
        b, t0 = c // 4, (c % 4) * TPC
        m = {"x3f": np.ascontiguousarray(x3[b, t0:t0 + TPC, :].T)}
        m.update(hi)
        in_maps.append(m)
    res = run_bass_kernel_spmd(nc, in_maps, core_ids=list(range(NCORE)))
    out = np.zeros((BATCH, SEQ, DM), np.float32)
    for c in range(NCORE):
        b, t0 = c // 4, (c % 4) * TPC
        out[b, t0:t0 + TPC, :] = res.results[c]["outf"].T
    return out


def build_l0ffn(dump_pa=False, dump_kv=-1):
    nc = bass.Bass("TRN2", target_bir_lowering=False)
    T = TPC
    XW = 2561
    NW = 10
    x_d = din(nc, "xin", [DM, XW])
    wqkv_d = din(nc, "Wqkv", [DM, 3 * DM])
    wo_d = din(nc, "Wo", [DM, DM])
    w1_d = din(nc, "W1p", [DM, 5632])
    b1_d = din(nc, "b1p", [128, 44])
    w2_d = din(nc, "W2p", [2816, DM])
    b2_d = din(nc, "b2c", [128, 8])
    lng_d = din(nc, "lngc", [128, 8])
    lnb_d = din(nc, "lnbc", [128, 8])
    qs_d = din(nc, "qscc", [128, 1])
    ks_d = din(nc, "kscc", [128, 1])
    qsw_d = din(nc, "qswc", [128, 1])
    ksw_d = din(nc, "kswc", [128, 1])
    tab_d = din(nc, "tabs", [128, 4 * XW], BF16)
    mskE_d = din(nc, "mskE", [128, 1024], BF16)
    mskO_d = din(nc, "mskO", [128, 1024], BF16)
    mskA_d = din(nc, "mskA", [128, 1024], BF16)
    rot_d = din(nc, "rotm", [128, 128], BF16)
    on64_d = din(nc, "on64", [128, 128])
    onc_d = din(nc, "onesc", [128, 1])
    onr_d = din(nc, "onesr", [1, 128])
    idb_d = din(nc, "identb", [128, 128], BF16)
    pa_d = nc.dram_tensor("pa", [DM, 2304], F32, kind="Internal").ap()
    out_d = dout(nc, "xout", [DM, T])
    kvd = nc.dram_tensor("kvdbg", [128, 16384], BF16,
                         kind="ExternalOutput").ap() if dump_kv >= 0 else None

    P = Prog()
    st = ExitStack()
    sb = lambda name, shape, dt=F32: st.enter_context(nc.sbuf_tensor(name, list(shape), dt))
    ps = lambda name, shape, dt=F32: st.enter_context(nc.psum_tensor(name, list(shape), dt))
    with st:
        xin = sb("xint", [128, NCH * XW])
        wqb = sb("wqbt", [128, NCH * 3072], BF16)
        wob = sb("wobt", [128, NCH * 1024], BF16)
        tabs = sb("tabst", [128, 4 * XW], BF16)
        mskE = sb("mskEt", [128, 1024], BF16)
        mskO = sb("mskOt", [128, 1024], BF16)
        mskA = sb("mskAt", [128, 1024], BF16)
        kroll = sb("krollt", [128, NCH * 512], BF16)
        vroll = sb("vrollt", [128, 4 * 1024], BF16)
        scrA = sb("scrA", [128, NCH * 256])
        xsb = scrA[:, 0:1024].bitcast(BF16)
        khat = scrA[:, 1024:2048].bitcast(BF16)
        outfm = scrA[:].bitcast(BF16)
        qhat = sb("qhatt", [128, NCH * 256], BF16)
        simt = sb("simtt", [128, 512])
        simt2 = mskA[:].bitcast(F32)
        scr2 = (simt, simt2)
        pbf = sb("pbft", [128, 1024], BF16)
        ptb = sb("ptbt", [128, 4 * 256], BF16)
        nmax = sb("nmaxt", [128, 8])
        evbA = pbf[:].bitcast(F32)
        evbB = ptb[:].bitcast(F32)
        qsc = sb("qscb", [128, 1])
        ksc = sb("kscb", [128, 1])
        qsw = sb("qswb", [128, 1])
        ksw = sb("kswb", [128, 1])
        rotm = sb("rotmb", [128, 128], BF16)
        on64 = sb("on64b", [128, 128])
        onc = sb("oncb", [128, 1])
        onr = sb("onrb", [1, 128])
        idb = sb("idbb", [128, 128], BF16)
        lng = sb("lngb", [128, 8])
        lnb = sb("lnbb", [128, 8])
        b1c = sb("b1cb", [128, 44])
        b2c = sb("b2cb", [128, 8])
        pas = xin
        stats = xin[0:1, NCH * 1025 + 256:NCH * 1025 + 256 + 2048]
        sqB = xin[:, 18000:18512]
        sqB2 = xin[:, 18512:19024]
        hbf = wqb[:].bitcast(BF16)
        xnbt = wob[:].bitcast(BF16)
        gs0 = mskE[:].bitcast(F32)
        gs1 = mskO[:].bitcast(F32)
        w1buf = (scrA[:, 0:1024].bitcast(BF16), scrA[:, 1024:2048].bitcast(BF16))
        w2buf = (vroll[:, 0:2816], kroll[:, 0:2816])

        P1 = ps("P1", [128, 512])
        P2 = ps("P2", [128, 512])
        P3 = ps("P3", [128, 512])
        P4 = ps("P4", [128, 512])
        PB = ps("PBB", [128, 1024])
        PT = ps("PTT", [128, 1024], BF16)
        PT2 = ps("PTT2", [128, 1024], BF16)

        cD = Ctr(st.enter_context(nc.semaphore("cD")))
        cW = Ctr(st.enter_context(nc.semaphore("cW")))
        cE = Ctr(st.enter_context(nc.semaphore("cE")))
        cV = Ctr(st.enter_context(nc.semaphore("cV")))
        cS = Ctr(st.enter_context(nc.semaphore("cS")))
        cO = Ctr(st.enter_context(nc.semaphore("cO")))
        cG = Ctr(st.enter_context(nc.semaphore("cG")))
        CTRS = (cD, cW, cE, cV, cS, cO, cG)
        block = st.enter_context(nc.Block())
        D = Dep(P)

        D.on("sync", lambda E: E.dma_start(
            out=xin[:].rearrange("p (c t) -> p c t", t=XW),
            in_=x_d.rearrange("(c p) t -> p c t", p=128)), writes=("xin",), inc=cD, dma=True)
        D.on("gpsimd", lambda E: E.dma_start(
            out=wqb[:].rearrange("p (c t) -> p c t", t=3072),
            in_=wqkv_d.rearrange("(c p) t -> p c t", p=128)), writes=("wqb",), inc=cG, dma=True)
        D.on("gpsimd", lambda E: E.dma_start(
            out=wob[:].rearrange("p (c t) -> p c t", t=1024),
            in_=wo_d.rearrange("(c p) t -> p c t", p=128)), writes=("wob",), inc=cG, dma=True)
        for nm, dst, key in ((tab_d, tabs[:], "tabs"), (mskE_d, mskE[:], "mskE"),
                             (mskO_d, mskO[:], "mskO"), (mskA_d, mskA[:], "mskA"),
                             (rot_d, rotm[:], "cst"), (idb_d, idb[:], "cst"),
                             (qs_d, qsc[:], "cst"), (ks_d, ksc[:], "cst"),
                             (qsw_d, qsw[:], "cst"), (ksw_d, ksw[:], "cst"),
                             (onc_d, onc[:], "cst"), (onr_d, onr[:], "cst"),
                             (lng_d, lng[:], "cst"), (lnb_d, lnb[:], "cst"),
                             (b1_d, b1c[:], "cst"), (b2_d, b2c[:], "cst"),
                             (on64_d, on64[:], "cst")):
            D.on("sync", lambda E, nm=nm, dst=dst: E.dma_start(out=dst, in_=nm[:]),
                 writes=(key,), inc=cW, dma=True)
        for c0, scl in ((0, qsc), (XW, qsw), (2 * XW, ksc), (3 * XW, ksw)):
            D.on("vector", lambda E, c0=c0, scl=scl: E.tensor_scalar_mul(
                tabs[:, c0:c0 + XW], tabs[:, c0:c0 + XW], scl[:, 0:1]),
                 reads=("tabs", "cst"), writes=("tabs",), inc=cV)

        for k in range(NW):
            par = k % 2
            col = 1 + 256 * k
            for c in range(NCH):
                off = col if c < 4 else col - 1
                D.on("vector", lambda E, c=c, off=off: E.tensor_copy(
                    xsb[:, c * 256:(c + 1) * 256],
                    xin[:, c * XW + off:c * XW + off + 256]),
                     reads=("xin",), writes=("scrA",), inc=cV)
            for which in range(2):
                wofs = which * 1024
                dsth = qhat if which == 0 else khat
                dkey = "qhat" if which == 0 else "scrA"
                tc0 = (0 if which == 0 else 2 * XW)
                ts0 = (XW if which == 0 else 3 * XW)
                for oc in range(NCH):
                    PQ, pk = (P1, "P1") if oc % 2 == 0 else (P2, "P2")
                    pp = (oc % 2) if k >= 3 else 0
                    sk = "simt%d" % pp
                    sqr = scr2[pp][:, 0:256]
                    rsd = scr2[pp][:, 256:512]
                    PL, plk = (P3, "P3") if pp == 0 else (P4, "P4")

                    def mm_q(E, oc=oc, wofs=wofs, PQ=PQ):
                        for c in range(NCH):
                            mm = E.matmul(PQ[:, 0:256],
                                          wqb[:, c * 3072 + wofs + oc * 128:
                                              c * 3072 + wofs + oc * 128 + 128],
                                          xsb[:, c * 256:(c + 1) * 256],
                                          start=(c == 0), stop=(c == NCH - 1))
                        return mm
                    D.on("tensor", mm_q, reads=("wqb", "scrA"), writes=(pk,), inc=cE)
                    D.on("scalar", lambda E, PQ=PQ, sqr=sqr: E.activation(
                        sqr, PQ[:, 0:256], AF.Square),
                         reads=(pk,), writes=(sk,), inc=cS)
                    D.on("tensor", lambda E, sqr=sqr, PL=PL: E.matmul(
                        PL[:, 0:256], on64[:], sqr, start=True, stop=True),
                         reads=(sk, "cst"), writes=(plk,), inc=cE)
                    D.on("scalar", lambda E, rsd=rsd, PL=PL: E.activation(
                        rsd, PL[:, 0:256], AF.Sqrt),
                         reads=(plk,), writes=(sk,), inc=cS)
                    D.on("vector", lambda E, rsd=rsd: E.tensor_scalar_max(rsd, rsd, 1e-12),
                         reads=(sk,), writes=(sk,), inc=cV)
                    D.on("vector", lambda E, rsd=rsd: E.reciprocal(rsd, rsd),
                         reads=(sk,), writes=(sk,), inc=cV)
                    D.on("vector", lambda E, oc=oc, PQ=PQ, dsth=dsth, dkey=dkey, rsd=rsd:
                         E.tensor_tensor(dsth[:, oc * 256:(oc + 1) * 256], PQ[:, 0:256],
                                         rsd, ALU.mult),
                         reads=(pk, sk), writes=(dkey,), inc=cV)
                for oc in range(NCH):
                    PR_, prk = (P3, "P3") if oc % 2 == 0 else (P4, "P4")
                    pp = (oc % 2) if k >= 3 else 0
                    sk = "simt%d" % pp
                    t1 = scr2[pp][:, 0:256]
                    t2 = scr2[pp][:, 256:512]
                    D.on("tensor", lambda E, oc=oc, dsth=dsth, PR_=PR_: E.matmul(
                        PR_[:, 0:256], rotm[:], dsth[:, oc * 256:(oc + 1) * 256],
                        start=True, stop=True), reads=("cst", dkey), writes=(prk,),
                         inc=cE)
                    D.on("vector", lambda E, oc=oc, dsth=dsth, tc0=tc0, col=col, t1=t1:
                         E.tensor_tensor(t1, dsth[:, oc * 256:(oc + 1) * 256],
                                         tabs[:, tc0 + col:tc0 + col + 256], ALU.mult),
                         reads=(dkey, "tabs"), writes=(sk,), inc=cV)
                    D.on("vector", lambda E, ts0=ts0, col=col, PR_=PR_, t2=t2:
                         E.tensor_tensor(t2, PR_[:, 0:256],
                                         tabs[:, ts0 + col:ts0 + col + 256], ALU.mult),
                         reads=(prk, "tabs"), writes=(sk,), inc=cV)
                    if which == 0:
                        D.on("vector", lambda E, oc=oc, t1=t1, t2=t2: E.tensor_add(
                            qhat[:, oc * 256:(oc + 1) * 256], t1, t2),
                             reads=(sk,), writes=("qhat",), inc=cV)
                    else:
                        D.on("vector", lambda E, oc=oc, par=par, t1=t1, t2=t2: E.tensor_add(
                            kroll[:, oc * 512 + par * 256:oc * 512 + par * 256 + 256],
                            t1, t2), reads=(sk,), writes=("kroll%d" % par,), inc=cV)
            for jt in range(2):
                def mm_v(E, jt=jt):
                    for half in range(2):
                        for c in range(NCH):
                            mm = E.matmul(PB[:, half * 512:(half + 1) * 512],
                                          xsb[:, c * 256 + jt * 128:c * 256 + jt * 128 + 128],
                                          wqb[:, c * 3072 + 2048 + half * 512:
                                              c * 3072 + 2048 + (half + 1) * 512],
                                          start=(c == 0), stop=(c == NCH - 1))
                    return mm
                D.on("tensor", mm_v, reads=("scrA", "wqb"), writes=("PBh0", "PBh1"),
                     inc=cE)
                D.on("vector", lambda E, jt=jt, par=par: E.tensor_copy(
                    vroll[:, (par * 2 + jt) * 1024:(par * 2 + jt + 1) * 1024],
                    PB[:, 0:1024]), reads=("PBh0", "PBh1"),
                     writes=("vroll%d" % (par * 2 + jt),), inc=cV)
            if k == dump_kv:
                D.on("sync", lambda E: E.dma_start(out=kvd[:, 0:4096], in_=kroll[:]),
                     reads=("kroll0", "kroll1"), inc=cO, dma=True)
                D.on("sync", lambda E: E.dma_start(out=kvd[:, 4096:8192], in_=vroll[:]),
                     reads=("vroll0", "vroll1", "vroll2", "vroll3"), inc=cO, dma=True)
                D.on("sync", lambda E: E.dma_start(out=kvd[:, 8192:10240], in_=qhat[:]),
                     reads=("qhat",), inc=cO, dma=True)
            if k == 0:
                continue
            msk = mskA if k == 2 else (mskO if par else mskE)
            mkey = "mskA" if k == 2 else ("mskO" if par else "mskE")
            for h in range(16):
                hc, hb = h // 2, (h % 2) * 64
                for it in range(2):
                    PSm, smk = (P1, "P1") if (2 * h + it) % 2 == 0 else (P2, "P2")
                    j4 = (2 * h + it) % 4
                    i2 = (2 * h + it) % 2
                    nmx = nmax[:, j4:j4 + 1]
                    nms = nmax[:, 4 + j4:5 + j4]
                    pb_ = pbf[:, i2 * 512:(i2 + 1) * 512]
                    pkey = "pbf%d" % i2
                    i2s = i2 if k >= 3 else 0
                    sims = scr2[i2s]
                    sik = "simt%d" % i2s

                    def mm_sim(E, hc=hc, hb=hb, it=it, PSm=PSm):
                        return E.matmul(PSm[:, 0:512],
                                        qhat[hb:hb + 64, hc * 256 + it * 128:
                                             hc * 256 + it * 128 + 128],
                                        kroll[hb:hb + 64, hc * 512:(hc + 1) * 512],
                                        start=True, stop=True)
                    D.on("tensor", mm_sim, reads=("qhat", "kroll0", "kroll1"),
                         writes=(smk,), inc=cE)
                    D.on("vector", lambda E, it=it, msk=msk, PSm=PSm, sims=sims:
                         E.scalar_tensor_tensor(sims[:], PSm[:, 0:512], 8.0,
                                                msk[:, it * 512:(it + 1) * 512],
                                                ALU.mult, ALU.add),
                         reads=(smk, mkey), writes=(sik,), inc=cV)
                    D.on("scalar", lambda E, nms=nms, pb_=pb_, sims=sims: E.activation(
                        pb_, sims[:], AF.Exp, accum_out=nms),
                         reads=(sik,), writes=(pkey, "ns%d" % j4), inc=cS)
                    rtok = D.on("vector", lambda E, nmx=nmx, nms=nms:
                                E.reciprocal(nmx, nms),
                                reads=("ns%d" % j4,), writes=("nm%d" % j4,), inc=cV)
                    D.on("vector", lambda E, nmx=nmx, pb_=pb_: E.tensor_scalar_mul(
                        pb_, pb_, nmx), reads=(pkey, "nm%d" % j4), writes=(pkey,),
                         inc=cV, force=(rtok,))
                    PTx, ptk = (PT, "PT") if it == 0 else (PT2, "PT2")

                    def mm_pt(E, PTx=PTx, pb_=pb_):
                        for jc in range(4):
                            mm = E.transpose(PTx[:, jc * 128:(jc + 1) * 128],
                                             pb_[:, jc * 128:(jc + 1) * 128], idb[:])
                        return mm
                    D.on("tensor", mm_pt, reads=(pkey, "cst"), writes=(ptk,), inc=cE)
                    for jc in range(4):
                        D.on("vector", lambda E, jc=jc, it=it, PTx=PTx: E.tensor_copy(
                            ptb[:, jc * 256 + it * 128:jc * 256 + it * 128 + 128],
                            PTx[:, jc * 128:(jc + 1) * 128]),
                             reads=(ptk,), writes=("ptb",), inc=cV)

                if k == dump_kv and h == 0:
                    D.on("sync", lambda E: E.dma_start(out=kvd[:, 14336:15360],
                                                       in_=ptb[:]),
                         reads=("ptb",), inc=cO, dma=True)
                    D.on("sync", lambda E: E.dma_start(out=kvd[:, 15360:16384],
                                                       in_=pbf[:]),
                         reads=("pbf0", "pbf1"), inc=cO, dma=True)
                PAo, pak = (P3, "P3") if h % 2 == 0 else (P4, "P4")

                def mm_av(E, h=h, PAo=PAo):
                    for jc in range(4):
                        mm = E.matmul(PAo[0:64, 0:256],
                                      vroll[:, jc * 1024 + 64 * h:jc * 1024 + 64 * h + 64],
                                      ptb[:, jc * 256:(jc + 1) * 256],
                                      start=(jc == 0), stop=(jc == 3))
                    return mm
                D.on("tensor", mm_av, reads=("ptb", "vroll0", "vroll1", "vroll2",
                                             "vroll3"), writes=(pak,), inc=cE)
                D.on("vector", lambda E, hc=hc, hb=hb, PAo=PAo: E.tensor_copy(
                    outfm[hb:hb + 64, hc * 256:(hc + 1) * 256], PAo[0:64, 0:256]),
                     reads=(pak,), writes=("scrA",), inc=cV)
            if k == dump_kv:
                D.on("sync", lambda E: E.dma_start(out=kvd[:, 10240:14336], in_=outfm[:]),
                     reads=("scrA",), inc=cO, dma=True)
            for oc in range(NCH):
                half = oc % 2

                def mm_wo(E, oc=oc, half=half):
                    for c in range(NCH):
                        mm = E.matmul(PB[:, half * 512:half * 512 + 256],
                                      wob[:, c * 1024 + oc * 128:c * 1024 + oc * 128 + 128],
                                      outfm[:, c * 256:(c + 1) * 256],
                                      start=(c == 0), stop=(c == NCH - 1))
                    return mm
                D.on("tensor", mm_wo, reads=("wob", "scrA"),
                     writes=("PBh%d" % half,), inc=cE)
                D.on("vector", lambda E, oc=oc, col=col, half=half: E.tensor_add(
                    evbA[:, half * 256:(half + 1) * 256],
                    PB[:, half * 512:half * 512 + 256],
                    xin[:, oc * XW + col:oc * XW + col + 256]),
                     reads=("PBh%d" % half, "xin"), writes=("pbf%d" % half,), inc=cV)
                D.on("sync", lambda E, oc=oc, k=k, half=half: E.dma_start(
                    out=pa_d[oc * 128:(oc + 1) * 128, 256 * (k - 1):256 * k],
                    in_=evbA[:, half * 256:(half + 1) * 256]),
                     reads=("pbf%d" % half,), writes=("pa%d" % oc,), inc=cO, dma=True)

        if dump_pa:
            for c in range(NCH):
                D.on("sync", lambda E, c=c: E.dma_start(
                    out=out_d[c * 128:(c + 1) * 128, :],
                    in_=pa_d[c * 128:(c + 1) * 128, 256:2304]),
                     reads=("pa%d" % c,), inc=cO, dma=True)
            P.on("sync", lambda E: E.nop(), after=(cO,))
            P.emit(block)
            return nc

        D.barrier(CTRS)
        for sl_i in range(2):
            pcol = 255 + 1024 * sl_i
            for c in range(NCH):
                D.on("sync", lambda E, c=c, pcol=pcol: E.dma_start(
                    out=pas[:, c * 1025:(c + 1) * 1025],
                    in_=pa_d[c * 128:(c + 1) * 128, pcol:pcol + 1025]),
                     writes=("xin",), inc=cD, dma=True)

            def xs2(c):
                off = 1 if c < 4 else 0
                return pas[:, c * 1025 + off:c * 1025 + off + 1024]
            for s in range(2):
                sl = slice(s * 512, (s + 1) * 512)

                def mm_sum(E, sl=sl):
                    for c in range(NCH):
                        mm = E.matmul(P3[0:1, 0:512], onc[:], xs2(c)[:, sl],
                                      start=(c == 0), stop=(c == NCH - 1))
                    return mm
                D.on("tensor", mm_sum, reads=("xin", "cst"), writes=("P3",), inc=cE)
                D.on("scalar", lambda E: E.copy(stats[:, 0:512], P3[0:1, 0:512]),
                     reads=("P3",), writes=("stats",), inc=cS)
                for c in range(NCH):
                    sqx, sqk = (sqB, "sqB") if c % 2 == 0 else (sqB2, "sqB2")
                    D.on("scalar", lambda E, c=c, sl=sl, sqx=sqx: E.activation(
                        sqx, xs2(c)[:, sl], AF.Square), reads=("xin",),
                         writes=(sqk,), inc=cS)
                    D.on("tensor", lambda E, c=c, sqx=sqx: E.matmul(
                        P4[0:1, 0:512], onc[:], sqx, start=(c == 0),
                        stop=(c == NCH - 1)),
                         reads=(sqk, "cst"), writes=("P4",), inc=cE)
                D.on("scalar", lambda E: E.copy(stats[:, 512:1024], P4[0:1, 0:512]),
                     reads=("P4",), writes=("stats",), inc=cS)
                D.on("vector", lambda E: E.tensor_scalar_mul(
                    stats[:, 0:512], stats[:, 0:512], 1.0 / DM),
                     reads=("stats",), writes=("stats",), inc=cV)
                D.on("vector", lambda E: E.tensor_scalar_mul(
                    stats[:, 512:1024], stats[:, 512:1024], 1.0 / DM),
                     reads=("stats",), writes=("stats",), inc=cV)
                D.on("vector", lambda E: E.tensor_tensor(
                    stats[:, 1024:1536], stats[:, 0:512], stats[:, 0:512], ALU.mult),
                     reads=("stats",), writes=("stats",), inc=cV)
                D.on("vector", lambda E: E.tensor_sub(
                    stats[:, 512:1024], stats[:, 512:1024], stats[:, 1024:1536]),
                     reads=("stats",), writes=("stats",), inc=cV)
                D.on("vector", lambda E: E.tensor_scalar_add(
                    stats[:, 1024:1536], stats[:, 512:1024], 1e-5),
                     reads=("stats",), writes=("stats",), inc=cV)
                D.on("scalar", lambda E: E.activation(
                    stats[:, 1536:2048], stats[:, 1024:1536], AF.Sqrt),
                     reads=("stats",), writes=("stats",), inc=cS)
                D.on("vector", lambda E: E.reciprocal(
                    stats[:, 1024:1536], stats[:, 1536:2048]),
                     reads=("stats",), writes=("stats",), inc=cV)
                D.on("vector", lambda E: E.tensor_tensor(
                    stats[:, 1536:2048], stats[:, 0:512], stats[:, 1024:1536],
                    ALU.mult), reads=("stats",), writes=("stats",), inc=cV)

                def mm_bc(E):
                    E.matmul(PB[:, 0:512], onr[:], stats[:, 1024:1536],
                             start=True, stop=True)
                    return E.matmul(PB[:, 512:1024], onr[:], stats[:, 1536:2048],
                                    start=True, stop=True)
                D.on("tensor", mm_bc, reads=("stats", "cst"),
                     writes=("PBh0", "PBh1"), inc=cE)
                for c in range(NCH):
                    D.on("vector", lambda E, c=c, sl=sl: E.tensor_tensor(
                        sqB, xs2(c)[:, sl], PB[:, 0:512], ALU.mult),
                         reads=("xin", "PBh0"), writes=("sqB",), inc=cV)
                    D.on("vector", lambda E: E.tensor_tensor(
                        sqB, sqB, PB[:, 512:1024], ALU.subtract),
                         reads=("sqB", "PBh1"), writes=("sqB",), inc=cV)
                    D.on("vector", lambda E, c=c, sl=sl: E.tensor_scalar(
                        xnbt[:, c * 1024:(c + 1) * 1024][:, sl], sqB,
                        lng[:, c:c + 1], lnb[:, c:c + 1], ALU.mult, ALU.add),
                         reads=("sqB", "cst"), writes=("xnb",), inc=cV)
            for t in range(22):
                w1b = w1buf[t % 2]
                wk = "w1%d" % (t % 2)
                D.on("gpsimd", lambda E, t=t, w1b=w1b: E.dma_start(
                    out=w1b.rearrange("p (c t) -> p c t", t=256),
                    in_=w1_d[:, 256 * t:256 * (t + 1)].rearrange("(c p) t -> p c t", p=128)),
                     writes=(wk,), inc=cG, dma=True)
                for n2 in range(2):
                    PAx, pakx = (P1, "P1") if n2 == 0 else (P2, "P2")
                    PGx, pgkx = (P3, "P3") if n2 == 0 else (P4, "P4")
                    gsx, gkx = (gs0, "gs0") if n2 == 0 else (gs1, "gs1")

                    def mm_a(E, n2=n2, PAx=PAx, w1b=w1b):
                        for c in range(NCH):
                            mm = E.matmul(PAx[:, 0:512],
                                          w1b[:, c * 256:c * 256 + 128],
                                          xnbt[:, c * 1024 + n2 * 512:c * 1024 + (n2 + 1) * 512],
                                          start=(c == 0), stop=(c == NCH - 1))
                        return mm
                    D.on("tensor", mm_a, reads=(wk, "xnb"), writes=(pakx,), inc=cE)

                    def mm_g(E, n2=n2, PGx=PGx, w1b=w1b):
                        for c in range(NCH):
                            mm = E.matmul(PGx[:, 0:512],
                                          w1b[:, c * 256 + 128:c * 256 + 256],
                                          xnbt[:, c * 1024 + n2 * 512:c * 1024 + (n2 + 1) * 512],
                                          start=(c == 0), stop=(c == NCH - 1))
                        return mm
                    D.on("tensor", mm_g, reads=(wk, "xnb"), writes=(pgkx,), inc=cE)
                    D.on("scalar", lambda E, t=t, PGx=PGx, gsx=gsx: E.activation(
                        gsx[:, 0:512], PGx[:, 0:512], AF.Gelu,
                        bias=b1c[:, 2 * t + 1:2 * t + 2]),
                         reads=(pgkx, "cst"), writes=(gkx,), inc=cS)
                    D.on("vector", lambda E, t=t, n2=n2, PAx=PAx, gsx=gsx:
                         E.scalar_tensor_tensor(
                             hbf[:, t * 1024 + n2 * 512:t * 1024 + (n2 + 1) * 512],
                             PAx[:, 0:512], b1c[:, 2 * t:2 * t + 1], gsx[:, 0:512],
                             ALU.add, ALU.mult), reads=(pakx, "cst", gkx),
                         writes=("hbf",), inc=cV)
            for oc in range(NCH):
                w2b = w2buf[oc % 2]
                w2k = "w2%d" % (oc % 2)
                D.on("gpsimd", lambda E, oc=oc, w2b=w2b: E.dma_start(
                    out=w2b.rearrange("p (c t) -> p c t", t=128),
                    in_=w2_d[:, oc * 128:(oc + 1) * 128].rearrange("(c p) t -> p c t", p=128)),
                     writes=(w2k,), inc=cG, dma=True)
                for n2 in range(2):
                    half = n2

                    def mm_y(E, n2=n2, w2b=w2b):
                        for t in range(22):
                            mm = E.matmul(PB[:, n2 * 512:(n2 + 1) * 512],
                                          w2b[:, t * 128:(t + 1) * 128],
                                          hbf[:, t * 1024 + n2 * 512:t * 1024 + (n2 + 1) * 512],
                                          start=(t == 0), stop=(t == 21))
                        return mm
                    D.on("tensor", mm_y, reads=(w2k, "hbf"),
                         writes=("PBh%d" % half,), inc=cE)
                    evx = evbA if half == 0 else evbB
                    evkeys = ("pbf0", "pbf1") if half == 0 else ("ptb",)
                    D.on("vector", lambda E, oc=oc, n2=n2, evx=evx:
                         E.scalar_tensor_tensor(
                             evx[:, 0:512],
                             PB[:, n2 * 512:(n2 + 1) * 512], b2c[:, oc:oc + 1],
                             pas[:, oc * 1025 + 1 + n2 * 512:oc * 1025 + 1 + (n2 + 1) * 512],
                             ALU.add, ALU.add), reads=("PBh%d" % half, "cst", "xin"),
                         writes=evkeys, inc=cV)
                    D.on("sync", lambda E, oc=oc, n2=n2, sl_i=sl_i, evx=evx:
                         E.dma_start(
                             out=out_d[oc * 128:(oc + 1) * 128,
                                       sl_i * 1024 + n2 * 512:sl_i * 1024 + (n2 + 1) * 512],
                             in_=evx[:, 0:512]),
                         reads=evkeys, writes=("out",), inc=cO, dma=True)
        P.on("sync", lambda E: E.nop(), after=(cO,))
        P.emit(block)
    return nc


def l0ffn_host_consts():
    import ml_dtypes
    XW = 2561
    DH, W = 64, 256
    half = np.arange(0, DH, 2, dtype=np.float32)
    inv_freq = (1.0 / 10000.0 ** (half / DH)).astype(np.float32)
    base_scale = ((half + 0.4 * DH) / (1.4 * DH)).astype(np.float32)
    def mk_mask(parity, window0):
        m = np.full((2, 128, 512), NEG, np.float32)
        for itile in range(2):
            for p in range(128):
                i = itile * 128 + p
                cur0 = 0 if parity == 0 else 256
                prv0 = 256 if parity == 0 else 0
                m[itile, p, cur0 + 0:cur0 + i + 1] = 0.0
                if not window0:
                    m[itile, p, prv0 + i:prv0 + 256] = 0.0
        return m.transpose(1, 0, 2).reshape(128, 1024)
    mskE = mk_mask(0, False)
    mskO = mk_mask(1, False)
    mskA0 = mk_mask(0, True)
    rot = np.zeros((128, 128), np.float32)
    for blk in range(2):
        b = blk * 64
        for d in range(32):
            rot[b + d + 32, b + d] = -1.0
            rot[b + d, b + d + 32] = 1.0
    on64 = np.zeros((128, 128), np.float32)
    on64[0:64, 0:64] = 1.0
    on64[64:128, 64:128] = 1.0
    return {
        "mskE": mskE.astype(ml_dtypes.bfloat16), "mskO": mskO.astype(ml_dtypes.bfloat16),
        "mskA0": mskA0.astype(ml_dtypes.bfloat16),
        "rotm": rot.astype(ml_dtypes.bfloat16),
        "on64": on64,
        "onesc": np.ones((128, 1), np.float32),
        "onesr": np.ones((1, 128), np.float32),
        "identb": np.eye(128, dtype=np.float32).astype(ml_dtypes.bfloat16),
        "inv_freq": inv_freq, "base_scale": base_scale,
    }


def l0ffn_tables(core, consts):
    import ml_dtypes
    XW = 2561
    t0 = (core % 4) * TPC - 513
    t = (t0 + np.arange(XW, dtype=np.float32))
    freqs = t[:, None] * consts["inv_freq"][None, :]
    cos = np.cos(np.concatenate([freqs, freqs], 1))
    sin = np.sin(np.concatenate([freqs, freqs], 1))
    power = (t - SEQ // 2) / (256 // 2)
    scale = consts["base_scale"][None, :] ** power[:, None]
    scale = np.concatenate([scale, scale], 1).astype(np.float32)
    cq = (cos * scale).T
    sq = (sin * scale).T
    ck = (cos / scale).T
    sk = (sin / scale).T
    tab = np.concatenate([np.tile(x, (2, 1)) for x in (cq, sq, ck, sk)], axis=1)
    return np.ascontiguousarray(tab).astype(ml_dtypes.bfloat16)


def l0ffn_layer_weights(inp, i):
    W1 = inp["L0_ffn_W1"][i]
    W2 = inp["L0_ffn_W2"][i]
    b1 = inp["L0_ffn_b1"][i]
    W1p = np.zeros((DM, 5632), np.float32)
    b1p = np.zeros((128, 44), np.float32)
    for tt in range(22):
        lo, hi = 128 * tt, min(128 * (tt + 1), 2730)
        n = hi - lo
        if n > 0:
            W1p[:, 256 * tt:256 * tt + n] = W1[:, lo:hi]
            W1p[:, 256 * tt + 128:256 * tt + 128 + n] = W1[:, 2730 + lo:2730 + hi]
            b1p[0:n, 2 * tt] = b1[lo:hi]
            b1p[0:n, 2 * tt + 1] = b1[2730 + lo:2730 + hi]
    W2p = np.zeros((2816, DM), np.float32)
    W2p[0:2730, :] = W2
    W2pp = np.zeros((2816, DM), np.float32)
    for tt in range(22):
        lo, hi = 128 * tt, min(128 * (tt + 1), 2730)
        n = hi - lo
        if n > 0:
            W2pp[128 * tt:128 * tt + n, :] = W2[lo:hi, :]
    qs0 = inp["L0_ffn_q_scale"][i]
    ks0 = inp["L0_ffn_k_scale"][i]
    qs = np.tile(qs0, 2)[:, None].astype(np.float32)
    ks = np.tile(ks0, 2)[:, None].astype(np.float32)
    qsw = np.tile(np.concatenate([qs0[32:], qs0[:32]]), 2)[:, None].astype(np.float32)
    ksw = np.tile(np.concatenate([ks0[32:], ks0[:32]]), 2)[:, None].astype(np.float32)
    return {
        "Wqkv": inp["L0_ffn_Wqkv"][i],
        "Wo": inp["L0_ffn_Wo"][i],
        "W1p": W1p, "b1p": b1p, "W2p": W2pp,
        "b2c": np.ascontiguousarray(inp["L0_ffn_b2"][i].reshape(8, 128).T),
        "lngc": np.ascontiguousarray(inp["L0_ffn_ln_g"][i].reshape(8, 128).T),
        "lnbc": np.ascontiguousarray(inp["L0_ffn_ln_b"][i].reshape(8, 128).T),
        "qscc": qs, "kscc": ks, "qswc": qsw, "kswc": ksw,
    }


def run_l0ffn_layer(nc_cache, x_full, inp, i, consts, dump_pa=False, dump_kv=-1):
    key = ("l0ffn", dump_pa, dump_kv)
    if nc_cache.get(key) is None:
        nc_cache[key] = build_l0ffn(dump_pa=dump_pa, dump_kv=dump_kv)
    nc = nc_cache[key]
    lw = l0ffn_layer_weights(inp, i)
    in_maps = []
    for c in range(NCORE):
        b, t0 = c // 4, (c % 4) * TPC
        lo = t0 - 513
        if lo >= 0:
            xs = x_full[b, lo:t0 + TPC, :]
        else:
            xs = np.concatenate([np.zeros((513, DM), np.float32), x_full[b, 0:t0 + TPC, :]], 0)
        m = {"xin": np.ascontiguousarray(xs.T),
             "tabs": nc_cache["tabs"][c],
             "mskE": consts["mskE"], "mskO": consts["mskO"],
             "mskA": consts["mskA0"] if t0 == 0 else consts["mskE"],
             "rotm": consts["rotm"], "on64": consts["on64"],
             "onesc": consts["onesc"], "onesr": consts["onesr"],
             "identb": consts["identb"]}
        m.update(lw)
        in_maps.append(m)
    res = run_bass_kernel_spmd(nc, in_maps, core_ids=list(range(NCORE)))
    out = np.zeros((BATCH, SEQ, DM), np.float32)
    for c in range(NCORE):
        b, t0 = c // 4, (c % 4) * TPC
        out[b, t0:t0 + TPC, :] = res.results[c]["xout"].T
    if dump_kv >= 0:
        return out, np.asarray(res.results[0]["kvdbg"])
    return out


def build_ssm():
    nc = bass.Bass("TRN2", target_bir_lowering=False)
    u_d = din(nc, "usig", [2, 128, 8192])
    alog_d = din(nc, "alog", [1, 64])
    ldt_d = din(nc, "ldt", [1, 128])
    cnt_d = din(nc, "cnt", [64, 128])
    drow_d = din(nc, "drow", [1, 128])
    fz_d = din(nc, "Fz", [64, 256])
    wz_d = din(nc, "Wz", [128, 256])
    f2re_d = din(nc, "F2re", [128, 128])
    f2im_d = din(nc, "F2im", [128, 128])
    fiz_d = din(nc, "Fiz", [128, 256])
    fiz2_d = din(nc, "Fiz2", [128, 256])
    w2z_d = din(nc, "W2z", [128, 256])
    f3re_d = din(nc, "F3re", [128, 64])
    f3imn_d = din(nc, "F3imn", [128, 64])
    rampj_d = din(nc, "rampj", [1, 128])
    rampc_d = din(nc, "rampc", [1, 64])
    onr_d = din(nc, "onesr", [1, 128])
    idf_d = din(nc, "identf", [128, 128])
    dta_d = nc.dram_tensor("dtat", [128, 64], F32, kind="Internal").ap()
    y_d = dout(nc, "ysig", [2, 128, 8192])

    P = Prog()
    st = ExitStack()
    sb = lambda name, shape, dt=F32: st.enter_context(nc.sbuf_tensor(name, list(shape), dt))
    ps = lambda name, shape, dt=F32: st.enter_context(nc.psum_tensor(name, list(shape), dt))
    with st:
        fz = sb("fz_s", [64, 256], F32R)
        wz = sb("wz_s", [128, 256])
        f2re = sb("f2re_s", [128, 128], F32R)
        f2im = sb("f2im_s", [128, 128], F32R)
        fiz = sb("fiz_s", [128, 256], F32R)
        fiz2 = sb("fiz2_s", [128, 256], F32R)
        w2z = sb("w2z_s", [128, 256])
        f3re = sb("f3re_s", [128, 64], F32R)
        f3imn = sb("f3imn_s", [128, 64], F32R)
        rampj = sb("rampj_s", [1, 128])
        rampc = sb("rampc_s", [1, 64])
        onr = sb("onr_s", [1, 128])
        idf = sb("idf_s", [128, 128])
        arow = sb("arow_s", [1, 64])
        dtrow = sb("dtrow_s", [1, 128])
        cnt = sb("cnt_s", [64, 128])
        dbc = sb("dbc_s", [64, 128])
        dtar = sb("dtar_s", [1, 8192])
        uslab = sb("uslab_s", [64, 2 * 8 * 128], F32R)
        rj = sb("rj_s", [64, 128], F32R)
        rc = sb("rc_s", [64, 64], F32R)
        crc = sb("crc_s", [64, 64], F32R)
        kst = sb("kst_s", [64, 128], F32R)
        tmp1 = sb("tmp1_s", [128, 128])
        tmp2 = sb("tmp2_s", [128, 128])
        bre = sb("bre_s", [128, 256], F32R)
        bsw = sb("bsw_s", [128, 256], F32R)
        xk = sb("xk_s", [128, 256], F32R)
        xu = sb("xu_s", [128, 256], F32R)
        pre = sb("pre_s", [128, 256], F32R)
        b2 = sb("b2_s", [128, 256], F32R)
        ysb = sb("ysb_s", [64, 128])

        PA = ps("PA", [128, 256])
        PX = ps("PX", [128, 256])
        PG = ps("PG", [128, 256])
        PY = ps("PY", [64, 128])
        PK = ps("PK", [128, 128])

        cD = Ctr(st.enter_context(nc.semaphore("cD")))
        cW = Ctr(st.enter_context(nc.semaphore("cW")))
        cE = Ctr(st.enter_context(nc.semaphore("cE")))
        cV = Ctr(st.enter_context(nc.semaphore("cV")))
        cS = Ctr(st.enter_context(nc.semaphore("cS")))
        cO = Ctr(st.enter_context(nc.semaphore("cO")))
        cG = Ctr(st.enter_context(nc.semaphore("cG")))
        P.extra["tensor"] = (cV, cS, cG)
        P.extra["vector"] = (cE, cG)
        P.extra["scalar"] = (cE, cG)
        block = st.enter_context(nc.Block())

        for nm, dst in ((fz_d, fz), (f2re_d, f2re), (f2im_d, f2im), (fiz_d, fiz),
                        (fiz2_d, fiz2), (f3re_d, f3re), (f3imn_d, f3imn)):
            P.on("gpsimd", lambda E, nm=nm, dst=dst: E.dma_start(out=dst[:], in_=nm[:]),
                 inc=cG, dma=True)
        for nm, dst in ((wz_d, wz), (w2z_d, w2z), (rampj_d, rampj), (rampc_d, rampc),
                        (onr_d, onr), (idf_d, idf), (alog_d, arow), (ldt_d, dtrow),
                        (cnt_d, cnt)):
            P.on("sync", lambda E, nm=nm, dst=dst: E.dma_start(out=dst[:], in_=nm[:]),
                 inc=cW, dma=True)

        P.on("scalar", lambda E: E.activation(arow[:], arow[:], AF.Exp), after=(cW,), inc=cS)
        P.on("vector", lambda E: E.tensor_scalar_mul(arow[:], arow[:], -1.0),
             after=(cS,), inc=cV)
        P.on("scalar", lambda E: E.activation(dtrow[:], dtrow[:], AF.Exp), after=(cW,), inc=cS)
        P.on("tensor", lambda E: E.matmul(PA[0:64, 0:128], arow[:], dtrow[:],
                                          start=True, stop=True), after=(cV, cS), inc=cE)
        P.on("vector", lambda E: E.tensor_copy(tmp1[0:64, 0:128], PA[0:64, 0:128]),
             after=(cE,), inc=cV)
        P.on("tensor", lambda E: E.transpose(PK[:, 0:64], tmp1[0:64, 0:128], idf[0:64, 0:64]),
             after=(cV,), inc=cE)
        P.on("vector", lambda E: E.tensor_copy(tmp2[:, 0:64], PK[:, 0:64]),
             after=(cE,), inc=cV)
        P.on("sync", lambda E: E.dma_start(out=dta_d[:], in_=tmp2[:, 0:64]),
             after=(cV,), inc=cO, dma=True)
        P.on("sync", lambda E: E.dma_start(out=dtar[:],
                                           in_=dta_d.rearrange("a b -> (a b)").rearrange(
                                               "x -> () x")), after=(cO,), inc=cD, dma=True)
        P.on("tensor", lambda E: E.matmul(PK[0:64, 0:128], onr[0:1, 0:64], drow_d[:]
                                          if False else dbc[0:1, :],
                                          start=True, stop=True), after=(cV, cS), inc=cE
             ) if False else None
        P.on("sync", lambda E: E.dma_start(out=dbc[0:1, :], in_=drow_d[:]), inc=cW, dma=True)
        P.on("tensor", lambda E: E.matmul(PK[0:64, 0:128], onr[0:1, 0:64], dbc[0:1, :],
                                          start=True, stop=True), after=(cW, cV, cS), inc=cE)
        P.on("vector", lambda E: E.tensor_copy(dbc[:], PK[0:64, 0:128]), after=(cE,), inc=cV)

        def twiddle(P_, psrc, wtile, bout, bswout, tag):
            P_.on("vector", lambda E: E.tensor_tensor(tmp1[:], psrc[:, 0:128],
                                                      wtile[:, 0:128], ALU.mult),
                  after=(cE,), inc=cV)
            P_.on("vector", lambda E: E.tensor_tensor(tmp2[:], psrc[:, 128:256],
                                                      wtile[:, 128:256], ALU.mult),
                  after=(cE,), inc=cV)
            P_.on("vector", lambda E: E.tensor_sub(bout[:, 0:128], tmp1[:], tmp2[:]),
                  after=(cV,), inc=cV)
            P_.on("vector", lambda E: E.tensor_tensor(tmp1[:], psrc[:, 0:128],
                                                      wtile[:, 128:256], ALU.mult),
                  after=(cV,), inc=cV)
            P_.on("vector", lambda E: E.tensor_tensor(tmp2[:], psrc[:, 128:256],
                                                      wtile[:, 0:128], ALU.mult),
                  after=(cE, cV), inc=cV)
            P_.on("vector", lambda E: E.tensor_add(bout[:, 128:256], tmp1[:], tmp2[:]),
                  after=(cV,), inc=cV)
            if bswout is not None:
                P_.on("vector", lambda E: E.tensor_scalar_mul(bswout[:, 0:128],
                                                              bout[:, 128:256], -1.0),
                      after=(cV,), inc=cV)
                P_.on("vector", lambda E: E.tensor_copy(bswout[:, 128:256], bout[:, 0:128]),
                      after=(cV,), inc=cV)

        def fwd_fft(P_, lhs_st, xout):
            P_.on("tensor", lambda E: E.matmul(PA[:], lhs_st, fz[:], start=True, stop=True),
                  after=(cD, cW), inc=cE)
            twiddle(P_, PA, wz, bre, bsw, "f")

            def mm2(E):
                E.matmul(PX[:], f2re[:], bre[:], start=True, stop=False)
                return E.matmul(PX[:], f2im[:], bsw[:], start=False, stop=True)
            P_.on("tensor", mm2, after=(cV,), inc=cE)
            P_.on("vector", lambda E, xout=xout: E.tensor_copy(xout[:], PX[:]),
                  after=(cE,), inc=cV)

        for mg in range(16):
            for b_ in range(2):
                P.on("gpsimd", lambda E, mg=mg, b_=b_: E.dma_start(
                    out=uslab[:, b_ * 1024:(b_ + 1) * 1024].rearrange(
                        "p (m t) -> p m t", t=128),
                    in_=u_d[b_, mg * 8:(mg + 1) * 8, :].rearrange(
                        "m (c t) -> c m t", c=64)), after=(cE, cV), inc=cG, dma=True)
            for ml in range(8):
                m = mg * 8 + ml
                P.on("tensor", lambda E, m=m: E.matmul(
                    PK[0:64, 0:128], dtar[0:1, m * 64:(m + 1) * 64], rampj[:],
                    start=True, stop=True), after=(cD, cV, cS), inc=cE)
                P.on("scalar", lambda E: E.activation(rj[:], PK[0:64, 0:128], AF.Exp),
                     after=(cE,), inc=cS)
                P.on("tensor", lambda E, m=m: E.matmul(
                    PK[0:64, 0:64], dtar[0:1, m * 64:(m + 1) * 64], rampc[:],
                    start=True, stop=True), after=(cS, cV), inc=cE)
                P.on("scalar", lambda E: E.activation(rc[:], PK[0:64, 0:64], AF.Exp),
                     after=(cE,), inc=cS)
                P.on("vector", lambda E, m=m: E.tensor_scalar_mul(crc[:], rc[:],
                                                                 cnt[:, m:m + 1]),
                     after=(cS,), inc=cV)
                P.on("tensor", lambda E: E.matmul(PK[0:64, 0:128], crc[:], rj[:],
                                                  start=True, stop=True),
                     after=(cV,), inc=cE)
                P.on("vector", lambda E: E.tensor_copy(kst[:], PK[0:64, 0:128]),
                     after=(cE,), inc=cV)
                fwd_fft(P, kst[:], xk)
                for b in range(2):
                    fwd_fft(P, uslab[:, (b * 8 + ml) * 128:(b * 8 + ml + 1) * 128], xu)
                    P.on("vector", lambda E: E.tensor_tensor(tmp1[:], xu[:, 0:128],
                                                             xk[:, 0:128], ALU.mult),
                         after=(cV,), inc=cV)
                    P.on("vector", lambda E: E.tensor_tensor(tmp2[:], xu[:, 128:256],
                                                             xk[:, 128:256], ALU.mult),
                         after=(cV,), inc=cV)
                    P.on("vector", lambda E: E.tensor_sub(pre[:, 0:128], tmp1[:], tmp2[:]),
                         after=(cV,), inc=cV)
                    P.on("vector", lambda E: E.tensor_tensor(tmp1[:], xu[:, 0:128],
                                                             xk[:, 128:256], ALU.mult),
                         after=(cV,), inc=cV)
                    P.on("vector", lambda E: E.tensor_tensor(tmp2[:], xu[:, 128:256],
                                                             xk[:, 0:128], ALU.mult),
                         after=(cV,), inc=cV)
                    P.on("vector", lambda E: E.tensor_add(pre[:, 128:256], tmp1[:], tmp2[:]),
                         after=(cV,), inc=cV)
                    def mmg(E):
                        E.matmul(PG[:], pre[:, 0:128], fiz[:], start=True, stop=False)
                        return E.matmul(PG[:], pre[:, 128:256], fiz2[:], start=False,
                                        stop=True)
                    P.on("tensor", mmg, after=(cV,), inc=cE)
                    twiddle(P, PG, w2z, b2, None, "i")
                    def mmy(E):
                        E.matmul(PY[:], f3re[:], b2[:, 0:128], start=True, stop=False)
                        return E.matmul(PY[:], f3imn[:], b2[:, 128:256], start=False,
                                        stop=True)
                    P.on("tensor", mmy, after=(cV,), inc=cE)
                    P.on("vector", lambda E, b=b, ml=ml, m=m: E.scalar_tensor_tensor(
                        ysb[:], uslab[:, (b * 8 + ml) * 128:(b * 8 + ml + 1) * 128],
                        dbc[:, m:m + 1], PY[:], ALU.mult, ALU.add), after=(cE,), inc=cV)
                    P.on("sync", lambda E, b=b, m=m: E.dma_start(
                        out=y_d[b, m, :].rearrange("(c t) -> c t", c=64), in_=ysb[:]),
                         after=(cV,), inc=cO, dma=True)
                    P.on("vector", lambda E: E.tensor_copy(tmp1[0:1, 0:1], tmp1[0:1, 0:1]),
                         after=(cO,), inc=cV)
        P.on("sync", lambda E: E.nop(), after=(cO,))
        P.emit(block)
    return nc


def ssm_host_consts():
    N = 16384
    n2 = np.arange(128)[:, None]
    k1 = np.arange(128)[None, :]
    F_re = np.cos(2 * np.pi * n2 * k1 / 128)
    F_im = -np.sin(2 * np.pi * n2 * k1 / 128)
    Fz = np.concatenate([F_re[0:64], F_im[0:64]], 1).astype(np.float32)
    W_re = np.cos(2 * np.pi * n2 * k1 / N)
    W_im = -np.sin(2 * np.pi * n2 * k1 / N)
    Wz = np.concatenate([W_re, W_im], 1).astype(np.float32)
    F2re = F_re.astype(np.float32)
    F2im = F_im.astype(np.float32)
    a = np.arange(128)[None, :]
    kk = np.arange(128)[:, None]
    Fi_re = np.cos(2 * np.pi * kk * a / 128)
    Fi_im = np.sin(2 * np.pi * kk * a / 128)
    Fiz = np.concatenate([Fi_re, Fi_im], 1).astype(np.float32)
    Fiz2 = np.concatenate([-Fi_im, Fi_re], 1).astype(np.float32)
    W2z = np.concatenate([np.cos(2 * np.pi * kk * a / N),
                          np.sin(2 * np.pi * kk * a / N)], 1).astype(np.float32)
    t1 = np.arange(64)[None, :]
    F3re = (np.cos(2 * np.pi * kk * t1 / 128) / N).astype(np.float32)
    F3imn = (-np.sin(2 * np.pi * kk * t1 / 128) / N).astype(np.float32)
    return {
        "Fz": Fz, "Wz": Wz, "F2re": F2re, "F2im": F2im, "Fiz": Fiz, "Fiz2": Fiz2,
        "W2z": W2z, "F3re": F3re, "F3imn": F3imn,
        "rampj": np.arange(128, dtype=np.float32)[None],
        "rampc": (128.0 * np.arange(64, dtype=np.float32))[None],
        "onesr": np.ones((1, 128), np.float32),
        "identf": np.eye(128, dtype=np.float32),
    }


def run_ssm(inp):
    nc = build_ssm()
    consts = ssm_host_consts()
    u = inp["L0_attn_u"]
    in_maps = []
    for c in range(NCORE):
        sl = slice(c * 128, (c + 1) * 128)
        m = {
            "usig": np.ascontiguousarray(u[:, :, sl].transpose(0, 2, 1)),
            "alog": inp["L0_attn_A_log"][None],
            "ldt": inp["L0_attn_log_dt"][None, sl],
            "cnt": np.ascontiguousarray(inp["L0_attn_C_re"][sl, :].T),
            "drow": inp["L0_attn_D"][None, sl],
        }
        m.update(consts)
        in_maps.append(m)
    res = run_bass_kernel_spmd(nc, in_maps, core_ids=list(range(NCORE)))
    y = np.zeros((BATCH, SEQ, DM), np.float32)
    for c in range(NCORE):
        sl = slice(c * 128, (c + 1) * 128)
        y[:, :, sl] = res.results[c]["ysig"].transpose(0, 2, 1)
    return y


def build_ln0():
    nc = bass.Bass("TRN2", target_bir_lowering=False)
    T = TPC
    u_dd = din(nc, "uf", [DM, T])
    y_dd = din(nc, "yf", [DM, T])
    onc_d = din(nc, "onesc", [128, 1])
    onr_d = din(nc, "onesr", [1, 128])
    x1_d = dout(nc, "x1f", [DM, T])

    P = Prog()
    st = ExitStack()
    sb = lambda name, shape, dt=F32: st.enter_context(nc.sbuf_tensor(name, list(shape), dt))
    ps = lambda name, shape, dt=F32: st.enter_context(nc.psum_tensor(name, list(shape), dt))
    with st:
        A = sb("A_l", [128, NCH * T])
        Bs = sb("B_l", [128, NCH * T])
        sq = sb("sq_l", [128, 512])
        stats = sb("st_l", [1, 2048])
        onc = sb("onc_l", [128, 1])
        onr = sb("onr_l", [1, 128])
        psS = ps("psS", [1, 1024])
        psB = ps("psB", [128, 1024])
        cD = Ctr(st.enter_context(nc.semaphore("cD")))
        cW = Ctr(st.enter_context(nc.semaphore("cW")))
        cE = Ctr(st.enter_context(nc.semaphore("cE")))
        cV = Ctr(st.enter_context(nc.semaphore("cV")))
        cS = Ctr(st.enter_context(nc.semaphore("cS")))
        cO = Ctr(st.enter_context(nc.semaphore("cO")))
        cG = Ctr(st.enter_context(nc.semaphore("cG")))
        P.extra["tensor"] = (cV, cS, cG)
        P.extra["vector"] = (cE, cG)
        P.extra["scalar"] = (cE, cG)
        block = st.enter_context(nc.Block())
        P.on("sync", lambda E: E.dma_start(
            out=A[:].rearrange("p (c t) -> p c t", t=T),
            in_=u_dd.rearrange("(c p) t -> p c t", p=128)), inc=cD, dma=True)
        P.on("sync", lambda E: E.dma_start(
            out=Bs[:].rearrange("p (c t) -> p c t", t=T),
            in_=y_dd.rearrange("(c p) t -> p c t", p=128)), inc=cD, dma=True)
        P.on("sync", lambda E: E.dma_start(out=onc[:], in_=onc_d[:]), inc=cW, dma=True)
        P.on("sync", lambda E: E.dma_start(out=onr[:], in_=onr_d[:]), inc=cW, dma=True)
        for c in range(NCH):
            P.on("vector", lambda E, c=c: E.tensor_add(
                Bs[:, c * T:(c + 1) * T], Bs[:, c * T:(c + 1) * T],
                A[:, c * T:(c + 1) * T]), after=(cD, cW), inc=cV)
        emit_ln(P, cE, cV, cS, T, lambda c: Bs[:, c * T:(c + 1) * T],
                lambda c: A[:, c * T:(c + 1) * T], None, onc, onr, psS, psB, sq, stats)
        for c in range(NCH):
            P.on("sync", lambda E, c=c: E.dma_start(out=x1_d[c * 128:(c + 1) * 128, :],
                                                    in_=A[:, c * T:(c + 1) * T]),
                 after=(cV,), inc=cO, dma=True)
        P.on("sync", lambda E: E.nop(), after=(cO,))
        P.emit(block)
    return nc


def run_ln0(u, y):
    nc = build_ln0()
    onesc = np.ones((128, 1), np.float32)
    onesr = np.ones((1, 128), np.float32)
    in_maps = []
    for c in range(NCORE):
        b, t0 = c // 4, (c % 4) * TPC
        in_maps.append({"uf": np.ascontiguousarray(u[b, t0:t0 + TPC, :].T),
                        "yf": np.ascontiguousarray(y[b, t0:t0 + TPC, :].T),
                        "onesc": onesc, "onesr": onesr})
    res = run_bass_kernel_spmd(nc, in_maps, core_ids=list(range(NCORE)))
    x1 = np.zeros((BATCH, SEQ, DM), np.float32)
    for c in range(NCORE):
        b, t0 = c // 4, (c % 4) * TPC
        x1[b, t0:t0 + TPC, :] = res.results[c]["x1f"].T
    return x1


def kernel(**inputs):
    inp = {k: np.asarray(v) for k, v in inputs.items()}
    _ = inp.get("L0_attn_C_im")
    y = run_ssm(inp)
    x1 = run_ln0(inp["L0_attn_u"], y)
    consts = l0ffn_host_consts()
    cache = {"tabs": [l0ffn_tables(c, consts) for c in range(NCORE)]}
    x = x1
    for i in range(6):
        x = run_l0ffn_layer(cache, x, inp, i, consts)
    x3 = run_att(x1, x, inp)
    out = run_moe(x3, inp)
    return out
